# version 04
# speedup vs baseline: 1.2580x; 1.1122x over previous
import contextlib
import numpy as np

import concourse.bass as bass
import concourse.mybir as mybir
from concourse.bass_utils import run_bass_kernel_spmd

F32 = mybir.dt.float32
BF16 = mybir.dt.bfloat16
I32 = mybir.dt.int32
AF = mybir.ActivationFunctionType
ALU = mybir.AluOpType
AX = mybir.AxisListType.X

D_MODEL = 1024
D_INNER = 2048
D_IN_PROJ = 4384
NM_IN = 35
L = 4096
TLOC = 2048
TILE = 512
NT = L // TILE
EPS = 1e-5
CR = TILE + 3

DBG_NAMES = [
    ("dbg_xn", [128, 8 * TILE]), ("dbg_xbc", [128, 18 * TILE]),
    ("dbg_dt", [32, TILE]), ("dbg_y", [128, 2048]),
    ("dbg_mrev", [128, 8 * L]), ("dbg_mnat", [128, 8 * TLOC]),
    ("dbg_s", [128, 8 * TLOC]), ("dbg_x1", [128, 8 * TLOC]),
    ("dbg_xn2", [128, 8 * TLOC]), ("dbg_g", [128, 8 * TLOC]),
    ("dbg_scores", [128, 64]), ("dbg_comb", [128, 64]),
    ("dbg_s1", [128, 8]), ("dbg_ffn", [128, 8 * TLOC]),
    ("dbg_upr", [128, 24 * 256]),
]

MGROUPS = [list(range(g, min(g + 4, NM_IN))) for g in range(0, NM_IN, 4)]
WHALF = 8 * 4 * 128


def build(debug=()):
    nc = bass.Bass("TRN2", target_bir_lowering=False, debug=False)
    stack = contextlib.ExitStack()

    d = {}
    def din(name, shape):
        d[name] = nc.dram_tensor(name, list(shape), F32, kind="ExternalInput")

    din("x", [D_MODEL, L])
    din("win", [D_MODEL, D_IN_PROJ])
    din("wout", [D_INNER, D_MODEL])
    din("convw", [2304, 4])
    din("convb", [128, 18])
    din("dtbias", [32, 1])
    din("alog", [32, 1])
    din("dhead", [128, 32])
    din("normw", [128, 16])
    din("ident", [128, 128])
    din("maskneg", [128, 128])
    din("jrev", [128, 128])
    din("rwT", [D_MODEL, 16])
    din("ebias", [128, 16])
    din("tiew", [128, 16])
    din("tiew4", [128, 4])
    din("gwT", [16, D_MODEL, 512])
    din("uwT", [16, D_MODEL, 512])
    din("dwT", [16, 512, D_MODEL])
    din("shgT", [D_MODEL, D_MODEL])
    din("shuT", [D_MODEL, D_MODEL])
    din("shdT", [D_MODEL, D_MODEL])
    din("wvT", [D_MODEL, D_MODEL])
    din("woT", [D_MODEL, D_MODEL])
    din("wgate", [3072, D_MODEL])
    din("wup", [3072, D_MODEL])
    din("wdown", [D_MODEL, 3072])

    out = nc.dram_tensor("out", [D_MODEL, TLOC], F32, kind="ExternalOutput")
    dbg = {}
    for name, shape in DBG_NAMES:
        if name in debug:
            dbg[name] = nc.dram_tensor(name, shape, F32, kind="ExternalOutput")

    ellbuf = nc.dram_tensor("ellbuf", [4 * 32, 128], F32)
    arin = nc.dram_tensor("arin", [128, 8 * L], BF16)
    arout = nc.dram_tensor("arout", [128, 8 * L], BF16)
    mnat = nc.dram_tensor("mnat", [128, 8 * TLOC], BF16)
    gwb = nc.dram_tensor("gwb", [16, D_MODEL, 512], BF16)
    uwb = nc.dram_tensor("uwb", [16, D_MODEL, 512], BF16)
    dwb = nc.dram_tensor("dwb", [16, 512, D_MODEL], BF16)
    shb = nc.dram_tensor("shb", [3, D_MODEL, D_MODEL], BF16)
    wvb = nc.dram_tensor("wvb", [D_MODEL, D_MODEL], BF16)
    wob = nc.dram_tensor("wob", [D_MODEL, D_MODEL], BF16)
    combuf = nc.dram_tensor("combuf", [32, 128], F32)
    x1d = nc.dram_tensor("x1d", [128, 8 * TLOC], F32)
    xn2d = nc.dram_tensor("xn2d", [128, 8 * TLOC], BF16)
    armx = nc.dram_tensor("armx", [1, 16], F32)
    armxo = nc.dram_tensor("armxo", [1, 16], F32)
    vsin = nc.dram_tensor("vsin", [128, 8], F32)
    vsout = nc.dram_tensor("vsout", [128, 8], F32)
    wuqT = nc.dram_tensor("wuqT", [1024, 3072], BF16)
    wgqT = nc.dram_tensor("wgqT", [1024, 3072], BF16)
    wdqT = nc.dram_tensor("wdqT", [3072, 1024], BF16)
    upraw = nc.dram_tensor("upraw", [128, 24 * TLOC], BF16)
    gateraw = nc.dram_tensor("gateraw", [128, 24 * TLOC], BF16)
    gate2d = nc.dram_tensor("gate2d", [128, 24 * TLOC], BF16)
    od = nc.dram_tensor("od", [128, 24 * TLOC], BF16)

    sbuf = lambda name, shape, dt=F32: stack.enter_context(nc.sbuf_tensor(name, shape, dt))
    psum = lambda name, shape, dt=F32: stack.enter_context(nc.psum_tensor(name, shape, dt))

    wslice = sbuf("wslice", [128, 2 * WHALF], BF16)
    woutb = sbuf("woutb", [128, 16 * D_MODEL], BF16)
    xnb = sbuf("xnb", [128, 8 * TILE], BF16)
    siluz = sbuf("siluz", [128, 16 * TILE], BF16)
    xbcraw = sbuf("xbcraw", [128, 18 * CR], BF16)
    tailb = sbuf("tailb", [128, 18 * 3], BF16)
    dtraw = sbuf("dtraw", [32, TILE], F32)
    dtp = sbuf("dtp", [32, TILE], F32)
    dta = sbuf("dta", [32, TILE], F32)
    lndt = sbuf("lndt", [32, TILE], F32)
    ones32 = sbuf("ones32", [32, 128], F32)
    onec32 = sbuf("onec32", [32, 1], F32)
    ellf = sbuf("ellf", [32, 128], F32)
    lmef = sbuf("lmef", [32, 128], F32)
    hl1 = sbuf("hl1", [32, 128], BF16)
    hl2 = sbuf("hl2", [32, 128], BF16)
    ellT = sbuf("ellT", [128, 32], F32)
    dtT = sbuf("dtT", [128, 32], F32)
    ecolT = sbuf("ecolT", [128, 32], F32)
    lendr = sbuf("lendr", [1, 32], F32)
    lendE = sbuf("lendE", [1, 32], F32)
    ffac = sbuf("ffac", [128, 32], BF16)
    ftmp = sbuf("ftmp", [128, 32], F32)
    ftmp2 = sbuf("ftmp2", [128, 32], F32)
    edec = sbuf("edec", [128, 32], F32)
    lstk = sbuf("lstk", [4, 4096], BF16)
    rstk = sbuf("rstk", [4, 4096], BF16)
    gtile = sbuf("gtile", [128, 128], F32)
    masks = sbuf("masks", [128, 128], F32)
    jrevs = sbuf("jrevs", [128, 128], BF16)
    idf = sbuf("idf", [128, 128], F32)
    idb = sbuf("idb", [128, 128], BF16)
    onesb = sbuf("onesb", [128, 1], BF16)
    onesrb = sbuf("onesrb", [1, 128], BF16)
    onesf = sbuf("onesf", [1, 128], F32)
    epsc = sbuf("epsc", [128, 1], F32)
    dstage = sbuf("dstage", [128, 32], BF16)
    dtbias_c = sbuf("dtbias_c", [32, 1], F32)
    alog_c = sbuf("alog_c", [32, 1], F32)
    acol = sbuf("acol", [32, 1], F32)
    convw_c = sbuf("convw_c", [128, 18 * 4], F32)
    convb_c = sbuf("convb_c", [128, 18], F32)
    normw_c = sbuf("normw_c", [128, 16], F32)
    xtok = sbuf("xtok", [128, 2048], BF16)
    btok = sbuf("btok", [128, 128], BF16)
    difsb = sbuf("difsb", [128, 2048], F32)
    expsb = sbuf("expsb", [128, 2048], BF16)
    lhsT_all = sbuf("lhsT_all", [128, 4096], BF16)
    ytok = sbuf("ytok", [128, 2048], F32)
    ytmp = sbuf("ytmp", [128, 2048], F32)
    ytmp2 = sbuf("ytmp2", [128, 2048], BF16)
    ygat = sbuf("ygat", [128, 2048], BF16)
    szt = sbuf("szt", [128, 2048], BF16)
    ynrm = sbuf("ynrm", [128, 2048], BF16)
    yrev = sbuf("yrev", [128, 2048], BF16)
    ytfm = sbuf("ytfm", [128, 2048], BF16)
    mstage = sbuf("mstage", [128, 8 * 128], BF16)
    rmscol = sbuf("rmscol", [128, 1], F32)
    rstdc = sbuf("rstdc", [128, 1], F32)
    rstd2c = sbuf("rstd2c", [128, 1], F32)
    hstf = sbuf("hstf", [128, 2048], F32)
    hstb = sbuf("hstb", [128, 2048], BF16)
    xts = sbuf("xts", [128, 2048], BF16)
    mrow = dtraw[0:1, 0:TILE]
    e2row = dtp[0:1, 0:TILE]
    vrow = dta[0:1, 0:TILE]
    vrow2 = lndt[0:1, 0:TILE]
    rstdr = lndt[0:1, 0:TILE]

    pf = psum("pf", [128, 2048], F32)
    pq = psum("pq", [128, 1024], F32)
    pg = psum("pg", [128, 1024], F32)
    pgb = pg[:, :].bitcast(BF16)

    class DQ:
        def __init__(self, name):
            self.h = stack.enter_context(nc.semaphore(name))
            self.n = 0
        def go(self, eng, out_, in_, **kw):
            eng.dma_start(out=out_, in_=in_, **kw).then_inc(self.h, 16)
            self.n += 16
        def wait(self, eng, n=None):
            eng.wait_ge(self.h, self.n if n is None else n)

    gq = DQ("gq")
    sq = DQ("sq")
    ccs = stack.enter_context(nc.semaphore("ccs"))
    cc_n = [0]

    def block():
        return nc.Block()

    def xpart(k):
        buf = ytok if k < 4 else ytmp
        return buf[:, (k % 4) * TILE:(k % 4 + 1) * TILE]

    xbf_v = siluz[:, 0:8 * TILE]
    xsq_v = siluz[:, 8 * TILE:16 * TILE]

    def grp_cols(ml):
        return 128 * len(ml) - (96 if 34 in ml else 0)

    def win_load(eng, grp, half):
        ml = MGROUPS[grp]
        gw = grp_cols(ml)
        gq.go(eng,
              wslice[:, half * WHALF:half * WHALF + 8 * gw]
              .rearrange("p (k m) -> p k m", k=8),
              d["win"].ap().rearrange("(k p) m -> p k m", p=128)
              [:, :, ml[0] * 128:ml[0] * 128 + gw])

    with block() as b:
        @b.gpsimd
        def _(g):
            gq.go(g, masks[:, :], d["maskneg"][:, :])
            gq.go(g, jrevs[:, :], d["jrev"][:, :])
            gq.go(g, idf[:, :], d["ident"][:, :])
            gq.go(g, idb[:, :], d["ident"][:, :])
            gq.go(g, dtbias_c[:, :], d["dtbias"][:, :])
            gq.go(g, alog_c[:, :], d["alog"][:, :])
            gq.go(g, dstage[:, :], d["dhead"][:, :])
            gq.go(g, convw_c[:, :].rearrange("p (c k) -> p c k", c=18),
                  d["convw"].ap().rearrange("(c p) k -> p c k", p=128))
            gq.go(g, convb_c[:, :], d["convb"][:, :])
            gq.go(g, normw_c[:, :], d["normw"][:, :])
            g.memset(onesb[:, :], 1.0)
            g.memset(onesrb[:, :], 1.0)
            g.memset(onesf[:, :], 1.0)
            g.memset(ones32[:, :], 1.0)
            g.memset(onec32[:, :], 1.0)
            g.memset(epsc[:, :], EPS)
            g.memset(lstk[:, :], 1.0)
            g.memset(rstk[:, :], 1.0)
            g.memset(hstf[:, :], 0.0)
            g.memset(hstb[:, :], 0.0)
            g.memset(tailb[:, :], 0.0)
            gq.go(g, woutb[:, :].rearrange("p (k m) -> p k m", k=16),
                  d["wout"].ap().rearrange("(k p) m -> p k m", p=128))
            win_load(g, 0, 0)
        @b.vector
        def _(v):
            gq.wait(v)
    with block() as b:
        @b.scalar
        def _(s):
            s.activation(out=acol[:, :], in_=alog_c[:, :], func=AF.Exp)
        @b.vector
        def _(v):
            for k in range(16):
                v.tensor_scalar(out=woutb[:, k * D_MODEL:(k + 1) * D_MODEL],
                                in0=woutb[:, k * D_MODEL:(k + 1) * D_MODEL],
                                scalar1=normw_c[:, k:k + 1], scalar2=None, op0=ALU.mult)
    with block() as b:
        @b.vector
        def _(v):
            v.tensor_scalar(out=acol[:, :], in0=acol[:, :], scalar1=-1.0,
                            scalar2=None, op0=ALU.mult)

    CASTC = 256 * 1024
    cast_chunks = []

    def add_cast_job(src, dst, n):
        for i in range(0, n, CASTC):
            cast_chunks.append((dst[:, i:i + CASTC], src[:, i:i + CASTC]))

    for e in range(16):
        add_cast_job(d["gwT"].ap().rearrange("e a b -> e (a b)")[e:e + 1, :],
                     gwb.ap().rearrange("e a b -> e (a b)")[e:e + 1, :], 512 * 1024)
        add_cast_job(d["uwT"].ap().rearrange("e a b -> e (a b)")[e:e + 1, :],
                     uwb.ap().rearrange("e a b -> e (a b)")[e:e + 1, :], 512 * 1024)
        add_cast_job(d["dwT"].ap().rearrange("e a b -> e (a b)")[e:e + 1, :],
                     dwb.ap().rearrange("e a b -> e (a b)")[e:e + 1, :], 512 * 1024)
    for i, nm in enumerate(["shgT", "shuT", "shdT"]):
        add_cast_job(d[nm].ap().rearrange("a b -> () (a b)"),
                     shb.ap().rearrange("e a b -> e (a b)")[i:i + 1, :], 1024 * 1024)
    add_cast_job(d["wvT"].ap().rearrange("a b -> () (a b)"),
                 wvb.ap().rearrange("a b -> () (a b)"), 1024 * 1024)
    add_cast_job(d["woT"].ap().rearrange("a b -> () (a b)"),
                 wob.ap().rearrange("a b -> () (a b)"), 1024 * 1024)

    def cast_feed(g, k=1):
        for _ in range(k):
            if cast_chunks:
                dst, src = cast_chunks.pop(0)
                gq.go(g, dst, src)

    pending_stores = []

    def store_flush(g):
        for dst, src in pending_stores:
            gq.go(g, dst, src)
        pending_stores.clear()

    for t in range(NT):
        tok0 = t * TILE
        with block() as b:
            @b.gpsimd
            def _(g, tok0=tok0):
                for k in range(8):
                    gq.go(g, xpart(k),
                          d["x"].ap().rearrange("(k p) q -> p k q", p=128)
                          [:, k, tok0:tok0 + TILE])
                gq.wait(g)
        with block() as b:
            @b.vector
            def _(v):
                for k in range(8):
                    v.tensor_copy(out=xbf_v[:, k * TILE:(k + 1) * TILE], in_=xpart(k))
            @b.scalar
            def _(s):
                for k in range(8):
                    s.activation(out=xsq_v[:, k * TILE:(k + 1) * TILE], in_=xpart(k),
                                 func=AF.Square)
        with block() as b:
            @b.tensor
            def _(tt):
                for k in range(8):
                    tt.matmul(pg[0:1, 0:TILE], onesb[:, 0:1],
                              xbf_v[:, k * TILE:(k + 1) * TILE],
                              start=(k == 0), stop=(k == 7))
                for k in range(8):
                    tt.matmul(pg[0:1, TILE:2 * TILE], onesb[:, 0:1],
                              xsq_v[:, k * TILE:(k + 1) * TILE],
                              start=(k == 0), stop=(k == 7))
        with block() as b:
            @b.vector
            def _(v):
                v.tensor_scalar(out=mrow, in0=pg[0:1, 0:TILE], scalar1=1.0 / 1024,
                                scalar2=None, op0=ALU.mult)
                v.tensor_scalar(out=e2row, in0=pg[0:1, TILE:2 * TILE],
                                scalar1=1.0 / 1024, scalar2=None, op0=ALU.mult)
        with block() as b:
            @b.vector
            def _(v):
                v.tensor_tensor(out=vrow, in0=mrow, in1=mrow, op=ALU.mult)
        with block() as b:
            @b.vector
            def _(v):
                v.tensor_tensor(out=vrow2, in0=e2row, in1=vrow,
                                op=ALU.subtract)
        with block() as b:
            @b.scalar
            def _(s):
                s.activation(out=vrow, in_=vrow2, func=AF.Ln, bias=epsc[0:1, 0:1])
        with block() as b:
            @b.scalar
            def _(s):
                s.activation(out=rstdr, in_=vrow, func=AF.Exp, scale=-0.5)
        with block() as b:
            @b.tensor
            def _(tt):
                tt.matmul(pf[:, 0:TILE], onesf[0:1, :], mrow, start=True, stop=True)
                tt.matmul(pf[:, TILE:2 * TILE], onesf[0:1, :], rstdr,
                          start=True, stop=True)
        with block() as b:
            @b.vector
            def _(v):
                for k in range(8):
                    v.tensor_tensor(out=xpart(k), in0=xpart(k), in1=pf[:, 0:TILE],
                                    op=ALU.subtract)
                    v.tensor_tensor(out=xnb[:, k * TILE:(k + 1) * TILE],
                                    in0=xpart(k), in1=pf[:, TILE:2 * TILE], op=ALU.mult)
        if "dbg_xn" in dbg and t == 0:
            with block() as b:
                @b.vector
                def _(v):
                    for k in range(8):
                        v.tensor_copy(out=xpart(k), in_=xnb[:, k * TILE:(k + 1) * TILE])
            with block() as b:
                @b.gpsimd
                def _(g):
                    for k in range(8):
                        gq.go(g, dbg["dbg_xn"][:, k * TILE:(k + 1) * TILE], xpart(k))
                    gq.wait(g)

        with block() as b:
            @b.vector
            def _(v):
                v.tensor_copy(
                    out=xbcraw[:, :].rearrange("p (c q) -> p c q", c=18)[:, :, 0:3],
                    in_=tailb[:, :].rearrange("p (c k) -> p c k", c=18))

        def inproj_psum(grp, j, mw=128):
            buf = pf if grp % 2 == 0 else (pq if j < 2 else pg)
            jj = j if grp % 2 == 0 else j % 2
            return buf[0:mw, jj * TILE:(jj + 1) * TILE]

        def inproj_evac(eng, grp, kind):
            for j, m in enumerate(MGROUPS[grp]):
                src = inproj_psum(grp, j)
                if m < 16 and kind == 's':
                    eng.activation(out=siluz[:, m * TILE:(m + 1) * TILE],
                                   in_=src, func=AF.Silu)
                elif 16 <= m < 34 and kind == 'v':
                    c = m - 16
                    eng.tensor_copy(out=xbcraw[:, c * CR + 3:(c + 1) * CR], in_=src)
                elif m == 34 and kind == 'v':
                    eng.tensor_copy(out=dtraw[:, :], in_=src[0:32, :])

        for grp, ml in enumerate(MGROUPS):
            half = grp % 2
            gw = grp_cols(ml)
            gq_target = gq.n
            with block() as b:
                @b.gpsimd
                def _(g, grp=grp):
                    nxt = grp + 1
                    if nxt < len(MGROUPS):
                        win_load(g, nxt, nxt % 2)
                    if grp >= 5:
                        cast_feed(g)
                @b.tensor
                def _(tt, grp=grp, ml=ml, half=half, gw=gw, gq_target=gq_target):
                    gq.wait(tt, gq_target)
                    for j, m in enumerate(ml):
                        mw = 128 if m < 34 else 32
                        for k in range(8):
                            tt.matmul(inproj_psum(grp, j, mw),
                                      wslice[:, half * WHALF + k * gw + j * 128:
                                             half * WHALF + k * gw + j * 128 + mw],
                                      xnb[:, k * TILE:(k + 1) * TILE],
                                      start=(k == 0), stop=(k == 7))
                @b.scalar
                def _(s, grp=grp):
                    if grp > 0:
                        inproj_evac(s, grp - 1, 's')
                @b.vector
                def _(v, grp=grp):
                    if grp > 0:
                        inproj_evac(v, grp - 1, 'v')
        with block() as b:
            @b.vector
            def _(v):
                inproj_evac(v, len(MGROUPS) - 1, 'v')

        with block() as b:
            @b.gpsimd
            def _(g, t=t):
                if t + 1 < NT:
                    win_load(g, 0, 0)
            @b.vector
            def _(v):
                v.tensor_copy(
                    out=tailb[:, :].rearrange("p (c k) -> p c k", c=18),
                    in_=xbcraw[:, :].rearrange("p (c q) -> p c q", c=18)[:, :, TILE:TILE + 3])

        for c in range(19):
            with block() as b:
                @b.vector
                def _(v, c=c):
                    if c < 18:
                        base = c * CR
                        o1 = (c % 2) * 1024
                        v.tensor_scalar(out=ytok[:, o1:o1 + TILE],
                                        in0=xbcraw[:, base:base + TILE],
                                        scalar1=convw_c[:, c * 4:c * 4 + 1],
                                        scalar2=convb_c[:, c:c + 1],
                                        op0=ALU.mult, op1=ALU.add)
                        v.scalar_tensor_tensor(out=ytok[:, o1 + TILE:o1 + 2 * TILE],
                                               in0=xbcraw[:, base + 1:base + 1 + TILE],
                                               scalar=convw_c[:, c * 4 + 1:c * 4 + 2],
                                               in1=ytok[:, o1:o1 + TILE],
                                               op0=ALU.mult, op1=ALU.add)
                        v.scalar_tensor_tensor(out=ytok[:, o1:o1 + TILE],
                                               in0=xbcraw[:, base + 2:base + 2 + TILE],
                                               scalar=convw_c[:, c * 4 + 2:c * 4 + 3],
                                               in1=ytok[:, o1 + TILE:o1 + 2 * TILE],
                                               op0=ALU.mult, op1=ALU.add)
                        v.scalar_tensor_tensor(out=ytok[:, o1 + TILE:o1 + 2 * TILE],
                                               in0=xbcraw[:, base + 3:base + 3 + TILE],
                                               scalar=convw_c[:, c * 4 + 3:c * 4 + 4],
                                               in1=ytok[:, o1:o1 + TILE],
                                               op0=ALU.mult, op1=ALU.add)
                @b.scalar
                def _(s, c=c):
                    cp = c - 1
                    if cp >= 0:
                        base = cp * CR
                        o1 = (cp % 2) * 1024
                        s.activation(out=xbcraw[:, base + 3:base + 3 + TILE],
                                     in_=ytok[:, o1 + TILE:o1 + 2 * TILE], func=AF.Silu)
        if "dbg_xbc" in dbg and t == 0:
            with block() as b:
                @b.gpsimd
                def _(g):
                    gq.go(g, dbg["dbg_xbc"][:, :].rearrange("p (c q) -> p c q", c=18),
                          xbcraw[:, :].rearrange("p (c q) -> p c q", c=18)[:, :, 3:3 + TILE])
                    gq.wait(g)

        with block() as b:
            @b.vector
            def _(v):
                v.tensor_scalar(out=dtp[:, :], in0=dtraw[:, :],
                                scalar1=dtbias_c[:, 0:1], scalar2=None, op0=ALU.add)
        with block() as b:
            @b.scalar
            def _(s):
                s.activation(out=dta[:, :], in_=dtp[:, :], func=AF.Abs)
            @b.vector
            def _(v):
                v.tensor_scalar(out=dtraw[:, :], in0=dtp[:, :], scalar1=0.0,
                                scalar2=None, op0=ALU.max)
        with block() as b:
            @b.scalar
            def _(s):
                s.activation(out=lndt[:, :], in_=dta[:, :], func=AF.Exp, scale=-1.0)
        with block() as b:
            @b.scalar
            def _(s):
                s.activation(out=dta[:, :], in_=lndt[:, :], func=AF.Ln, bias=onec32[:, 0:1])
        with block() as b:
            @b.vector
            def _(v):
                v.tensor_tensor(out=dtp[:, :], in0=dtraw[:, :], in1=dta[:, :], op=ALU.add)
        with block() as b:
            @b.vector
            def _(v):
                v.tensor_scalar(out=dta[:, :], in0=dtp[:, :], scalar1=acol[:, 0:1],
                                scalar2=None, op0=ALU.mult)
            @b.scalar
            def _(s):
                s.activation(out=lndt[:, :], in_=dtp[:, :], func=AF.Ln)
        if "dbg_dt" in dbg and t == 0:
            with block() as b:
                @b.gpsimd
                def _(g):
                    gq.go(g, dbg["dbg_dt"][:, :], dtp[:, :])
                    gq.wait(g)

        for c in range(4):
            cg = t * 4 + c
            c0 = c * 128
            with block() as b:
                @b.vector
                def _(v, c0=c0):
                    v.tensor_tensor_scan(out=ellf[:, :], data0=ones32[:, :],
                                         data1=dta[:, c0:c0 + 128], initial=0.0,
                                         op0=ALU.mult, op1=ALU.add)
            with block() as b:
                @b.vector
                def _(v, c0=c0):
                    v.tensor_tensor(out=lmef[:, :], in0=lndt[:, c0:c0 + 128],
                                    in1=ellf[:, :], op=ALU.subtract)
            with block() as b:
                @b.vector
                def _(v):
                    v.tensor_copy(out=hl1[:, :], in_=lmef[:, :])
                    v.tensor_copy(out=hl2[:, :], in_=ellf[:, :])
            with block() as b:
                @b.vector
                def _(v):
                    v.tensor_tensor(out=lmef[:, :], in0=lmef[:, :], in1=hl1[:, :],
                                    op=ALU.subtract)
                    v.tensor_tensor(out=ellf[:, :], in0=ellf[:, :], in1=hl2[:, :],
                                    op=ALU.subtract)
            with block() as b:
                @b.gpsimd
                def _(g):
                    gq.go(g, ellbuf[0:32, :], hl1[:, :])
                    gq.go(g, ellbuf[32:64, :], lmef[:, :])
                    gq.go(g, ellbuf[64:96, :], hl2[:, :])
                    gq.go(g, ellbuf[96:128, :], ellf[:, :])
                    gq.wait(g)
                    flat = ellbuf.ap().rearrange("(a p) q -> a (p q)", a=4)
                    gq.go(g, lstk[0:1, :], flat[0:1, :])
                    gq.go(g, lstk[1:2, :], flat[1:2, :])
                    gq.go(g, rstk[2:3, :], flat[2:3, :])
                    gq.go(g, rstk[3:4, :], flat[3:4, :])
                    gq.wait(g)
            with block() as b:
                @b.vector
                def _(v):
                    v.tensor_tensor(out=ellf[:, :], in0=hl2[:, :], in1=ellf[:, :], op=ALU.add)
            with block() as b:
                @b.tensor
                def _(tt, c0=c0):
                    tt.transpose(pg[:, 0:32], ellf[:, :], idf[0:32, 0:32])
                    tt.transpose(pg[:, 32:64], dtp[:, c0:c0 + 128], idf[0:32, 0:32])
                    tt.transpose(pg[0:1, 64:96], ellf[:, 127:128], idf[0:32, 0:32])
            with block() as b:
                @b.vector
                def _(v):
                    v.tensor_copy(out=ellT[:, :], in_=pg[:, 0:32])
                    v.tensor_copy(out=dtT[:, :], in_=pg[:, 32:64])
                    v.tensor_copy(out=lendr[:, :], in_=pg[0:1, 64:96])
            with block() as b:
                @b.scalar
                def _(s):
                    s.activation(out=ecolT[:, :], in_=ellT[:, :], func=AF.Exp)
            with block() as b:
                @b.scalar
                def _(s):
                    s.activation(out=lendE[:, :], in_=lendr[:, :], func=AF.Exp)
            with block() as b:
                @b.tensor
                def _(tt, c0=c0):
                    tt.matmul(pf[:, 0:32], onesf[0:1, :], lendr[:, :], start=True, stop=True)
                    tt.matmul(pf[:, 32:64], onesf[0:1, :], lendE[:, :], start=True, stop=True)
                    tt.matmul(pf[:, 128:256],
                              xbcraw[:, 16 * CR + 3 + c0:16 * CR + 3 + c0 + 128],
                              xbcraw[:, 17 * CR + 3 + c0:17 * CR + 3 + c0 + 128],
                              start=True, stop=True)
            with block() as b:
                @b.vector
                def _(v):
                    v.tensor_tensor(out=ftmp[:, :], in0=pf[:, 0:32], in1=ellT[:, :],
                                    op=ALU.subtract)
                    v.tensor_copy(out=edec[:, :], in_=pf[:, 32:64])
                    v.tensor_copy(out=gtile[:, :], in_=pf[:, 128:256])
            with block() as b:
                @b.scalar
                def _(s):
                    s.activation(out=ftmp2[:, :], in_=ftmp[:, :], func=AF.Exp)
            with block() as b:
                @b.vector
                def _(v):
                    v.tensor_tensor(out=ffac[:, :], in0=ftmp2[:, :], in1=dtT[:, :],
                                    op=ALU.mult)
            with block() as b:
                @b.tensor
                def _(tt, c0=c0):
                    for kc in range(16):
                        tt.transpose(pgb[:, kc * 128:(kc + 1) * 128],
                                     xbcraw[:, kc * CR + 3 + c0:kc * CR + 3 + c0 + 128],
                                     idb[:, :])
            with block() as b:
                @b.vector
                def _(v):
                    for kc in range(8):
                        v.tensor_copy(out=xtok[:, kc * 128:(kc + 1) * 128],
                                      in_=pgb[:, kc * 128:(kc + 1) * 128])
                @b.scalar
                def _(s):
                    for kc in range(8, 16):
                        s.copy(out=xtok[:, kc * 128:(kc + 1) * 128],
                               in_=pgb[:, kc * 128:(kc + 1) * 128])
            with block() as b:
                @b.tensor
                def _(tt, c0=c0):
                    for kc in range(16):
                        tt.transpose(pgb[:, kc * 128:(kc + 1) * 128],
                                     siluz[:, kc * TILE + c0:kc * TILE + c0 + 128],
                                     idb[:, :])
            with block() as b:
                @b.vector
                def _(v):
                    for kc in range(8):
                        v.tensor_copy(out=szt[:, kc * 128:(kc + 1) * 128],
                                      in_=pgb[:, kc * 128:(kc + 1) * 128])
                @b.scalar
                def _(s):
                    for kc in range(8, 16):
                        s.copy(out=szt[:, kc * 128:(kc + 1) * 128],
                               in_=pgb[:, kc * 128:(kc + 1) * 128])
            with block() as b:
                @b.tensor
                def _(tt, c0=c0):
                    tt.transpose(pgb[:, 0:128],
                                 xbcraw[:, 16 * CR + 3 + c0:16 * CR + 3 + c0 + 128],
                                 idb[:, :])
            with block() as b:
                @b.vector
                def _(v):
                    v.tensor_copy(out=btok[:, :], in_=pgb[:, 0:128])
            with block() as b:
                @b.tensor
                def _(tt):
                    for h in range(16):
                        tt.matmul(pf[:, h * 128:(h + 1) * 128],
                                  lstk[:, h * 128:(h + 1) * 128],
                                  rstk[:, h * 128:(h + 1) * 128], start=True, stop=True)
            with block() as b:
                @b.vector
                def _(v):
                    v.tensor_tensor(
                        out=difsb[:, :].rearrange("p (h q) -> p h q", h=16),
                        in0=pf[:, 0:2048].rearrange("p (h q) -> p h q", h=16),
                        in1=masks[:, :].rearrange("p q -> p () q").broadcast_to((128, 16, 128)),
                        op=ALU.add)
            with block() as b:
                @b.scalar
                def _(s):
                    s.activation(out=expsb[:, :], in_=difsb[:, :], func=AF.Exp)
                @b.tensor
                def _(tt):
                    for h in range(16, 32):
                        tt.matmul(pf[:, (h - 16) * 128:(h - 15) * 128],
                                  lstk[:, h * 128:(h + 1) * 128],
                                  rstk[:, h * 128:(h + 1) * 128], start=True, stop=True)
            with block() as b:
                @b.vector
                def _(v):
                    v.tensor_tensor(
                        out=lhsT_all[:, 0:2048].rearrange("p (h q) -> p h q", h=16),
                        in0=expsb[:, :].rearrange("p (h q) -> p h q", h=16),
                        in1=gtile[:, :].rearrange("p q -> p () q").broadcast_to((128, 16, 128)),
                        op=ALU.mult)
                    v.tensor_tensor(
                        out=difsb[:, :].rearrange("p (h q) -> p h q", h=16),
                        in0=pf[:, 0:2048].rearrange("p (h q) -> p h q", h=16),
                        in1=masks[:, :].rearrange("p q -> p () q").broadcast_to((128, 16, 128)),
                        op=ALU.add)
            with block() as b:
                @b.scalar
                def _(s):
                    s.activation(out=expsb[:, :], in_=difsb[:, :], func=AF.Exp)
            with block() as b:
                @b.vector
                def _(v):
                    v.tensor_tensor(
                        out=lhsT_all[:, 2048:4096].rearrange("p (h q) -> p h q", h=16),
                        in0=expsb[:, :].rearrange("p (h q) -> p h q", h=16),
                        in1=gtile[:, :].rearrange("p q -> p () q").broadcast_to((128, 16, 128)),
                        op=ALU.mult)
            for half in range(2):
                h0 = half * 16
                with block() as b:
                    @b.tensor
                    def _(tt, h0=h0, c0=c0):
                        for i in range(16):
                            h = h0 + i
                            tt.matmul(pf[:, i * 64:(i + 1) * 64],
                                      lhsT_all[:, h * 128:(h + 1) * 128],
                                      xtok[:, h * 64:(h + 1) * 64], start=True, stop=True)
                        for i in range(16):
                            h = h0 + i
                            tt.matmul(pf[:, 1024 + i * 64:1024 + (i + 1) * 64],
                                      xbcraw[:, 17 * CR + 3 + c0:17 * CR + 3 + c0 + 128],
                                      hstb[:, h * 64:(h + 1) * 64], start=True, stop=True)
                with block() as b:
                    @b.vector
                    def _(v, h0=h0, half=half):
                        v.tensor_tensor(
                            out=ytmp2[:, 0:1024].rearrange("p (h q) -> p h q", h=16),
                            in0=pf[:, 1024:2048].rearrange("p (h q) -> p h q", h=16),
                            in1=ecolT[:, h0:h0 + 16].rearrange("p h -> p h ()")
                                .broadcast_to((128, 16, 64)),
                            op=ALU.mult)
                        v.tensor_tensor(out=ygat[:, half * 1024:(half + 1) * 1024],
                                        in0=ytmp2[:, 0:1024], in1=pf[:, 0:1024],
                                        op=ALU.add)
            with block() as b:
                @b.vector
                def _(v):
                    v.tensor_tensor(
                        out=ytmp2[:, :].rearrange("p (h q) -> p h q", h=32),
                        in0=xtok[:, :].rearrange("p (h q) -> p h q", h=32),
                        in1=dstage[:, :].rearrange("p h -> p h ()").broadcast_to((128, 32, 64)),
                        op=ALU.mult)
            with block() as b:
                @b.vector
                def _(v):
                    v.tensor_tensor(out=ynrm[:, :], in0=ygat[:, :], in1=ytmp2[:, :],
                                    op=ALU.add)
            with block() as b:
                @b.vector
                def _(v):
                    v.tensor_tensor(out=ygat[:, :], in0=ynrm[:, :], in1=szt[:, :],
                                    op=ALU.mult)
            if "dbg_y" in dbg and cg == 0:
                with block() as b:
                    @b.gpsimd
                    def _(g):
                        gq.go(g, dbg["dbg_y"][:, :], ygat[:, :])
                        gq.wait(g)
            with block() as b:
                @b.scalar
                def _(s):
                    s.activation(out=ytmp2[:, :], in_=ygat[:, :], func=AF.Square,
                                 accum_out=rmscol[:, :])
            with block() as b:
                @b.scalar
                def _(s):
                    s.activation(out=rstdc[:, :], in_=rmscol[:, :], func=AF.Ln,
                                 bias=epsc[:, 0:1], scale=1.0 / 2048)
            with block() as b:
                @b.scalar
                def _(s):
                    s.activation(out=rstd2c[:, :], in_=rstdc[:, :], func=AF.Exp, scale=-0.5)
            with block() as b:
                @b.vector
                def _(v):
                    v.tensor_scalar(out=ynrm[:, :], in0=ygat[:, :], scalar1=rstd2c[:, 0:1],
                                    scalar2=None, op0=ALU.mult)
            with block() as b:
                @b.tensor
                def _(tt):
                    for j in range(4):
                        tt.matmul(pf[:, j * 512:(j + 1) * 512], jrevs[:, :],
                                  ynrm[:, j * 512:(j + 1) * 512], start=True, stop=True)
            with block() as b:
                @b.vector
                def _(v):
                    for j in range(2):
                        v.tensor_copy(out=yrev[:, j * 512:(j + 1) * 512],
                                      in_=pf[:, j * 512:(j + 1) * 512])
                @b.scalar
                def _(s):
                    for j in range(2, 4):
                        s.copy(out=yrev[:, j * 512:(j + 1) * 512],
                               in_=pf[:, j * 512:(j + 1) * 512])

            def op_pass(src, dst_dram, slot):
                with block() as b:
                    @b.tensor
                    def _(tt, src=src):
                        for kc in range(16):
                            tt.transpose(pgb[:, kc * 128:(kc + 1) * 128],
                                         src[:, kc * 128:(kc + 1) * 128], idb[:, :])
                    @b.gpsimd
                    def _(g):
                        store_flush(g)
                with block() as b:
                    @b.vector
                    def _(v):
                        for kc in range(8):
                            v.tensor_copy(out=ytfm[:, kc * 128:(kc + 1) * 128],
                                          in_=pgb[:, kc * 128:(kc + 1) * 128])
                    @b.scalar
                    def _(s):
                        for kc in range(8, 16):
                            s.copy(out=ytfm[:, kc * 128:(kc + 1) * 128],
                                   in_=pgb[:, kc * 128:(kc + 1) * 128])
                with block() as b:
                    @b.tensor
                    def _(tt):
                        for m in range(8):
                            for k in range(16):
                                tt.matmul(pf[:, m * 128:(m + 1) * 128],
                                          woutb[:, k * D_MODEL + m * 128:
                                                k * D_MODEL + (m + 1) * 128],
                                          ytfm[:, k * 128:(k + 1) * 128],
                                          start=(k == 0), stop=(k == 15))
                    @b.gpsimd
                    def _(g):
                        cast_feed(g)
                with block() as b:
                    @b.vector
                    def _(v):
                        for m in range(4):
                            v.tensor_copy(out=mstage[:, m * 128:(m + 1) * 128],
                                          in_=pf[:, m * 128:(m + 1) * 128])
                    @b.scalar
                    def _(s):
                        for m in range(4, 8):
                            s.copy(out=mstage[:, m * 128:(m + 1) * 128],
                                   in_=pf[:, m * 128:(m + 1) * 128])
                pending_stores.append(
                    (dst_dram.ap().rearrange("p (k q) -> p k q", k=8)
                     [:, :, slot * 128:(slot + 1) * 128],
                     mstage[:, :].rearrange("p (k q) -> p k q", k=8)))

            op_pass(yrev, arin, 31 - cg)
            if cg < 16:
                op_pass(ynrm, mnat, cg)

            with block() as b:
                @b.vector
                def _(v):
                    v.tensor_tensor(
                        out=xts[:, :].rearrange("p (h q) -> p h q", h=32),
                        in0=xtok[:, :].rearrange("p (h q) -> p h q", h=32),
                        in1=ffac[:, :].rearrange("p h -> p h ()").broadcast_to((128, 32, 64)),
                        op=ALU.mult)
            with block() as b:
                @b.tensor
                def _(tt):
                    for h in range(32):
                        tt.matmul(pf[:, h * 64:(h + 1) * 64], btok[:, :],
                                  xts[:, h * 64:(h + 1) * 64], start=True, stop=True)
                @b.gpsimd
                def _(g):
                    store_flush(g)
                    cast_feed(g)
            with block() as b:
                @b.vector
                def _(v):
                    v.tensor_tensor(
                        out=ytok[:, :].rearrange("p (h q) -> p h q", h=32),
                        in0=hstf[:, :].rearrange("p (h q) -> p h q", h=32),
                        in1=edec[:, :].rearrange("p h -> p h ()").broadcast_to((128, 32, 64)),
                        op=ALU.mult)
            with block() as b:
                @b.vector
                def _(v):
                    v.tensor_tensor(out=hstf[:, :], in0=ytok[:, :], in1=pf[:, 0:2048],
                                    op=ALU.add)
            with block() as b:
                @b.scalar
                def _(s):
                    s.copy(out=hstb[:, :], in_=hstf[:, :])

    with block() as b:
        @b.gpsimd
        def _(g):
            g.collective_compute(
                "AllReduce", ALU.add,
                replica_groups=[[0, 1], [2, 3], [4, 5], [6, 7]],
                ins=[arin.ap().opt()], outs=[arout.ap().opt()],
            ).then_inc(ccs)
            cc_n[0] += 1
            g.wait_ge(ccs, cc_n[0])

    if "dbg_mrev" in dbg:
        with block() as b:
            @b.gpsimd
            def _(g):
                gq.go(g, dbg["dbg_mrev"][:, :], arin[:, :])
                gq.wait(g)
    if "dbg_mnat" in dbg:
        with block() as b:
            @b.gpsimd
            def _(g):
                gq.go(g, dbg["dbg_mnat"][:, :], mnat[:, :])
                gq.wait(g)

    if cast_chunks or pending_stores:
        with block() as b:
            @b.gpsimd
            def _(g):
                store_flush(g)
                cast_feed(g, len(cast_chunks))

    rwf = gtile
    with block() as b:
        @b.gpsimd
        def _(g):
            gq.go(g, rwf[:, :].rearrange("p (k e) -> p k e", k=8),
                  d["rwT"].ap().rearrange("(k p) e -> p k e", p=128))
            gq.go(g, dstage[:, 0:16], d["ebias"][:, 0:16])
            gq.go(g, ffac[:, 0:16], d["tiew"][:, 0:16])
            gq.go(g, ffac[:, 16:20], d["tiew4"][:, 0:4])
            g.memset(rmscol[:, :], 0.0)
            gq.wait(g)
    ebias_t = ellT
    tiew_t = dtT
    with block() as b:
        @b.gpsimd
        def _(g):
            gq.go(g, ebias_t[:, 0:16], d["ebias"][:, 0:16])
            gq.go(g, tiew_t[:, 0:16], d["tiew"][:, 0:16])
            gq.go(g, tiew_t[:, 16:20], d["tiew4"][:, 0:4])
            gq.wait(g)

    rwh = mstage[:, 0:128]
    rwl = mstage[:, 128:256]
    with block() as b:
        @b.vector
        def _(v):
            v.tensor_copy(out=rwh, in_=rwf[:, :])
    with block() as b:
        @b.vector
        def _(v):
            v.tensor_tensor(out=rwl, in0=rwf[:, :], in1=rwh, op=ALU.subtract)

    NTB = 8
    TB = 256
    gww = siluz
    uww = xbcraw
    dww_a = lhsT_all[:, 0:4096]
    dww_b = wslice[:, 0:4096]

    def moe_w_load(g, e, half):
        moe_w_load_gu(g, e, half)
        moe_w_load_d(g, e, half)

    def moe_w_load_gu(g, e, half):
        gq.go(g, gww[:, half * 4096:(half + 1) * 4096].rearrange("p (k i) -> p k i", k=8),
              gwb.ap().rearrange("e (k p) i -> p (e k) i", p=128)[:, e * 8:(e + 1) * 8, :])
        gq.go(g, uww[:, half * 4096:(half + 1) * 4096].rearrange("p (k i) -> p k i", k=8),
              uwb.ap().rearrange("e (k p) i -> p (e k) i", p=128)[:, e * 8:(e + 1) * 8, :])

    def moe_w_load_d(g, e, half):
        dst = dww_a if half == 0 else dww_b
        gq.go(g, dst.rearrange("p (k f) -> p k f", k=4),
              dwb.ap().rearrange("e (k p) f -> p (e k) f", p=128)[:, e * 4:(e + 1) * 4, :])

    moe_tgt = {}
    dwring = [lhsT_all[:, 0:4096], wslice[:, 0:4096], wslice[:, 4096:8192],
              difsb[:, :].bitcast(BF16)[:, 0:4096]]
    hbuf = [ygat[:, 0:1024], ygat[:, 1024:2048]]
    gsbuf = [xts[:, 0:1024], xts[:, 1024:2048]]
    usbuf = [ytmp2[:, 0:1024], ytmp2[:, 1024:2048]]

    def moe_w_load2(g, e):
        half = e % 2
        gq.go(g, gww[:, half * 4096:(half + 1) * 4096]
          .rearrange("p (k i) -> p k i", k=8),
          gwb.ap().rearrange("e (k p) i -> p (e k) i", p=128)
          [:, e * 8:(e + 1) * 8, :])
        gq.go(g, uww[:, half * 4096:(half + 1) * 4096]
          .rearrange("p (k i) -> p k i", k=8),
          uwb.ap().rearrange("e (k p) i -> p (e k) i", p=128)
          [:, e * 8:(e + 1) * 8, :])
        gq.go(g, dwring[e % 3].rearrange("p (k f) -> p k f", k=4),
          dwb.ap().rearrange("e (k p) f -> p (e k) f", p=128)
          [:, e * 4:(e + 1) * 4, :])

    for ti in range(NTB):
        tok0 = ti * TB
        with block() as b:
            @b.gpsimd
            def _(g, tok0=tok0, ti=ti):
                for k in range(8):
                    gq.go(g, xpart(k)[:, 0:TB],
                          d["x"].ap().rearrange("(k p) q -> p k q", p=128)
                          [:, k, tok0:tok0 + TB])
                gq.go(g, xtok[:, :].rearrange("p (k q) -> p k q", k=8),
                      arout.ap().rearrange("p (k q) -> p k q", k=8)[:, :, tok0:tok0 + TB])
                gq.go(g, szt[:, :].rearrange("p (k q) -> p k q", k=8),
                      arin.ap().rearrange("p (k q) -> p k q", k=8)[:, :, tok0:tok0 + TB])
                gq.go(g, ynrm[:, :].rearrange("p (k q) -> p k q", k=8),
                      mnat.ap().rearrange("p (k q) -> p k q", k=8)[:, :, tok0:tok0 + TB])
                moe_w_load(g, 0, 0)
                moe_tgt[0] = gq.n
                gq.wait(g)
        if "dbg_y" in dbg and ti == 0:
            with block() as b:
                @b.gpsimd
                def _(g):
                    for k in range(8):
                        gq.go(g, dbg["dbg_y"][:, k * TB:(k + 1) * TB], xpart(k)[:, 0:TB])
                    gq.wait(g)
        with block() as b:
            @b.vector
            def _(v):
                v.tensor_tensor(out=yrev[:, :], in0=xtok[:, :], in1=szt[:, :],
                                op=ALU.subtract)
        with block() as b:
            @b.vector
            def _(v):
                v.tensor_tensor(out=yrev[:, :], in0=yrev[:, :], in1=ynrm[:, :], op=ALU.add)
        if "dbg_s" in dbg:
            with block() as b:
                @b.gpsimd
                def _(g, tok0=tok0):
                    gq.go(g, dbg["dbg_s"][:, :].rearrange("p (k q) -> p k q", k=8)
                          [:, :, tok0:tok0 + TB],
                          yrev[:, :].rearrange("p (k q) -> p k q", k=8))
                    gq.wait(g)
        with block() as b:
            @b.vector
            def _(v):
                for k in range(8):
                    v.tensor_copy(out=expsb[:, k * TB:(k + 1) * TB], in_=xpart(k)[:, 0:TB])
            @b.scalar
            def _(s):
                for k in range(8):
                    s.activation(out=hstb[:, k * TB:(k + 1) * TB], in_=xpart(k)[:, 0:TB],
                                 func=AF.Square)
        with block() as b:
            @b.tensor
            def _(tt):
                for k in range(8):
                    tt.matmul(pg[0:1, 0:TB], onesb[:, 0:1], expsb[:, k * TB:(k + 1) * TB],
                              start=(k == 0), stop=(k == 7))
                for k in range(8):
                    tt.matmul(pg[0:1, 512:512 + TB], onesb[:, 0:1],
                              hstb[:, k * TB:(k + 1) * TB],
                              start=(k == 0), stop=(k == 7))
        with block() as b:
            @b.vector
            def _(v):
                v.tensor_scalar(out=mrow[0:1, 0:TB], in0=pg[0:1, 0:TB], scalar1=1.0 / 1024,
                                scalar2=None, op0=ALU.mult)
                v.tensor_scalar(out=e2row[0:1, 0:TB], in0=pg[0:1, 512:512 + TB],
                                scalar1=1.0 / 1024, scalar2=None, op0=ALU.mult)
        with block() as b:
            @b.vector
            def _(v):
                v.tensor_tensor(out=vrow[0:1, 0:TB], in0=mrow[0:1, 0:TB],
                                in1=mrow[0:1, 0:TB], op=ALU.mult)
        with block() as b:
            @b.vector
            def _(v):
                v.tensor_tensor(out=vrow2[0:1, 0:TB], in0=e2row[0:1, 0:TB],
                                in1=vrow[0:1, 0:TB], op=ALU.subtract)
        with block() as b:
            @b.scalar
            def _(s):
                s.activation(out=vrow[0:1, 0:TB], in_=vrow2[0:1, 0:TB], func=AF.Ln,
                             bias=epsc[0:1, 0:1])
        with block() as b:
            @b.scalar
            def _(s):
                s.activation(out=rstdr[0:1, 0:TB], in_=vrow[0:1, 0:TB], func=AF.Exp,
                             scale=-0.5)
        with block() as b:
            @b.tensor
            def _(tt):
                tt.matmul(pq[:, 0:TB], onesf[0:1, :], mrow[0:1, 0:TB], start=True, stop=True)
                tt.matmul(pq[:, 512:512 + TB], onesf[0:1, :], rstdr[0:1, 0:TB],
                          start=True, stop=True)
        with block() as b:
            @b.vector
            def _(v):
                for k in range(8):
                    v.tensor_tensor(out=difsb[:, k * TB:(k + 1) * TB],
                                    in0=xpart(k)[:, 0:TB], in1=pq[:, 0:TB], op=ALU.subtract)
        with block() as b:
            @b.vector
            def _(v):
                for k in range(8):
                    v.tensor_tensor(out=difsb[:, k * TB:(k + 1) * TB],
                                    in0=difsb[:, k * TB:(k + 1) * TB],
                                    in1=pq[:, 512:512 + TB], op=ALU.mult)
        with block() as b:
            @b.vector
            def _(v):
                v.tensor_copy(out=ytfm[:, :], in_=difsb[:, 0:2048])
        with block() as b:
            @b.vector
            def _(v):
                v.tensor_tensor(out=expsb[:, :], in0=difsb[:, 0:2048], in1=ytfm[:, :],
                                op=ALU.subtract)
        if "dbg_ffn" in dbg and ti == 0:
            with block() as b:
                @b.gpsimd
                def _(g):
                    gq.go(g, dbg["dbg_ffn"][:, 0:2048], ytfm[:, :])
                    for k in range(8):
                        gq.go(g, dbg["dbg_ffn"][:, 2048 + k * TB:2048 + (k + 1) * TB],
                              xpart(k)[:, 0:TB])
                    gq.wait(g)
        if "dbg_upr" in dbg and ti == 0:
            with block() as b:
                @b.gpsimd
                def _(g):
                    gq.go(g, dbg["dbg_upr"][:, 0:2048], ytfm[:, :])
                    gq.go(g, dbg["dbg_upr"][:, 2048:4096], expsb[:, :])
                    gq.go(g, dbg["dbg_upr"][:, 4096:4224], rwh)
                    gq.go(g, dbg["dbg_upr"][:, 4224:4352], rwl)
                    gq.wait(g)
        with block() as b:
            @b.tensor
            def _(tt):
                for k in range(8):
                    for lhs, rhs in ((rwh, ytfm), (rwh, expsb), (rwl, ytfm)):
                        tt.matmul(pg[0:16, 0:TB],
                                  lhs[:, k * 16:(k + 1) * 16],
                                  rhs[:, k * TB:(k + 1) * TB],
                                  start=(k == 0 and lhs is rwh and rhs is ytfm),
                                  stop=(k == 7 and lhs is rwl))
        with block() as b:
            @b.vector
            def _(v):
                v.tensor_copy(out=dtraw[0:16, 0:TB], in_=pg[0:16, 0:TB])
        with block() as b:
            @b.tensor
            def _(tt):
                for j in range(2):
                    tt.transpose(pq[0:128, j * 512:j * 512 + 16],
                                 dtraw[0:16, j * 128:(j + 1) * 128], idf[0:16, 0:16])
        with block() as b:
            @b.vector
            def _(v):
                v.tensor_copy(out=ecolT[:, 0:16], in_=pq[0:128, 0:16])
                v.tensor_copy(out=ecolT[:, 16:32], in_=pq[0:128, 512:528])
        with block() as b:
            @b.scalar
            def _(s):
                s.activation(out=edec[:, 0:32], in_=ecolT[:, 0:32], func=AF.Sigmoid)
        with block() as b:
            @b.vector
            def _(v):
                v.tensor_tensor(
                    out=ftmp2[:, 0:32].rearrange("p (s e) -> p s e", s=2),
                    in0=edec[:, 0:32].rearrange("p (s e) -> p s e", s=2),
                    in1=ebias_t[:, 0:16].rearrange("p e -> p () e").broadcast_to((128, 2, 16)),
                    op=ALU.add)
        gsv = difsb[:, 0:8]
        with block() as b:
            @b.vector
            def _(v):
                scg = ftmp2[:, 0:32].rearrange("p (sg j) -> p sg j", j=4)
                for i, (a, bb) in enumerate([(0, 1), (0, 2), (0, 3), (1, 2), (1, 3), (2, 3)]):
                    v.tensor_tensor(
                        out=difsb[:, 1024 + i * 8:1032 + i * 8].rearrange("p (sg) -> p sg ()"),
                        in0=scg[:, :, a:a + 1], in1=scg[:, :, bb:bb + 1], op=ALU.add)
        with block() as b:
            @b.vector
            def _(v):
                v.tensor_tensor(out=difsb[:, 1072:1080], in0=difsb[:, 1024:1032],
                                in1=difsb[:, 1032:1040], op=ALU.max)
                v.tensor_tensor(out=difsb[:, 1080:1088], in0=difsb[:, 1040:1048],
                                in1=difsb[:, 1048:1056], op=ALU.max)
                v.tensor_tensor(out=difsb[:, 1088:1096], in0=difsb[:, 1056:1064],
                                in1=difsb[:, 1064:1072], op=ALU.max)
        with block() as b:
            @b.vector
            def _(v):
                v.tensor_tensor(out=difsb[:, 1096:1104], in0=difsb[:, 1072:1080],
                                in1=difsb[:, 1080:1088], op=ALU.max)
        with block() as b:
            @b.vector
            def _(v):
                v.tensor_tensor(out=gsv, in0=difsb[:, 1096:1104],
                                in1=difsb[:, 1088:1096], op=ALU.max)
        if "dbg_upr" in dbg and ti == 0:
            with block() as b:
                @b.gpsimd
                def _(g):
                    gq.go(g, dbg["dbg_upr"][:, 4352:4360], gsv)
                    gq.wait(g)
        gmask = difsb[:, 16:24]
        with block() as b:
            @b.vector
            def _(v):
                v.tensor_scalar(out=gmask, in0=gsv, scalar1=0.0, scalar2=None, op0=ALU.mult)
        for it in range(2):
            with block() as b:
                @b.vector
                def _(v):
                    for sj in range(2):
                        gss = difsb[:, sj * 4:(sj + 1) * 4]
                        v.tensor_reduce(out=difsb[:, 24 + sj:25 + sj], in_=gss, axis=AX,
                                        op=ALU.max)
            with block() as b:
                @b.vector
                def _(v):
                    for sj in range(2):
                        gss = difsb[:, sj * 4:(sj + 1) * 4]
                        v.tensor_scalar(out=difsb[:, 32 + sj * 4:32 + (sj + 1) * 4],
                                        in0=gss, scalar1=difsb[:, 24 + sj:25 + sj],
                                        scalar2=None, op0=ALU.is_equal)
            with block() as b:
                @b.vector
                def _(v):
                    v.tensor_tensor(out=difsb[:, 32:40].rearrange("p (s j) -> p s j", s=2),
                                    in0=difsb[:, 32:40].rearrange("p (s j) -> p s j", s=2),
                                    in1=tiew_t[:, 16:20].rearrange("p j -> p () j")
                                    .broadcast_to((128, 2, 4)),
                                    op=ALU.mult)
            with block() as b:
                @b.vector
                def _(v):
                    for sj in range(2):
                        v.tensor_reduce(out=difsb[:, 24 + sj:25 + sj],
                                        in_=difsb[:, 32 + sj * 4:32 + (sj + 1) * 4],
                                        axis=AX, op=ALU.max)
            with block() as b:
                @b.vector
                def _(v):
                    for sj in range(2):
                        v.tensor_scalar(out=difsb[:, 40 + sj * 4:40 + (sj + 1) * 4],
                                        in0=difsb[:, 32 + sj * 4:32 + (sj + 1) * 4],
                                        scalar1=difsb[:, 24 + sj:25 + sj], scalar2=None,
                                        op0=ALU.is_equal)
            with block() as b:
                @b.vector
                def _(v):
                    v.tensor_tensor(out=gmask, in0=gmask, in1=difsb[:, 40:48], op=ALU.add)
                    v.scalar_tensor_tensor(out=difsb[:, 0:8], in0=difsb[:, 40:48],
                                           scalar=-1e30, in1=difsb[:, 0:8],
                                           op0=ALU.mult, op1=ALU.add)
        with block() as b:
            @b.vector
            def _(v):
                v.tensor_copy(
                    out=difsb[:, 64:96].rearrange("p (sg j) -> p sg j", j=4),
                    in_=gmask.rearrange("p sg -> p sg ()").broadcast_to((128, 8, 4)))
        with block() as b:
            @b.vector
            def _(v):
                v.tensor_tensor(out=difsb[:, 96:128], in0=ftmp2[:, 0:32],
                                in1=difsb[:, 64:96], op=ALU.mult)
        sel = difsb[:, 128:160]
        with block() as b:
            @b.vector
            def _(v):
                v.tensor_scalar(out=sel, in0=difsb[:, 96:128], scalar1=0.0,
                                scalar2=None, op0=ALU.mult)
        for it in range(4):
            with block() as b:
                @b.vector
                def _(v):
                    for sj in range(2):
                        v.tensor_reduce(out=difsb[:, 24 + sj:25 + sj],
                                        in_=difsb[:, 96 + sj * 16:96 + (sj + 1) * 16],
                                        axis=AX, op=ALU.max)
            with block() as b:
                @b.vector
                def _(v):
                    for sj in range(2):
                        v.tensor_scalar(out=difsb[:, 160 + sj * 16:160 + (sj + 1) * 16],
                                        in0=difsb[:, 96 + sj * 16:96 + (sj + 1) * 16],
                                        scalar1=difsb[:, 24 + sj:25 + sj], scalar2=None,
                                        op0=ALU.is_equal)
            with block() as b:
                @b.vector
                def _(v):
                    v.tensor_tensor(out=difsb[:, 160:192].rearrange("p (s e) -> p s e", s=2),
                                    in0=difsb[:, 160:192].rearrange("p (s e) -> p s e", s=2),
                                    in1=tiew_t[:, 0:16].rearrange("p e -> p () e")
                                    .broadcast_to((128, 2, 16)),
                                    op=ALU.mult)
            with block() as b:
                @b.vector
                def _(v):
                    for sj in range(2):
                        v.tensor_reduce(out=difsb[:, 24 + sj:25 + sj],
                                        in_=difsb[:, 160 + sj * 16:160 + (sj + 1) * 16],
                                        axis=AX, op=ALU.max)
            with block() as b:
                @b.vector
                def _(v):
                    for sj in range(2):
                        v.tensor_scalar(out=difsb[:, 192 + sj * 16:192 + (sj + 1) * 16],
                                        in0=difsb[:, 160 + sj * 16:160 + (sj + 1) * 16],
                                        scalar1=difsb[:, 24 + sj:25 + sj], scalar2=None,
                                        op0=ALU.is_equal)
            with block() as b:
                @b.vector
                def _(v):
                    v.tensor_tensor(out=sel, in0=sel, in1=difsb[:, 192:224], op=ALU.add)
                    v.scalar_tensor_tensor(out=difsb[:, 96:128], in0=difsb[:, 192:224],
                                           scalar=-1e30, in1=difsb[:, 96:128],
                                           op0=ALU.mult, op1=ALU.add)
        if "dbg_upr" in dbg and ti == 0:
            with block() as b:
                @b.gpsimd
                def _(g):
                    gq.go(g, dbg["dbg_upr"][:, 4360:4368], gmask)
                    gq.go(g, dbg["dbg_upr"][:, 4368:4400], sel)
                    gq.go(g, dbg["dbg_upr"][:, 4400:4432], difsb[:, 64:96])
                    gq.wait(g)
        with block() as b:
            @b.vector
            def _(v):
                v.tensor_tensor(out=difsb[:, 224:256], in0=edec[:, 0:32], in1=sel,
                                op=ALU.mult)
        with block() as b:
            @b.vector
            def _(v):
                for sj in range(2):
                    v.tensor_reduce(out=difsb[:, 24 + sj:25 + sj],
                                    in_=difsb[:, 224 + sj * 16:224 + (sj + 1) * 16],
                                    axis=AX, op=ALU.add)
        with block() as b:
            @b.vector
            def _(v):
                v.tensor_scalar(out=difsb[:, 24:26], in0=difsb[:, 24:26], scalar1=1e-20,
                                scalar2=None, op0=ALU.add)
        with block() as b:
            @b.vector
            def _(v):
                v.reciprocal(out=difsb[:, 26:28], in_=difsb[:, 24:26])
        with block() as b:
            @b.vector
            def _(v):
                for sj in range(2):
                    v.tensor_scalar(out=difsb[:, 224 + sj * 16:224 + (sj + 1) * 16],
                                    in0=difsb[:, 224 + sj * 16:224 + (sj + 1) * 16],
                                    scalar1=difsb[:, 26 + sj:27 + sj], scalar2=2.5,
                                    op0=ALU.mult, op1=ALU.mult)
        if "dbg_scores" in dbg and ti == 0:
            with block() as b:
                @b.gpsimd
                def _(g):
                    gq.go(g, dbg["dbg_scores"][:, 0:32], ecolT[:, 0:32])
                    gq.go(g, dbg["dbg_scores"][0:16, 32:64].rearrange(
                        "p q -> p () q") if False else dbg["dbg_scores"][0:16, 32:64],
                        dtraw[0:16, 0:32])
                    gq.wait(g)
        if "dbg_comb" in dbg and ti == 0:
            with block() as b:
                @b.vector
                def _(v):
                    v.tensor_copy(out=ftmp[:, 0:32], in_=difsb[:, 224:256])
            with block() as b:
                @b.gpsimd
                def _(g):
                    gq.go(g, dbg["dbg_comb"][:, 0:32], ftmp[:, 0:32])
                    gq.go(g, dbg["dbg_comb"][:, 32:64], edec[:, 0:32])
                    gq.wait(g)
        with block() as b:
            @b.tensor
            def _(tt):
                for j in range(2):
                    tt.transpose(pq[0:16, j * 512:j * 512 + 128],
                                 difsb[:, 224 + j * 16:224 + (j + 1) * 16],
                                 idf[:, :])
        with block() as b:
            @b.vector
            def _(v):
                for j in range(2):
                    v.tensor_copy(out=dta[0:16, j * 128:(j + 1) * 128],
                                  in_=pq[0:16, j * 512:j * 512 + 128])
        if "dbg_scores" in dbg and ti == 0:
            with block() as b:
                @b.gpsimd
                def _(g):
                    gq.go(g, dbg["dbg_scores"][0:16, 0:32], dta[0:16, 0:32])
                    gq.wait(g)
        with block() as b:
            @b.gpsimd
            def _(g):
                gq.go(g, combuf.ap()[0:32, :].rearrange("(e a) q -> e (a q)", a=2),
                      dta[0:16, 0:TB])
                gq.wait(g)
                gq.go(g, lstk[0:1, :],
                      combuf.ap().rearrange("p q -> () (p q)"))
                gq.wait(g)
        if "dbg_scores" in dbg and ti == 0:
            with block() as b:
                @b.gpsimd
                def _(g):
                    gq.go(g, dbg["dbg_scores"][0:1, 32:64], lstk[0:1, 0:32])
                    gq.wait(g)
        def moe_down(tt, e):
            dw = dww_a if e % 2 == 0 else dww_b
            for m in range(8):
                for ic in range(4):
                    tt.matmul(pf[:, m * TB:(m + 1) * TB],
                              dw[:, ic * 1024 + m * 128:ic * 1024 + (m + 1) * 128],
                              ygat[:, 1024 + ic * TB:1024 + (ic + 1) * TB],
                              start=(e == 0 and ic == 0 and m % 2 == 0),
                              stop=False)

        for e in range(16):
            half = e % 2
            goff = half * 4096
            with block() as b:
                @b.tensor
                def _(tt, goff=goff, e=e):
                    if e > 0:
                        moe_down(tt, e - 1)
                    tt.wait_ge(gq.h, moe_tgt[e])
                    for ic in range(4):
                        for k in range(8):
                            tt.matmul(pq[:, ic * TB:(ic + 1) * TB],
                                      gww[:, goff + k * 512 + ic * 128:
                                          goff + k * 512 + (ic + 1) * 128],
                                      ytfm[:, k * TB:(k + 1) * TB],
                                      start=(k == 0), stop=(k == 7))
                    for ic in range(4):
                        for k in range(8):
                            tt.matmul(pg[:, ic * TB:(ic + 1) * TB],
                                      uww[:, goff + k * 512 + ic * 128:
                                          goff + k * 512 + (ic + 1) * 128],
                                      ytfm[:, k * TB:(k + 1) * TB],
                                      start=(k == 0), stop=(k == 7))
                @b.gpsimd
                def _(g, e=e):
                    if e + 1 < 16:
                        moe_w_load_gu(g, e + 1, (e + 1) % 2)
                        moe_tgt[e + 1] = gq.n
            with block() as b:
                @b.scalar
                def _(s):
                    s.activation(out=xts[:, 0:1024], in_=pq[:, 0:1024], func=AF.Silu)
                @b.vector
                def _(v):
                    v.tensor_copy(out=xts[:, 1024:2048], in_=pg[:, 0:1024])
                @b.gpsimd
                def _(g, e=e):
                    if e + 1 < 16:
                        moe_w_load_d(g, e + 1, (e + 1) % 2)
            with block() as b:
                @b.tensor
                def _(tt, e=e):
                    tt.matmul(pg[:, 0:TB], onesrb[0:1, :],
                              lstk[0:1, e * TB:(e + 1) * TB], start=True, stop=True)
            with block() as b:
                @b.vector
                def _(v):
                    v.tensor_copy(out=difsb[:, 256:256 + TB], in_=pg[:, 0:TB])
                    v.tensor_tensor(
                        out=ygat[:, 0:1024].rearrange("p (i q) -> p i q", i=4),
                        in0=xts[:, 1024:2048].rearrange("p (i q) -> p i q", i=4),
                        in1=difsb[:, 256:256 + TB].rearrange("p q -> p () q")
                        .broadcast_to((128, 4, TB)),
                        op=ALU.mult)
                    v.tensor_tensor(out=ygat[:, 1024:2048], in0=xts[:, 0:1024],
                                    in1=ygat[:, 0:1024], op=ALU.mult)
        shv = shb.ap().rearrange("e (k p) f -> p (e k) f", p=128)
        with block() as b:
            @b.tensor
            def _(tt):
                moe_down(tt, 15)
            @b.gpsimd
            def _(g):
                gq.go(g, gww[:, :].rearrange("p (k f) -> p k f", k=8), shv[:, 0:8, :])
                gq.go(g, uww[:, 0:8192].rearrange("p (k f) -> p k f", k=8), shv[:, 8:16, :])
        sh_gu_tgt = gq.n
        with block() as b:
            @b.tensor
            def _(tt):
                tt.wait_ge(gq.h, sh_gu_tgt)
                for m in range(8):
                    dst = pq[:, (m % 4) * TB:(m % 4 + 1) * TB] if m < 4                         else pg[:, (m % 4) * TB:(m % 4 + 1) * TB]
                    for k in range(8):
                        tt.matmul(dst, gww[:, k * 1024 + m * 128:k * 1024 + (m + 1) * 128],
                                  ytfm[:, k * TB:(k + 1) * TB],
                                  start=(k == 0), stop=(k == 7))
            @b.gpsimd
            def _(g):
                gq.go(g, dww_a.rearrange("p (k f) -> p k f", k=4), shv[:, 16:20, :])
                gq.go(g, dww_b.rearrange("p (k f) -> p k f", k=4), shv[:, 20:24, :])
        sh_d_tgt = gq.n
        with block() as b:
            @b.scalar
            def _(s):
                s.activation(out=xts[:, 0:1024], in_=pq[:, 0:1024], func=AF.Silu)
                s.activation(out=xts[:, 1024:2048], in_=pg[:, 0:1024], func=AF.Silu)
        with block() as b:
            @b.tensor
            def _(tt):
                for m in range(8):
                    dst = pq[:, (m % 4) * TB:(m % 4 + 1) * TB] if m < 4                         else pg[:, (m % 4) * TB:(m % 4 + 1) * TB]
                    for k in range(8):
                        tt.matmul(dst, uww[:, k * 1024 + m * 128:k * 1024 + (m + 1) * 128],
                                  ytfm[:, k * TB:(k + 1) * TB],
                                  start=(k == 0), stop=(k == 7))
        with block() as b:
            @b.vector
            def _(v):
                v.tensor_tensor(out=hstb[:, 0:1024], in0=xts[:, 0:1024], in1=pq[:, 0:1024],
                                op=ALU.mult)
                v.tensor_tensor(out=hstb[:, 1024:2048], in0=xts[:, 1024:2048],
                                in1=pg[:, 0:1024], op=ALU.mult)
        with block() as b:
            @b.tensor
            def _(tt):
                tt.wait_ge(gq.h, sh_d_tgt)
                for m in range(8):
                    for k in range(8):
                        dw = dww_a if k < 4 else dww_b
                        tt.matmul(pf[:, m * TB:(m + 1) * TB],
                                  dw[:, (k % 4) * 1024 + m * 128:(k % 4) * 1024 + (m + 1) * 128],
                                  hstb[:, k * TB:(k + 1) * TB],
                                  start=False, stop=(k == 7))
        with block() as b:
            @b.scalar
            def _(s):
                s.activation(out=hstb[:, :], in_=pf[:, 0:2048], func=AF.Silu)
        if "dbg_g" in dbg:
            with block() as b:
                @b.gpsimd
                def _(g, tok0=tok0):
                    gq.go(g, dbg["dbg_g"][:, :].rearrange("p (k q) -> p k q", k=8)
                          [:, :, tok0:tok0 + TB],
                          hstb[:, :].rearrange("p (k q) -> p k q", k=8))
                    gq.wait(g)
        with block() as b:
            @b.vector
            def _(v):
                v.tensor_tensor(out=ytmp2[:, :], in0=yrev[:, :], in1=hstb[:, :], op=ALU.mult)
        with block() as b:
            @b.vector
            def _(v):
                for k in range(8):
                    v.tensor_tensor(out=xpart(k)[:, 0:TB], in0=xpart(k)[:, 0:TB],
                                    in1=ytmp2[:, k * TB:(k + 1) * TB], op=ALU.add)
        with block() as b:
            @b.gpsimd
            def _(g, tok0=tok0):
                for k in range(8):
                    gq.go(g, x1d.ap().rearrange("p (k q) -> p k q", k=8)
                          [:, k, tok0:tok0 + TB], xpart(k)[:, 0:TB])
                gq.wait(g)
        if "dbg_x1" in dbg:
            with block() as b:
                @b.gpsimd
                def _(g, tok0=tok0):
                    for k in range(8):
                        gq.go(g, dbg["dbg_x1"][:, :].rearrange("p (k q) -> p k q", k=8)
                              [:, k, tok0:tok0 + TB], xpart(k)[:, 0:TB])
                    gq.wait(g)
        with block() as b:
            @b.vector
            def _(v):
                for k in range(8):
                    v.tensor_copy(out=expsb[:, k * TB:(k + 1) * TB], in_=xpart(k)[:, 0:TB])
            @b.scalar
            def _(s):
                for k in range(8):
                    s.activation(out=hstb[:, k * TB:(k + 1) * TB], in_=xpart(k)[:, 0:TB],
                                 func=AF.Square)
        with block() as b:
            @b.tensor
            def _(tt):
                for k in range(8):
                    tt.matmul(pg[0:1, 0:TB], onesb[:, 0:1], expsb[:, k * TB:(k + 1) * TB],
                              start=(k == 0), stop=(k == 7))
                for k in range(8):
                    tt.matmul(pg[0:1, 512:512 + TB], onesb[:, 0:1],
                              hstb[:, k * TB:(k + 1) * TB],
                              start=(k == 0), stop=(k == 7))
        with block() as b:
            @b.vector
            def _(v):
                v.tensor_scalar(out=mrow[0:1, 0:TB], in0=pg[0:1, 0:TB], scalar1=1.0 / 1024,
                                scalar2=None, op0=ALU.mult)
                v.tensor_scalar(out=e2row[0:1, 0:TB], in0=pg[0:1, 512:512 + TB],
                                scalar1=1.0 / 1024, scalar2=None, op0=ALU.mult)
        with block() as b:
            @b.vector
            def _(v):
                v.tensor_tensor(out=vrow[0:1, 0:TB], in0=mrow[0:1, 0:TB],
                                in1=mrow[0:1, 0:TB], op=ALU.mult)
        with block() as b:
            @b.vector
            def _(v):
                v.tensor_tensor(out=vrow2[0:1, 0:TB], in0=e2row[0:1, 0:TB],
                                in1=vrow[0:1, 0:TB], op=ALU.subtract)
        with block() as b:
            @b.scalar
            def _(s):
                s.activation(out=vrow[0:1, 0:TB], in_=vrow2[0:1, 0:TB], func=AF.Ln,
                             bias=epsc[0:1, 0:1])
        with block() as b:
            @b.scalar
            def _(s):
                s.activation(out=rstdr[0:1, 0:TB], in_=vrow[0:1, 0:TB], func=AF.Exp,
                             scale=-0.5)
        with block() as b:
            @b.tensor
            def _(tt):
                tt.matmul(pq[:, 0:TB], onesf[0:1, :], mrow[0:1, 0:TB], start=True, stop=True)
                tt.matmul(pq[:, 512:512 + TB], onesf[0:1, :], rstdr[0:1, 0:TB],
                          start=True, stop=True)
        with block() as b:
            @b.vector
            def _(v):
                for k in range(8):
                    v.tensor_tensor(out=difsb[:, k * TB:(k + 1) * TB],
                                    in0=xpart(k)[:, 0:TB], in1=pq[:, 0:TB], op=ALU.subtract)
        with block() as b:
            @b.vector
            def _(v):
                for k in range(8):
                    v.tensor_tensor(out=ytfm[:, k * TB:(k + 1) * TB],
                                    in0=difsb[:, k * TB:(k + 1) * TB],
                                    in1=pq[:, 512:512 + TB], op=ALU.mult)
        with block() as b:
            @b.gpsimd
            def _(g, tok0=tok0):
                gq.go(g, xn2d.ap().rearrange("p (k q) -> p k q", k=8)
                      [:, :, tok0:tok0 + TB],
                      ytfm[:, :].rearrange("p (k q) -> p k q", k=8))
                gq.wait(g)
        if "dbg_xn2" in dbg:
            with block() as b:
                @b.gpsimd
                def _(g, tok0=tok0):
                    gq.go(g, dbg["dbg_xn2"][:, :].rearrange("p (k q) -> p k q", k=8)
                          [:, :, tok0:tok0 + TB],
                          ytfm[:, :].rearrange("p (k q) -> p k q", k=8))
                    gq.wait(g)
        with block() as b:
            @b.vector
            def _(v):
                v.tensor_reduce(out=rstdc[:, :], in_=ytfm[:, :], axis=AX, op=ALU.max,
                                apply_absolute_value=True)
        with block() as b:
            @b.vector
            def _(v):
                v.tensor_tensor(out=rmscol[:, :], in0=rmscol[:, :], in1=rstdc[:, :],
                                op=ALU.max)

    itile = ytmp[:, :].bitcast(I32)
    with block() as b:
        @b.gpsimd
        def _(g):
            gq.go(g, gww[:, :].rearrange("p (k f) -> p k f", k=8),
                  wvb.ap().rearrange("(k p) f -> p k f", p=128))
            g.memset(edec[:, 0:8], 0.0)
            g.memset(lendr[:, :], 0.0)
            gq.wait(g)
    for ti in range(8):
        tok0 = ti * 256
        with block() as b:
            @b.gpsimd
            def _(g, tok0=tok0):
                gq.go(g, ytfm[:, :].rearrange("p (k q) -> p k q", k=8),
                      xn2d.ap().rearrange("p (k q) -> p k q", k=8)[:, :, tok0:tok0 + 256])
                gq.wait(g)
        with block() as b:
            @b.tensor
            def _(tt):
                for m in range(8):
                    dst = pq[:, (m % 4) * 256:(m % 4 + 1) * 256] if m < 4 \
                        else pg[:, (m % 4) * 256:(m % 4 + 1) * 256]
                    for k in range(8):
                        tt.matmul(dst, gww[:, k * 1024 + m * 128:k * 1024 + (m + 1) * 128],
                                  ytfm[:, k * 256:(k + 1) * 256],
                                  start=(k == 0), stop=(k == 7))
        with block() as b:
            @b.vector
            def _(v):
                for m in range(8):
                    src = pq[:, (m % 4) * 256:(m % 4 + 1) * 256] if m < 4 \
                        else pg[:, (m % 4) * 256:(m % 4 + 1) * 256]
                    v.tensor_reduce(out=difsb[:, 24 + m:25 + m], in_=src, axis=AX,
                                    op=ALU.add)
        with block() as b:
            @b.vector
            def _(v):
                for m in range(8):
                    v.tensor_tensor(out=edec[:, m:m + 1], in0=edec[:, m:m + 1],
                                    in1=difsb[:, 24 + m:25 + m], op=ALU.add)
    with block() as b:
        @b.gpsimd
        def _(g):
            gq.go(g, vsin[:, :], edec[:, 0:8])
            gq.wait(g)
            g.collective_compute(
                "AllReduce", ALU.add,
                replica_groups=[[0, 1], [2, 3], [4, 5], [6, 7]],
                ins=[vsin.ap().opt()], outs=[vsout.ap().opt()],
            ).then_inc(ccs)
            cc_n[0] += 1
            g.wait_ge(ccs, cc_n[0])
            gq.go(g, edec[:, 8:16], vsout[:, :])
            gq.wait(g)
    with block() as b:
        @b.vector
        def _(v):
            v.tensor_copy(out=ffac[:, 0:8], in_=edec[:, 8:16])
    with block() as b:
        @b.gpsimd
        def _(g):
            gq.go(g, gww[:, :].rearrange("p (k f) -> p k f", k=8),
                  wob.ap().rearrange("(k p) f -> p k f", p=128))
            gq.wait(g)
    with block() as b:
        @b.tensor
        def _(tt):
            for m in range(8):
                for k in range(8):
                    tt.matmul(pq[:, m:m + 1],
                              gww[:, k * 1024 + m * 128:k * 1024 + (m + 1) * 128],
                              ffac[:, k:k + 1], start=(k == 0), stop=(k == 7))
    with block() as b:
        @b.vector
        def _(v):
            v.tensor_copy(out=ecolT[:, 0:8], in_=pq[:, 0:8])
    if "dbg_s1" in dbg:
        with block() as b:
            @b.gpsimd
            def _(g):
                gq.go(g, dbg["dbg_s1"][:, :], ecolT[:, 0:8])
                gq.wait(g)
    with block() as b:
        @b.tensor
        def _(tt):
            tt.transpose(pg[0:1, 0:128], rmscol[:, :], idf[:, :])
    with block() as b:
        @b.vector
        def _(v):
            v.tensor_copy(out=lendr[0:1, 0:16].rearrange("a (b q) -> a b q", b=1)
                          if False else lendE[0:1, 0:32], in_=pg[0:1, 0:32])
    with block() as b:
        @b.vector
        def _(v):
            v.tensor_copy(out=dtraw[0:1, 0:128], in_=pg[0:1, 0:128])
    with block() as b:
        @b.vector
        def _(v):
            v.tensor_reduce(out=lendr[0:1, 0:1], in_=dtraw[0:1, 0:128], axis=AX, op=ALU.max)
    with block() as b:
        @b.gpsimd
        def _(g):
            g.memset(dtraw[0:1, 128:144], 0.0)
    with block() as b:
        @b.vector
        def _(v):
            v.tensor_copy(out=dtraw[0:1, 128:129], in_=lendr[0:1, 0:1])
    with block() as b:
        @b.gpsimd
        def _(g):
            gq.go(g, armx[0:1, 0:16], dtraw[0:1, 128:144])
            gq.wait(g)
            g.collective_compute(
                "AllReduce", ALU.max, replica_groups=[list(range(8))],
                ins=[armx.ap().opt()], outs=[armxo.ap().opt()],
            ).then_inc(ccs)
            cc_n[0] += 1
            g.wait_ge(ccs, cc_n[0])
            gq.go(g, lendr[0:1, 0:16], armxo[0:1, 0:16])
            gq.wait(g)
    def make_scale(src_ap, dst_off):
        with block() as b:
            @b.vector
            def _(v):
                v.tensor_scalar(out=lendE[0:1, dst_off:dst_off + 1], in0=src_ap,
                                scalar1=1.0 / 32767.0, scalar2=1e-12,
                                op0=ALU.mult, op1=ALU.max)
        with block() as b:
            @b.vector
            def _(v):
                v.reciprocal(out=lendE[0:1, dst_off + 1:dst_off + 2],
                             in_=lendE[0:1, dst_off:dst_off + 1])
    make_scale(lendr[0:1, 0:1], 0)
    with block() as b:
        @b.tensor
        def _(tt):
            tt.matmul(pq[:, 0:2], onesf[0:1, :], lendE[0:1, 0:2], start=True, stop=True)
    with block() as b:
        @b.vector
        def _(v):
            v.tensor_copy(out=ftmp[:, 0:2], in_=pq[:, 0:2])

    def dequant(src_rows, n_iter, dstT, dst_row_of, dst_col_of):
        wring = [difsb[:, 0:1024], difsb[:, 1024:2048], ytok[:, 0:1024],
                 ytok[:, 1024:2048], hstf[:, 0:1024]]
        mxring = [gtile[:, 0:32], gtile[:, 32:64], gtile[:, 64:96]]
        scring = [ytmp[:, 1120:1152], ytmp[:, 1152:1184], ytmp[:, 1184:1216]]
        ivring = [ytmp[:, 1024:1056], ytmp[:, 1056:1088], ytmp[:, 1088:1120]]
        wqring = [ytfm[:, 0:1024], ytfm[:, 1024:2048]]
        evring = [ygat[:, 0:1024], ygat[:, 1024:2048]]
        iscr = ytmp[:, :].bitcast(I32)[:, 0:1024]
        for j in range(n_iter + 8):
            with block() as b:
                @b.gpsimd
                def _(g, j=j):
                    if j < n_iter:
                        gq.go(g, wring[j % 5], src_rows(j))
                    i = j - 7
                    if 0 <= i < n_iter:
                        for kc in range(8):
                            gq.go(g, dstT[dst_row_of(i, kc):dst_row_of(i, kc) + 128,
                                          dst_col_of(i):dst_col_of(i) + 128],
                                  evring[i % 2][:, kc * 128:(kc + 1) * 128])
                    gq.wait(g)
                @b.vector
                def _(v, j=j):
                    i = j - 1
                    if 0 <= i < n_iter:
                        v.tensor_reduce(
                            out=mxring[i % 3].rearrange("p (a q) -> p a q", q=1),
                            in_=wring[i % 5].rearrange("p (a q) -> p a q", q=32),
                            axis=AX, op=ALU.max, apply_absolute_value=True)
                    i = j - 2
                    if 0 <= i < n_iter:
                        v.tensor_scalar(out=scring[i % 3], in0=mxring[i % 3],
                                        scalar1=1.0 / 7.0, scalar2=1e-12,
                                        op0=ALU.mult, op1=ALU.max)
                    i = j - 3
                    if 0 <= i < n_iter:
                        v.reciprocal(out=ivring[i % 3], in_=scring[i % 3])
                    i = j - 4
                    if 0 <= i < n_iter:
                        w = wring[i % 5]
                        v.tensor_tensor(
                            out=w.rearrange("p (a q) -> p a q", q=32),
                            in0=w.rearrange("p (a q) -> p a q", q=32),
                            in1=ivring[i % 3].rearrange("p a -> p a ()")
                            .broadcast_to((128, 32, 32)), op=ALU.mult)
                        v.tensor_copy(out=iscr, in_=w)
                        v.tensor_copy(out=w, in_=iscr)
                        v.tensor_tensor(
                            out=wqring[i % 2].rearrange("p (a q) -> p a q", q=32),
                            in0=w.rearrange("p (a q) -> p a q", q=32),
                            in1=scring[i % 3].rearrange("p a -> p a ()")
                            .broadcast_to((128, 32, 32)), op=ALU.mult)
                    i = j - 6
                    if 0 <= i < n_iter:
                        v.tensor_copy(out=evring[i % 2],
                                      in_=pgb[:, (i % 2) * 1024:(i % 2) * 1024 + 1024])
                @b.tensor
                def _(tt, j=j):
                    i = j - 5
                    if 0 <= i < n_iter:
                        for kc in range(8):
                            tt.transpose(pgb[:, (i % 2) * 1024 + kc * 128:
                                             (i % 2) * 1024 + (kc + 1) * 128],
                                         wqring[i % 2][:, kc * 128:(kc + 1) * 128],
                                         idb[:, :])

    dequant(lambda r: d["wup"].ap()[r * 128:(r + 1) * 128, :], 24, wuqT,
            lambda r, kc: kc * 128, lambda r: r * 128)
    dequant(lambda r: d["wgate"].ap()[r * 128:(r + 1) * 128, :], 24, wgqT,
            lambda r, kc: kc * 128, lambda r: r * 128)
    dequant(lambda r: d["wdown"].ap()[(r // 3) * 128:(r // 3 + 1) * 128,
                                      (r % 3) * 1024:(r % 3 + 1) * 1024], 24, wdqT,
            lambda r, kc: (r % 3) * 1024 + kc * 128, lambda r: (r // 3) * 128)

    with block() as b:
        @b.gpsimd
        def _(g):
            g.memset(ftmp2[:, 0:4], 0.0)
    for ti in range(8):
        tok0 = ti * 256
        with block() as b:
            @b.gpsimd
            def _(g, tok0=tok0):
                gq.go(g, ytfm[:, :].rearrange("p (k q) -> p k q", k=8),
                      xn2d.ap().rearrange("p (k q) -> p k q", k=8)[:, :, tok0:tok0 + 256])
                gq.wait(g)
        with block() as b:
            @b.vector
            def _(v):
                v.tensor_scalar(out=difsb[:, 0:2048], in0=ytfm[:, :],
                                scalar1=ftmp[:, 1:2], scalar2=None, op0=ALU.mult)
                v.tensor_copy(out=itile[:, 0:2048], in_=difsb[:, 0:2048])
                v.tensor_copy(out=difsb[:, 0:2048], in_=itile[:, 0:2048])
                v.tensor_scalar(out=ytfm[:, :], in0=difsb[:, 0:2048],
                                scalar1=ftmp[:, 0:1], scalar2=None, op0=ALU.mult)
        steps = [(srcT, dst_t, mg)
                 for srcT, dst_t in [(wuqT, xbcraw), (wgqT, siluz)] for mg in range(6)]
        wring2 = [wslice[:, 0:4096], wslice[:, 4096:8192]]
        for j in range(len(steps) + 2):
            with block() as b:
                @b.gpsimd
                def _(g, j=j):
                    if j < len(steps):
                        srcT, _, mg = steps[j]
                        gq.go(g, wring2[j % 2].rearrange("p (k c) -> p k c", k=8),
                              srcT.ap().rearrange("(k p) c -> p k c", p=128)
                              [:, :, mg * 512:(mg + 1) * 512])
                    gq.wait(g)
                @b.tensor
                def _(tt, j=j):
                    i = j - 1
                    if 0 <= i < len(steps):
                        ps_ = pq if i % 2 == 0 else pg
                        wr = wring2[i % 2]
                        for mi in range(4):
                            for k in range(8):
                                tt.matmul(ps_[:, mi * 256:(mi + 1) * 256],
                                          wr[:, k * 512 + mi * 128:k * 512 + (mi + 1) * 128],
                                          ytfm[:, k * 256:(k + 1) * 256],
                                          start=(k == 0), stop=(k == 7))
                @b.vector
                def _(v, j=j):
                    i = j - 2
                    if 0 <= i < len(steps):
                        _, dst_t, mg = steps[i]
                        ps_ = pq if i % 2 == 0 else pg
                        v.tensor_copy(out=dst_t[:, mg * 1024:(mg + 1) * 1024],
                                      in_=ps_[:, 0:1024])
        if "dbg_upr" in dbg and ti == 0:
            with block() as b:
                @b.gpsimd
                def _(g):
                    gq.go(g, dbg["dbg_upr"][:, 0:6144], xbcraw[:, 0:6144])
                    gq.wait(g)
        with block() as b:
            @b.vector
            def _(v):
                v.tensor_reduce(out=difsb[:, 24:25], in_=xbcraw[:, 0:6144], axis=AX,
                                op=ALU.max, apply_absolute_value=True)
                v.tensor_reduce(out=difsb[:, 25:26], in_=siluz[:, 0:6144], axis=AX,
                                op=ALU.max, apply_absolute_value=True)
        with block() as b:
            @b.vector
            def _(v):
                v.tensor_tensor(out=ftmp2[:, 0:2], in0=ftmp2[:, 0:2],
                                in1=difsb[:, 24:26], op=ALU.max)
        with block() as b:
            @b.gpsimd
            def _(g, tok0=tok0):
                gq.go(g, upraw.ap().rearrange("p (m q) -> p m q", m=24)
                      [:, :, tok0:tok0 + 256],
                      xbcraw[:, 0:6144].rearrange("p (m q) -> p m q", m=24))
                gq.go(g, gateraw.ap().rearrange("p (m q) -> p m q", m=24)
                      [:, :, tok0:tok0 + 256],
                      siluz[:, 0:6144].rearrange("p (m q) -> p m q", m=24))
                gq.wait(g)
    with block() as b:
        @b.tensor
        def _(tt):
            tt.transpose(pg[0:2, 0:128], ftmp2[:, 0:2], idf[:, :])
    with block() as b:
        @b.vector
        def _(v):
            v.tensor_copy(out=dtraw[0:2, 0:128], in_=pg[0:2, 0:128])
    with block() as b:
        @b.vector
        def _(v):
            v.tensor_reduce(out=dtraw[0:2, 200:201], in_=dtraw[0:2, 0:128], axis=AX,
                            op=ALU.max)
    with block() as b:
        @b.gpsimd
        def _(g):
            g.memset(dtraw[0:1, 128:144], 0.0)
    with block() as b:
        @b.gpsimd
        def _(g):
            gq.go(g, armx[0:1, 0:16], dtraw[0:1, 128:144])
            gq.wait(g)
            gq.go(g, armx[0:1, 0:1], dtraw[0:1, 200:201])
            gq.go(g, armx[0:1, 1:2], dtraw[1:2, 200:201])
            gq.wait(g)
            g.collective_compute(
                "AllReduce", ALU.max, replica_groups=[list(range(8))],
                ins=[armx.ap().opt()], outs=[armxo.ap().opt()],
            ).then_inc(ccs)
            cc_n[0] += 1
            g.wait_ge(ccs, cc_n[0])
            gq.go(g, lendr[0:1, 8:10], armxo[0:1, 0:2])
            gq.wait(g)
    make_scale(lendr[0:1, 8:9], 4)
    make_scale(lendr[0:1, 9:10], 6)
    with block() as b:
        @b.tensor
        def _(tt):
            tt.matmul(pq[:, 4:8], onesf[0:1, :], lendE[0:1, 4:8], start=True, stop=True)
    with block() as b:
        @b.vector
        def _(v):
            v.tensor_copy(out=ftmp[:, 4:8], in_=pq[:, 4:8])
    with block() as b:
        @b.gpsimd
        def _(g):
            g.memset(ftmp2[:, 2:4], 0.0)
    for ti in range(8):
        tok0 = ti * 256
        with block() as b:
            @b.gpsimd
            def _(g, tok0=tok0):
                gq.go(g, siluz[:, 0:6144].rearrange("p (m q) -> p m q", m=24),
                      gateraw.ap().rearrange("p (m q) -> p m q", m=24)
                      [:, :, tok0:tok0 + 256])
                gq.wait(g)
        for ch in range(3):
            co = ch * 2048
            with block() as b:
                @b.vector
                def _(v, co=co):
                    v.tensor_scalar(out=difsb[:, 0:2048], in0=siluz[:, co:co + 2048],
                                    scalar1=ftmp[:, 7:8], scalar2=None, op0=ALU.mult)
                    v.tensor_copy(out=itile[:, 0:2048], in_=difsb[:, 0:2048])
                    v.tensor_copy(out=difsb[:, 0:2048], in_=itile[:, 0:2048])
                    v.tensor_scalar(out=siluz[:, co:co + 2048], in0=difsb[:, 0:2048],
                                    scalar1=ftmp[:, 6:7], scalar2=None, op0=ALU.mult)
            with block() as b:
                @b.scalar
                def _(sx, co=co):
                    sx.activation(out=difsb[:, 0:2048], in_=siluz[:, co:co + 2048],
                                  func=AF.Sigmoid)
            with block() as b:
                @b.vector
                def _(v, co=co):
                    v.tensor_scalar(out=ytok[:, 0:2048], in0=difsb[:, 0:2048],
                                    scalar1=65536.0, scalar2=None, op0=ALU.mult)
                    v.tensor_copy(out=itile[:, 0:2048], in_=ytok[:, 0:2048])
                    v.tensor_copy(out=ytok[:, 0:2048], in_=itile[:, 0:2048])
                    v.tensor_scalar(out=ytok[:, 0:2048], in0=ytok[:, 0:2048],
                                    scalar1=65535.0, scalar2=1.0 / 65536.0,
                                    op0=ALU.min, op1=ALU.mult)
                    v.tensor_tensor(out=siluz[:, co:co + 2048], in0=siluz[:, co:co + 2048],
                                    in1=ytok[:, 0:2048], op=ALU.mult)
        with block() as b:
            @b.vector
            def _(v):
                v.tensor_reduce(out=difsb[:, 24:25], in_=siluz[:, 0:6144], axis=AX,
                                op=ALU.max, apply_absolute_value=True)
        with block() as b:
            @b.vector
            def _(v):
                v.tensor_tensor(out=ftmp2[:, 2:3], in0=ftmp2[:, 2:3],
                                in1=difsb[:, 24:25], op=ALU.max)
        with block() as b:
            @b.gpsimd
            def _(g, tok0=tok0):
                gq.go(g, gate2d.ap().rearrange("p (m q) -> p m q", m=24)
                      [:, :, tok0:tok0 + 256],
                      siluz[:, 0:6144].rearrange("p (m q) -> p m q", m=24))
                gq.wait(g)
    with block() as b:
        @b.tensor
        def _(tt):
            tt.transpose(pg[0:1, 0:128], ftmp2[:, 2:3], idf[:, :])
    with block() as b:
        @b.vector
        def _(v):
            v.tensor_copy(out=dtraw[0:1, 0:128], in_=pg[0:1, 0:128])
    with block() as b:
        @b.vector
        def _(v):
            v.tensor_reduce(out=lendr[0:1, 4:5], in_=dtraw[0:1, 0:128], axis=AX, op=ALU.max)
    with block() as b:
        @b.gpsimd
        def _(g):
            g.memset(dtraw[0:1, 128:144], 0.0)
    with block() as b:
        @b.vector
        def _(v):
            v.tensor_copy(out=dtraw[0:1, 128:129], in_=lendr[0:1, 4:5])
    with block() as b:
        @b.gpsimd
        def _(g):
            gq.go(g, armx[0:1, 0:16], dtraw[0:1, 128:144])
            gq.wait(g)
            g.collective_compute(
                "AllReduce", ALU.max, replica_groups=[list(range(8))],
                ins=[armx.ap().opt()], outs=[armxo.ap().opt()],
            ).then_inc(ccs)
            cc_n[0] += 1
            g.wait_ge(ccs, cc_n[0])
            gq.go(g, lendr[0:1, 10:11], armxo[0:1, 0:1])
            gq.wait(g)
    make_scale(lendr[0:1, 10:11], 8)
    with block() as b:
        @b.tensor
        def _(tt):
            tt.matmul(pq[:, 8:10], onesf[0:1, :], lendE[0:1, 8:10], start=True, stop=True)
    with block() as b:
        @b.vector
        def _(v):
            v.tensor_copy(out=ftmp[:, 8:10], in_=pq[:, 8:10])
    with block() as b:
        @b.gpsimd
        def _(g):
            g.memset(ftmp2[:, 3:4], 0.0)
    for ti in range(8):
        tok0 = ti * 256
        with block() as b:
            @b.gpsimd
            def _(g, tok0=tok0):
                gq.go(g, siluz[:, 0:6144].rearrange("p (m q) -> p m q", m=24),
                      gate2d.ap().rearrange("p (m q) -> p m q", m=24)
                      [:, :, tok0:tok0 + 256])
                gq.go(g, xbcraw[:, 0:6144].rearrange("p (m q) -> p m q", m=24),
                      upraw.ap().rearrange("p (m q) -> p m q", m=24)
                      [:, :, tok0:tok0 + 256])
                gq.wait(g)
        for ch in range(3):
            co = ch * 2048
            with block() as b:
                @b.vector
                def _(v, co=co):
                    v.tensor_scalar(out=difsb[:, 0:2048], in0=siluz[:, co:co + 2048],
                                    scalar1=ftmp[:, 9:10], scalar2=None, op0=ALU.mult)
                    v.tensor_copy(out=itile[:, 0:2048], in_=difsb[:, 0:2048])
                    v.tensor_copy(out=difsb[:, 0:2048], in_=itile[:, 0:2048])
                    v.tensor_scalar(out=siluz[:, co:co + 2048], in0=difsb[:, 0:2048],
                                    scalar1=ftmp[:, 8:9], scalar2=None, op0=ALU.mult)
                    v.tensor_scalar(out=difsb[:, 0:2048], in0=xbcraw[:, co:co + 2048],
                                    scalar1=ftmp[:, 5:6], scalar2=None, op0=ALU.mult)
                    v.tensor_copy(out=itile[:, 0:2048], in_=difsb[:, 0:2048])
                    v.tensor_copy(out=difsb[:, 0:2048], in_=itile[:, 0:2048])
                    v.tensor_scalar(out=xbcraw[:, co:co + 2048], in0=difsb[:, 0:2048],
                                    scalar1=ftmp[:, 4:5], scalar2=None, op0=ALU.mult)
                    v.tensor_tensor(out=siluz[:, co:co + 2048], in0=siluz[:, co:co + 2048],
                                    in1=xbcraw[:, co:co + 2048], op=ALU.mult)
        with block() as b:
            @b.vector
            def _(v):
                v.tensor_reduce(out=difsb[:, 24:25], in_=siluz[:, 0:6144], axis=AX,
                                op=ALU.max, apply_absolute_value=True)
        with block() as b:
            @b.vector
            def _(v):
                v.tensor_tensor(out=ftmp2[:, 3:4], in0=ftmp2[:, 3:4],
                                in1=difsb[:, 24:25], op=ALU.max)
        with block() as b:
            @b.gpsimd
            def _(g, tok0=tok0):
                gq.go(g, od.ap().rearrange("p (m q) -> p m q", m=24)
                      [:, :, tok0:tok0 + 256],
                      siluz[:, 0:6144].rearrange("p (m q) -> p m q", m=24))
                gq.wait(g)
    with block() as b:
        @b.tensor
        def _(tt):
            tt.transpose(pg[0:1, 0:128], ftmp2[:, 3:4], idf[:, :])
    with block() as b:
        @b.vector
        def _(v):
            v.tensor_copy(out=dtraw[0:1, 0:128], in_=pg[0:1, 0:128])
    with block() as b:
        @b.vector
        def _(v):
            v.tensor_reduce(out=lendr[0:1, 5:6], in_=dtraw[0:1, 0:128], axis=AX, op=ALU.max)
    with block() as b:
        @b.gpsimd
        def _(g):
            g.memset(dtraw[0:1, 128:144], 0.0)
    with block() as b:
        @b.vector
        def _(v):
            v.tensor_copy(out=dtraw[0:1, 128:129], in_=lendr[0:1, 5:6])
    with block() as b:
        @b.gpsimd
        def _(g):
            gq.go(g, armx[0:1, 0:16], dtraw[0:1, 128:144])
            gq.wait(g)
            g.collective_compute(
                "AllReduce", ALU.max, replica_groups=[list(range(8))],
                ins=[armx.ap().opt()], outs=[armxo.ap().opt()],
            ).then_inc(ccs)
            cc_n[0] += 1
            g.wait_ge(ccs, cc_n[0])
            gq.go(g, lendr[0:1, 11:12], armxo[0:1, 0:1])
            gq.wait(g)
    make_scale(lendr[0:1, 11:12], 10)
    with block() as b:
        @b.tensor
        def _(tt):
            tt.matmul(pq[:, 10:12], onesf[0:1, :], lendE[0:1, 10:12], start=True, stop=True)
    with block() as b:
        @b.vector
        def _(v):
            v.tensor_copy(out=ftmp[:, 10:12], in_=pq[:, 10:12])
    if False:
        pass
    for ti in range(8):
        tok0 = ti * 256
        with block() as b:
            @b.gpsimd
            def _(g, tok0=tok0):
                gq.go(g, siluz[:, 0:6144].rearrange("p (m q) -> p m q", m=24),
                      od.ap().rearrange("p (m q) -> p m q", m=24)[:, :, tok0:tok0 + 256])
                gq.wait(g)
        if "dbg_xbc" in dbg and ti == 0:
            with block() as b:
                @b.gpsimd
                def _(g):
                    gq.go(g, dbg["dbg_xbc"][:, 0:6144], siluz[:, 0:6144])
                    gq.wait(g)
        for ch in range(3):
            co = ch * 2048
            with block() as b:
                @b.vector
                def _(v, co=co):
                    v.tensor_scalar(out=difsb[:, 0:2048], in0=siluz[:, co:co + 2048],
                                    scalar1=ftmp[:, 11:12], scalar2=None, op0=ALU.mult)
                    v.tensor_copy(out=itile[:, 0:2048], in_=difsb[:, 0:2048])
                    v.tensor_copy(out=difsb[:, 0:2048], in_=itile[:, 0:2048])
                    v.tensor_scalar(out=siluz[:, co:co + 2048], in0=difsb[:, 0:2048],
                                    scalar1=ftmp[:, 10:11], scalar2=None, op0=ALU.mult)
        for kg in range(3):
            with block() as b:
                @b.gpsimd
                def _(g, kg=kg):
                    gq.go(g, xbcraw[:, 0:8192].rearrange("p (k f) -> p k f", k=8),
                          wdqT.ap().rearrange("(k p) f -> p k f", p=128)
                          [:, kg * 8:(kg + 1) * 8, :])
                    gq.wait(g)
            with block() as b:
                @b.tensor
                def _(tt, kg=kg):
                    for m in range(8):
                        dst = pq[:, (m % 4) * 256:(m % 4 + 1) * 256] if m < 4 \
                            else pg[:, (m % 4) * 256:(m % 4 + 1) * 256]
                        for k in range(8):
                            tt.matmul(dst,
                                      xbcraw[:, k * 1024 + m * 128:k * 1024 + (m + 1) * 128],
                                      siluz[:, (kg * 8 + k) * 256:(kg * 8 + k + 1) * 256],
                                      start=(kg == 0 and k == 0 and m % 2 == 0),
                                      stop=(kg == 2 and k == 7))
        with block() as b:
            @b.scalar
            def _(sx):
                sx.activation(out=hstb[:, 0:1024], in_=pq[:, 0:1024], func=AF.Silu)
                sx.activation(out=hstb[:, 1024:2048], in_=pg[:, 0:1024], func=AF.Silu)
        if "dbg_ffn" in dbg:
            with block() as b:
                @b.vector
                def _(v):
                    v.tensor_copy(out=ytok[:, 0:1024], in_=pq[:, 0:1024])
                    v.tensor_copy(out=ytok[:, 1024:2048], in_=pg[:, 0:1024])
            with block() as b:
                @b.gpsimd
                def _(g, tok0=tok0):
                    gq.go(g, dbg["dbg_ffn"][:, :].rearrange("p (k q) -> p k q", k=8)
                          [:, :, tok0:tok0 + 256],
                          ytok[:, 0:2048].rearrange("p (k q) -> p k q", k=8))
                    gq.wait(g)
        with block() as b:
            @b.gpsimd
            def _(g, tok0=tok0):
                gq.go(g, difsb[:, 0:2048].rearrange("p (k q) -> p k q", k=8),
                      x1d.ap().rearrange("p (k q) -> p k q", k=8)[:, :, tok0:tok0 + 256])
                gq.wait(g)
        with block() as b:
            @b.vector
            def _(v):
                for k in range(8):
                    v.tensor_scalar(out=ygat[:, k * 256:(k + 1) * 256],
                                    in0=hstb[:, k * 256:(k + 1) * 256],
                                    scalar1=ecolT[:, k:k + 1], scalar2=None, op0=ALU.mult)
        with block() as b:
            @b.vector
            def _(v):
                v.tensor_tensor(out=difsb[:, 0:2048], in0=difsb[:, 0:2048],
                                in1=ygat[:, 0:2048], op=ALU.add)
        with block() as b:
            @b.gpsimd
            def _(g, tok0=tok0):
                gq.go(g, out.ap().rearrange("(k p) q -> p k q", p=128)
                      [:, :, tok0:tok0 + 256],
                      difsb[:, 0:2048].rearrange("p (k q) -> p k q", k=8))
                gq.wait(g)
    if "dbg_mrev" in dbg:
        with block() as b:
            @b.gpsimd
            def _(g):
                for rr in range(4):
                    gq.go(g, ytfm[:, 0:1024],
                          wdqT.ap()[rr * 128:(rr + 1) * 128, 0:1024])
                    gq.wait(g)
                    gq.go(g, dbg["dbg_mrev"][:, rr * 1024:(rr + 1) * 1024], ytfm[:, 0:1024])
                    gq.wait(g)
    with block() as b:
        @b.gpsimd
        def _(g):
            gq.wait(g)
        @b.sync
        def _(sx):
            sq.wait(sx)

    stack.close()
    return nc


def _stage(inputs):
    ident = np.eye(128, dtype=np.float32)
    jrev = ident[:, ::-1].copy()
    ii, jj = np.meshgrid(np.arange(128), np.arange(128), indexing="ij")
    maskneg = np.where(jj >= ii, 0.0, -1e30).astype(np.float32)
    tiew = np.tile(np.arange(16, 0, -1, dtype=np.float32), (128, 1)).copy()
    tiew4 = np.tile(np.arange(4, 0, -1, dtype=np.float32), (128, 1)).copy()

    x = inputs["L0_attn_hidden_states"]
    per_core = []
    for c in range(8):
        b, dd = c // 2, c % 2
        pre = "L0_attn_fwd_" if dd == 0 else "L0_attn_bwd_"
        xb = x[b] if dd == 0 else x[b][::-1]
        m = {
            "x": np.ascontiguousarray(xb.T),
            "win": inputs[pre + "W_in"],
            "wout": inputs[pre + "W_out"],
            "convw": np.ascontiguousarray(inputs[pre + "conv_w"].T),
            "convb": np.ascontiguousarray(inputs[pre + "conv_b"].reshape(18, 128).T),
            "dtbias": inputs[pre + "dt_bias"].reshape(32, 1),
            "alog": inputs[pre + "A_log"].reshape(32, 1),
            "dhead": np.ascontiguousarray(np.tile(inputs[pre + "D"], (128, 1))),
            "normw": np.ascontiguousarray(inputs[pre + "norm_w"].reshape(16, 128).T),
            "ident": ident,
            "maskneg": maskneg,
            "jrev": jrev,
            "rwT": np.ascontiguousarray(inputs["L0_ffn_router_weight"].T),
            "ebias": np.ascontiguousarray(np.tile(inputs["L0_ffn_e_bias"], (128, 1))),
            "tiew": tiew,
            "tiew4": tiew4,
            "gwT": np.ascontiguousarray(inputs["L0_ffn_gate_w"].transpose(0, 2, 1)),
            "uwT": np.ascontiguousarray(inputs["L0_ffn_up_w"].transpose(0, 2, 1)),
            "dwT": np.ascontiguousarray(inputs["L0_ffn_down_w"].transpose(0, 2, 1)),
            "shgT": np.ascontiguousarray(inputs["L0_ffn_sh_gate_w"].T),
            "shuT": np.ascontiguousarray(inputs["L0_ffn_sh_up_w"].T),
            "shdT": np.ascontiguousarray(inputs["L0_ffn_sh_down_w"].T),
            "wvT": np.ascontiguousarray(inputs["L1_attn_w_qkv"][2048:3072].T),
            "woT": np.ascontiguousarray(inputs["L1_attn_w_o"].T),
            "wgate": inputs["L1_ffn_w_gate"],
            "wup": inputs["L1_ffn_w_up"],
            "wdown": inputs["L1_ffn_w_down"],
        }
        per_core.append({k: np.ascontiguousarray(v, dtype=np.float32)
                         for k, v in m.items()})
    return per_core


def kernel(**inputs):
    per_core = _stage(inputs)
    nc = build()
    res = run_bass_kernel_spmd(nc, per_core, core_ids=list(range(8)))
    out = np.empty((4, L, D_MODEL), dtype=np.float32)
    for c in range(8):
        b, dd = c // 2, c % 2
        o = np.asarray(res.results[c]["out"])
        if dd == 0:
            out[b, 0:TLOC] = o.T
        else:
            out[b, TLOC:] = o.T[::-1]
    return out


# version 05 (final)
# speedup vs baseline: 1.3135x; 1.0441x over previous
import contextlib
import numpy as np

import concourse.bass as bass
import concourse.mybir as mybir
from concourse.bass_utils import run_bass_kernel_spmd

F32 = mybir.dt.float32
BF16 = mybir.dt.bfloat16
I32 = mybir.dt.int32
AF = mybir.ActivationFunctionType
ALU = mybir.AluOpType
AX = mybir.AxisListType.X

D_MODEL = 1024
D_INNER = 2048
D_IN_PROJ = 4384
NM_IN = 35
L = 4096
TLOC = 2048
TILE = 512
NT = L // TILE
EPS = 1e-5
CR = TILE + 3

DBG_NAMES = [
    ("dbg_xn", [128, 8 * TILE]), ("dbg_xbc", [128, 18 * TILE]),
    ("dbg_dt", [32, TILE]), ("dbg_y", [128, 2048]),
    ("dbg_mrev", [128, 8 * L]), ("dbg_mnat", [128, 8 * TLOC]),
    ("dbg_s", [128, 8 * TLOC]), ("dbg_x1", [128, 8 * TLOC]),
    ("dbg_xn2", [128, 8 * TLOC]), ("dbg_g", [128, 8 * TLOC]),
    ("dbg_scores", [128, 64]), ("dbg_comb", [128, 64]),
    ("dbg_s1", [128, 8]), ("dbg_ffn", [128, 8 * TLOC]),
    ("dbg_upr", [128, 24 * 256]),
]

MGROUPS = [list(range(g, min(g + 4, NM_IN))) for g in range(0, NM_IN, 4)]
WHALF = 8 * 4 * 128


def build(debug=()):
    nc = bass.Bass("TRN2", target_bir_lowering=False, debug=False)
    stack = contextlib.ExitStack()

    d = {}
    def din(name, shape):
        d[name] = nc.dram_tensor(name, list(shape), F32, kind="ExternalInput")

    din("x", [D_MODEL, L])
    din("win", [D_MODEL, D_IN_PROJ])
    din("wout", [D_INNER, D_MODEL])
    din("convw", [2304, 4])
    din("convb", [128, 18])
    din("dtbias", [32, 1])
    din("alog", [32, 1])
    din("dhead", [128, 32])
    din("normw", [128, 16])
    din("ident", [128, 128])
    din("maskneg", [128, 128])
    din("jrev", [128, 128])
    din("rwT", [D_MODEL, 16])
    din("ebias", [128, 16])
    din("tiew", [128, 16])
    din("tiew4", [128, 4])
    din("gwT", [16, D_MODEL, 512])
    din("uwT", [16, D_MODEL, 512])
    din("dwT", [16, 512, D_MODEL])
    din("shgT", [D_MODEL, D_MODEL])
    din("shuT", [D_MODEL, D_MODEL])
    din("shdT", [D_MODEL, D_MODEL])
    din("wvT", [D_MODEL, D_MODEL])
    din("woT", [D_MODEL, D_MODEL])
    din("wgate", [3072, D_MODEL])
    din("wup", [3072, D_MODEL])
    din("wdown", [D_MODEL, 3072])

    out = nc.dram_tensor("out", [D_MODEL, TLOC], F32, kind="ExternalOutput")
    dbg = {}
    for name, shape in DBG_NAMES:
        if name in debug:
            dbg[name] = nc.dram_tensor(name, shape, F32, kind="ExternalOutput")

    ellbuf = nc.dram_tensor("ellbuf", [4 * 32, 128], F32)
    arin = nc.dram_tensor("arin", [128, 8 * L], BF16)
    arout = nc.dram_tensor("arout", [128, 8 * L], BF16)
    mnat = nc.dram_tensor("mnat", [128, 8 * TLOC], BF16)
    gwb = nc.dram_tensor("gwb", [16, D_MODEL, 512], BF16)
    uwb = nc.dram_tensor("uwb", [16, D_MODEL, 512], BF16)
    dwb = nc.dram_tensor("dwb", [16, 512, D_MODEL], BF16)
    shb = nc.dram_tensor("shb", [3, D_MODEL, D_MODEL], BF16)
    wvb = nc.dram_tensor("wvb", [D_MODEL, D_MODEL], BF16)
    wob = nc.dram_tensor("wob", [D_MODEL, D_MODEL], BF16)
    combuf = nc.dram_tensor("combuf", [32, 128], F32)
    x1d = nc.dram_tensor("x1d", [128, 8 * TLOC], F32)
    xn2d = nc.dram_tensor("xn2d", [128, 8 * TLOC], BF16)
    armx = nc.dram_tensor("armx", [1, 16], F32)
    armxo = nc.dram_tensor("armxo", [1, 16], F32)
    vsin = nc.dram_tensor("vsin", [128, 8], F32)
    vsout = nc.dram_tensor("vsout", [128, 8], F32)
    wuqT = nc.dram_tensor("wuqT", [1024, 3072], BF16)
    wgqT = nc.dram_tensor("wgqT", [1024, 3072], BF16)
    wdqT = nc.dram_tensor("wdqT", [3072, 1024], BF16)
    upraw = nc.dram_tensor("upraw", [128, 24 * TLOC], BF16)
    gateraw = nc.dram_tensor("gateraw", [128, 24 * TLOC], BF16)
    gate2d = nc.dram_tensor("gate2d", [128, 24 * TLOC], BF16)
    od = nc.dram_tensor("od", [128, 24 * TLOC], BF16)

    sbuf = lambda name, shape, dt=F32: stack.enter_context(nc.sbuf_tensor(name, shape, dt))
    psum = lambda name, shape, dt=F32: stack.enter_context(nc.psum_tensor(name, shape, dt))

    wslice = sbuf("wslice", [128, 2 * WHALF], BF16)
    woutb = sbuf("woutb", [128, 16 * D_MODEL], BF16)
    xnb = sbuf("xnb", [128, 8 * TILE], BF16)
    siluz = sbuf("siluz", [128, 16 * TILE], BF16)
    xbcraw = sbuf("xbcraw", [128, 18 * CR], BF16)
    tailb = sbuf("tailb", [128, 18 * 3], BF16)
    dtraw = sbuf("dtraw", [32, TILE], F32)
    dtp = sbuf("dtp", [32, TILE], F32)
    dta = sbuf("dta", [32, TILE], F32)
    lndt = sbuf("lndt", [32, TILE], F32)
    ones32 = sbuf("ones32", [32, 128], F32)
    onec32 = sbuf("onec32", [32, 1], F32)
    ellf = sbuf("ellf", [32, 128], F32)
    lmef = sbuf("lmef", [32, 128], F32)
    hl1 = sbuf("hl1", [32, 128], BF16)
    hl2 = sbuf("hl2", [32, 128], BF16)
    ellT = sbuf("ellT", [128, 32], F32)
    dtT = sbuf("dtT", [128, 32], F32)
    ecolT = sbuf("ecolT", [128, 32], F32)
    lendr = sbuf("lendr", [1, 32], F32)
    lendE = sbuf("lendE", [1, 32], F32)
    ffac = sbuf("ffac", [128, 32], BF16)
    ftmp = sbuf("ftmp", [128, 32], F32)
    ftmp2 = sbuf("ftmp2", [128, 32], F32)
    edec = sbuf("edec", [128, 32], F32)
    lstk = sbuf("lstk", [4, 4096], BF16)
    rstk = sbuf("rstk", [4, 4096], BF16)
    gtile = sbuf("gtile", [128, 128], F32)
    masks = sbuf("masks", [128, 128], F32)
    jrevs = sbuf("jrevs", [128, 128], BF16)
    idf = sbuf("idf", [128, 128], F32)
    idb = sbuf("idb", [128, 128], BF16)
    onesb = sbuf("onesb", [128, 1], BF16)
    onesrb = sbuf("onesrb", [1, 128], BF16)
    onesf = sbuf("onesf", [1, 128], F32)
    epsc = sbuf("epsc", [128, 1], F32)
    dstage = sbuf("dstage", [128, 32], BF16)
    dtbias_c = sbuf("dtbias_c", [32, 1], F32)
    alog_c = sbuf("alog_c", [32, 1], F32)
    acol = sbuf("acol", [32, 1], F32)
    convw_c = sbuf("convw_c", [128, 18 * 4], F32)
    convb_c = sbuf("convb_c", [128, 18], F32)
    normw_c = sbuf("normw_c", [128, 16], F32)
    xtok = sbuf("xtok", [128, 2048], BF16)
    btok = sbuf("btok", [128, 128], BF16)
    difsb = sbuf("difsb", [128, 2048], F32)
    expsb = sbuf("expsb", [128, 2048], BF16)
    lhsT_all = sbuf("lhsT_all", [128, 4096], BF16)
    ytok = sbuf("ytok", [128, 2048], F32)
    ytmp = sbuf("ytmp", [128, 2048], F32)
    ytmp2 = sbuf("ytmp2", [128, 2048], BF16)
    ygat = sbuf("ygat", [128, 2048], BF16)
    szt = sbuf("szt", [128, 2048], BF16)
    ynrm = sbuf("ynrm", [128, 2048], BF16)
    yrev = sbuf("yrev", [128, 2048], BF16)
    ytfm = sbuf("ytfm", [128, 2048], BF16)
    mstage = sbuf("mstage", [128, 8 * 128], BF16)
    rmscol = sbuf("rmscol", [128, 1], F32)
    rstdc = sbuf("rstdc", [128, 1], F32)
    rstd2c = sbuf("rstd2c", [128, 1], F32)
    hstf = sbuf("hstf", [128, 2048], F32)
    hstb = sbuf("hstb", [128, 2048], BF16)
    xts = sbuf("xts", [128, 2048], BF16)
    mrow = dtraw[0:1, 0:TILE]
    e2row = dtp[0:1, 0:TILE]
    vrow = dta[0:1, 0:TILE]
    vrow2 = lndt[0:1, 0:TILE]
    rstdr = lndt[0:1, 0:TILE]

    pf = psum("pf", [128, 2048], F32)
    pq = psum("pq", [128, 1024], F32)
    pg = psum("pg", [128, 1024], F32)
    pgb = pg[:, :].bitcast(BF16)

    class DQ:
        def __init__(self, name):
            self.h = stack.enter_context(nc.semaphore(name))
            self.n = 0
        def go(self, eng, out_, in_, **kw):
            eng.dma_start(out=out_, in_=in_, **kw).then_inc(self.h, 16)
            self.n += 16
        def wait(self, eng, n=None):
            eng.wait_ge(self.h, self.n if n is None else n)

    gq = DQ("gq")
    sq = DQ("sq")
    ccs = stack.enter_context(nc.semaphore("ccs"))
    cc_n = [0]

    def block():
        return nc.Block()

    def xpart(k):
        buf = ytok if k < 4 else ytmp
        return buf[:, (k % 4) * TILE:(k % 4 + 1) * TILE]

    xbf_v = siluz[:, 0:8 * TILE]
    xsq_v = siluz[:, 8 * TILE:16 * TILE]

    def grp_cols(ml):
        return 128 * len(ml) - (96 if 34 in ml else 0)

    def win_load(eng, grp, half):
        ml = MGROUPS[grp]
        gw = grp_cols(ml)
        gq.go(eng,
              wslice[:, half * WHALF:half * WHALF + 8 * gw]
              .rearrange("p (k m) -> p k m", k=8),
              d["win"].ap().rearrange("(k p) m -> p k m", p=128)
              [:, :, ml[0] * 128:ml[0] * 128 + gw])

    with block() as b:
        @b.gpsimd
        def _(g):
            gq.go(g, masks[:, :], d["maskneg"][:, :])
            gq.go(g, jrevs[:, :], d["jrev"][:, :])
            gq.go(g, idf[:, :], d["ident"][:, :])
            gq.go(g, idb[:, :], d["ident"][:, :])
            gq.go(g, dtbias_c[:, :], d["dtbias"][:, :])
            gq.go(g, alog_c[:, :], d["alog"][:, :])
            gq.go(g, dstage[:, :], d["dhead"][:, :])
            gq.go(g, convw_c[:, :].rearrange("p (c k) -> p c k", c=18),
                  d["convw"].ap().rearrange("(c p) k -> p c k", p=128))
            gq.go(g, convb_c[:, :], d["convb"][:, :])
            gq.go(g, normw_c[:, :], d["normw"][:, :])
            g.memset(onesb[:, :], 1.0)
            g.memset(onesrb[:, :], 1.0)
            g.memset(onesf[:, :], 1.0)
            g.memset(ones32[:, :], 1.0)
            g.memset(onec32[:, :], 1.0)
            g.memset(epsc[:, :], EPS)
            g.memset(lstk[:, :], 1.0)
            g.memset(rstk[:, :], 1.0)
            g.memset(hstf[:, :], 0.0)
            g.memset(hstb[:, :], 0.0)
            g.memset(tailb[:, :], 0.0)
            gq.go(g, woutb[:, :].rearrange("p (k m) -> p k m", k=16),
                  d["wout"].ap().rearrange("(k p) m -> p k m", p=128))
            win_load(g, 0, 0)
        @b.vector
        def _(v):
            gq.wait(v)
    with block() as b:
        @b.scalar
        def _(s):
            s.activation(out=acol[:, :], in_=alog_c[:, :], func=AF.Exp)
        @b.vector
        def _(v):
            for k in range(16):
                v.tensor_scalar(out=woutb[:, k * D_MODEL:(k + 1) * D_MODEL],
                                in0=woutb[:, k * D_MODEL:(k + 1) * D_MODEL],
                                scalar1=normw_c[:, k:k + 1], scalar2=None, op0=ALU.mult)
    with block() as b:
        @b.vector
        def _(v):
            v.tensor_scalar(out=acol[:, :], in0=acol[:, :], scalar1=-1.0,
                            scalar2=None, op0=ALU.mult)

    CASTC = 256 * 1024
    cast_chunks = []

    def add_cast_job(src, dst, n):
        for i in range(0, n, CASTC):
            cast_chunks.append((dst[:, i:i + CASTC], src[:, i:i + CASTC]))

    for e in range(16):
        add_cast_job(d["gwT"].ap().rearrange("e a b -> e (a b)")[e:e + 1, :],
                     gwb.ap().rearrange("e a b -> e (a b)")[e:e + 1, :], 512 * 1024)
        add_cast_job(d["uwT"].ap().rearrange("e a b -> e (a b)")[e:e + 1, :],
                     uwb.ap().rearrange("e a b -> e (a b)")[e:e + 1, :], 512 * 1024)
        add_cast_job(d["dwT"].ap().rearrange("e a b -> e (a b)")[e:e + 1, :],
                     dwb.ap().rearrange("e a b -> e (a b)")[e:e + 1, :], 512 * 1024)
    for i, nm in enumerate(["shgT", "shuT", "shdT"]):
        add_cast_job(d[nm].ap().rearrange("a b -> () (a b)"),
                     shb.ap().rearrange("e a b -> e (a b)")[i:i + 1, :], 1024 * 1024)
    add_cast_job(d["wvT"].ap().rearrange("a b -> () (a b)"),
                 wvb.ap().rearrange("a b -> () (a b)"), 1024 * 1024)
    add_cast_job(d["woT"].ap().rearrange("a b -> () (a b)"),
                 wob.ap().rearrange("a b -> () (a b)"), 1024 * 1024)

    def cast_feed(g, k=1):
        for _ in range(k):
            if cast_chunks:
                dst, src = cast_chunks.pop(0)
                gq.go(g, dst, src)

    pending_stores = []

    def store_flush(g):
        for dst, src in pending_stores:
            gq.go(g, dst, src)
        pending_stores.clear()

    for t in range(NT):
        tok0 = t * TILE
        with block() as b:
            @b.gpsimd
            def _(g, tok0=tok0):
                for k in range(8):
                    gq.go(g, xpart(k),
                          d["x"].ap().rearrange("(k p) q -> p k q", p=128)
                          [:, k, tok0:tok0 + TILE])
                gq.wait(g)
        with block() as b:
            @b.vector
            def _(v):
                for k in range(8):
                    v.tensor_copy(out=xbf_v[:, k * TILE:(k + 1) * TILE], in_=xpart(k))
            @b.scalar
            def _(s):
                for k in range(8):
                    s.activation(out=xsq_v[:, k * TILE:(k + 1) * TILE], in_=xpart(k),
                                 func=AF.Square)
        with block() as b:
            @b.tensor
            def _(tt):
                for k in range(8):
                    tt.matmul(pg[0:1, 0:TILE], onesb[:, 0:1],
                              xbf_v[:, k * TILE:(k + 1) * TILE],
                              start=(k == 0), stop=(k == 7))
                for k in range(8):
                    tt.matmul(pg[0:1, TILE:2 * TILE], onesb[:, 0:1],
                              xsq_v[:, k * TILE:(k + 1) * TILE],
                              start=(k == 0), stop=(k == 7))
        with block() as b:
            @b.vector
            def _(v):
                v.tensor_scalar(out=mrow, in0=pg[0:1, 0:TILE], scalar1=1.0 / 1024,
                                scalar2=None, op0=ALU.mult)
                v.tensor_scalar(out=e2row, in0=pg[0:1, TILE:2 * TILE],
                                scalar1=1.0 / 1024, scalar2=None, op0=ALU.mult)
        with block() as b:
            @b.vector
            def _(v):
                v.tensor_tensor(out=vrow, in0=mrow, in1=mrow, op=ALU.mult)
        with block() as b:
            @b.vector
            def _(v):
                v.tensor_tensor(out=vrow2, in0=e2row, in1=vrow,
                                op=ALU.subtract)
        with block() as b:
            @b.scalar
            def _(s):
                s.activation(out=vrow, in_=vrow2, func=AF.Ln, bias=epsc[0:1, 0:1])
        with block() as b:
            @b.scalar
            def _(s):
                s.activation(out=rstdr, in_=vrow, func=AF.Exp, scale=-0.5)
        with block() as b:
            @b.tensor
            def _(tt):
                tt.matmul(pf[:, 0:TILE], onesf[0:1, :], mrow, start=True, stop=True)
                tt.matmul(pf[:, TILE:2 * TILE], onesf[0:1, :], rstdr,
                          start=True, stop=True)
        with block() as b:
            @b.vector
            def _(v):
                for k in range(8):
                    v.tensor_tensor(out=xpart(k), in0=xpart(k), in1=pf[:, 0:TILE],
                                    op=ALU.subtract)
                    v.tensor_tensor(out=xnb[:, k * TILE:(k + 1) * TILE],
                                    in0=xpart(k), in1=pf[:, TILE:2 * TILE], op=ALU.mult)
        if "dbg_xn" in dbg and t == 0:
            with block() as b:
                @b.vector
                def _(v):
                    for k in range(8):
                        v.tensor_copy(out=xpart(k), in_=xnb[:, k * TILE:(k + 1) * TILE])
            with block() as b:
                @b.gpsimd
                def _(g):
                    for k in range(8):
                        gq.go(g, dbg["dbg_xn"][:, k * TILE:(k + 1) * TILE], xpart(k))
                    gq.wait(g)

        with block() as b:
            @b.vector
            def _(v):
                v.tensor_copy(
                    out=xbcraw[:, :].rearrange("p (c q) -> p c q", c=18)[:, :, 0:3],
                    in_=tailb[:, :].rearrange("p (c k) -> p c k", c=18))

        def inproj_psum(grp, j, mw=128):
            buf = pf if grp % 2 == 0 else (pq if j < 2 else pg)
            jj = j if grp % 2 == 0 else j % 2
            return buf[0:mw, jj * TILE:(jj + 1) * TILE]

        def inproj_evac(eng, grp, kind):
            for j, m in enumerate(MGROUPS[grp]):
                src = inproj_psum(grp, j)
                if m < 16 and kind == 's':
                    eng.activation(out=siluz[:, m * TILE:(m + 1) * TILE],
                                   in_=src, func=AF.Silu)
                elif 16 <= m < 34 and kind == 'v':
                    c = m - 16
                    eng.tensor_copy(out=xbcraw[:, c * CR + 3:(c + 1) * CR], in_=src)
                elif m == 34 and kind == 'v':
                    eng.tensor_copy(out=dtraw[:, :], in_=src[0:32, :])

        for grp, ml in enumerate(MGROUPS):
            half = grp % 2
            gw = grp_cols(ml)
            gq_target = gq.n
            with block() as b:
                @b.gpsimd
                def _(g, grp=grp):
                    nxt = grp + 1
                    if nxt < len(MGROUPS):
                        win_load(g, nxt, nxt % 2)
                    if grp >= 5:
                        cast_feed(g)
                @b.tensor
                def _(tt, grp=grp, ml=ml, half=half, gw=gw, gq_target=gq_target):
                    gq.wait(tt, gq_target)
                    for j, m in enumerate(ml):
                        mw = 128 if m < 34 else 32
                        for k in range(8):
                            tt.matmul(inproj_psum(grp, j, mw),
                                      wslice[:, half * WHALF + k * gw + j * 128:
                                             half * WHALF + k * gw + j * 128 + mw],
                                      xnb[:, k * TILE:(k + 1) * TILE],
                                      start=(k == 0), stop=(k == 7))
                @b.scalar
                def _(s, grp=grp):
                    if grp > 0:
                        inproj_evac(s, grp - 1, 's')
                @b.vector
                def _(v, grp=grp):
                    if grp > 0:
                        inproj_evac(v, grp - 1, 'v')
        with block() as b:
            @b.vector
            def _(v):
                inproj_evac(v, len(MGROUPS) - 1, 'v')

        with block() as b:
            @b.gpsimd
            def _(g, t=t):
                if t + 1 < NT:
                    win_load(g, 0, 0)
            @b.vector
            def _(v):
                v.tensor_copy(
                    out=tailb[:, :].rearrange("p (c k) -> p c k", c=18),
                    in_=xbcraw[:, :].rearrange("p (c q) -> p c q", c=18)[:, :, TILE:TILE + 3])

        for c in range(19):
            with block() as b:
                @b.vector
                def _(v, c=c):
                    if c < 18:
                        base = c * CR
                        o1 = (c % 2) * 1024
                        v.tensor_scalar(out=ytok[:, o1:o1 + TILE],
                                        in0=xbcraw[:, base:base + TILE],
                                        scalar1=convw_c[:, c * 4:c * 4 + 1],
                                        scalar2=convb_c[:, c:c + 1],
                                        op0=ALU.mult, op1=ALU.add)
                        v.scalar_tensor_tensor(out=ytok[:, o1 + TILE:o1 + 2 * TILE],
                                               in0=xbcraw[:, base + 1:base + 1 + TILE],
                                               scalar=convw_c[:, c * 4 + 1:c * 4 + 2],
                                               in1=ytok[:, o1:o1 + TILE],
                                               op0=ALU.mult, op1=ALU.add)
                        v.scalar_tensor_tensor(out=ytok[:, o1:o1 + TILE],
                                               in0=xbcraw[:, base + 2:base + 2 + TILE],
                                               scalar=convw_c[:, c * 4 + 2:c * 4 + 3],
                                               in1=ytok[:, o1 + TILE:o1 + 2 * TILE],
                                               op0=ALU.mult, op1=ALU.add)
                        v.scalar_tensor_tensor(out=ytok[:, o1 + TILE:o1 + 2 * TILE],
                                               in0=xbcraw[:, base + 3:base + 3 + TILE],
                                               scalar=convw_c[:, c * 4 + 3:c * 4 + 4],
                                               in1=ytok[:, o1:o1 + TILE],
                                               op0=ALU.mult, op1=ALU.add)
                @b.scalar
                def _(s, c=c):
                    cp = c - 1
                    if cp >= 0:
                        base = cp * CR
                        o1 = (cp % 2) * 1024
                        s.activation(out=xbcraw[:, base + 3:base + 3 + TILE],
                                     in_=ytok[:, o1 + TILE:o1 + 2 * TILE], func=AF.Silu)
        if "dbg_xbc" in dbg and t == 0:
            with block() as b:
                @b.gpsimd
                def _(g):
                    gq.go(g, dbg["dbg_xbc"][:, :].rearrange("p (c q) -> p c q", c=18),
                          xbcraw[:, :].rearrange("p (c q) -> p c q", c=18)[:, :, 3:3 + TILE])
                    gq.wait(g)

        with block() as b:
            @b.vector
            def _(v):
                v.tensor_scalar(out=dtp[:, :], in0=dtraw[:, :],
                                scalar1=dtbias_c[:, 0:1], scalar2=None, op0=ALU.add)
        with block() as b:
            @b.scalar
            def _(s):
                s.activation(out=dta[:, :], in_=dtp[:, :], func=AF.Abs)
            @b.vector
            def _(v):
                v.tensor_scalar(out=dtraw[:, :], in0=dtp[:, :], scalar1=0.0,
                                scalar2=None, op0=ALU.max)
        with block() as b:
            @b.scalar
            def _(s):
                s.activation(out=lndt[:, :], in_=dta[:, :], func=AF.Exp, scale=-1.0)
        with block() as b:
            @b.scalar
            def _(s):
                s.activation(out=dta[:, :], in_=lndt[:, :], func=AF.Ln, bias=onec32[:, 0:1])
        with block() as b:
            @b.vector
            def _(v):
                v.tensor_tensor(out=dtp[:, :], in0=dtraw[:, :], in1=dta[:, :], op=ALU.add)
        with block() as b:
            @b.vector
            def _(v):
                v.tensor_scalar(out=dta[:, :], in0=dtp[:, :], scalar1=acol[:, 0:1],
                                scalar2=None, op0=ALU.mult)
            @b.scalar
            def _(s):
                s.activation(out=lndt[:, :], in_=dtp[:, :], func=AF.Ln)
        if "dbg_dt" in dbg and t == 0:
            with block() as b:
                @b.gpsimd
                def _(g):
                    gq.go(g, dbg["dbg_dt"][:, :], dtp[:, :])
                    gq.wait(g)

        for c in range(4):
            cg = t * 4 + c
            c0 = c * 128
            with block() as b:
                @b.vector
                def _(v, c0=c0):
                    v.tensor_tensor_scan(out=ellf[:, :], data0=ones32[:, :],
                                         data1=dta[:, c0:c0 + 128], initial=0.0,
                                         op0=ALU.mult, op1=ALU.add)
            with block() as b:
                @b.vector
                def _(v, c0=c0):
                    v.tensor_tensor(out=lmef[:, :], in0=lndt[:, c0:c0 + 128],
                                    in1=ellf[:, :], op=ALU.subtract)
            with block() as b:
                @b.vector
                def _(v):
                    v.tensor_copy(out=hl1[:, :], in_=lmef[:, :])
                    v.tensor_copy(out=hl2[:, :], in_=ellf[:, :])
            with block() as b:
                @b.vector
                def _(v):
                    v.tensor_tensor(out=lmef[:, :], in0=lmef[:, :], in1=hl1[:, :],
                                    op=ALU.subtract)
                    v.tensor_tensor(out=ellf[:, :], in0=ellf[:, :], in1=hl2[:, :],
                                    op=ALU.subtract)
            with block() as b:
                @b.gpsimd
                def _(g):
                    gq.go(g, ellbuf[0:32, :], hl1[:, :])
                    gq.go(g, ellbuf[32:64, :], lmef[:, :])
                    gq.go(g, ellbuf[64:96, :], hl2[:, :])
                    gq.go(g, ellbuf[96:128, :], ellf[:, :])
                    gq.wait(g)
                    flat = ellbuf.ap().rearrange("(a p) q -> a (p q)", a=4)
                    gq.go(g, lstk[0:1, :], flat[0:1, :])
                    gq.go(g, lstk[1:2, :], flat[1:2, :])
                    gq.go(g, rstk[2:3, :], flat[2:3, :])
                    gq.go(g, rstk[3:4, :], flat[3:4, :])
                    gq.wait(g)
            with block() as b:
                @b.vector
                def _(v):
                    v.tensor_tensor(out=ellf[:, :], in0=hl2[:, :], in1=ellf[:, :], op=ALU.add)
            with block() as b:
                @b.tensor
                def _(tt, c0=c0):
                    tt.transpose(pg[:, 0:32], ellf[:, :], idf[0:32, 0:32])
                    tt.transpose(pg[:, 32:64], dtp[:, c0:c0 + 128], idf[0:32, 0:32])
                    tt.transpose(pg[0:1, 64:96], ellf[:, 127:128], idf[0:32, 0:32])
            with block() as b:
                @b.vector
                def _(v):
                    v.tensor_copy(out=ellT[:, :], in_=pg[:, 0:32])
                    v.tensor_copy(out=dtT[:, :], in_=pg[:, 32:64])
                    v.tensor_copy(out=lendr[:, :], in_=pg[0:1, 64:96])
            with block() as b:
                @b.scalar
                def _(s):
                    s.activation(out=ecolT[:, :], in_=ellT[:, :], func=AF.Exp)
            with block() as b:
                @b.scalar
                def _(s):
                    s.activation(out=lendE[:, :], in_=lendr[:, :], func=AF.Exp)
            with block() as b:
                @b.tensor
                def _(tt, c0=c0):
                    tt.matmul(pf[:, 0:32], onesf[0:1, :], lendr[:, :], start=True, stop=True)
                    tt.matmul(pf[:, 32:64], onesf[0:1, :], lendE[:, :], start=True, stop=True)
                    tt.matmul(pf[:, 128:256],
                              xbcraw[:, 16 * CR + 3 + c0:16 * CR + 3 + c0 + 128],
                              xbcraw[:, 17 * CR + 3 + c0:17 * CR + 3 + c0 + 128],
                              start=True, stop=True)
            with block() as b:
                @b.vector
                def _(v):
                    v.tensor_tensor(out=ftmp[:, :], in0=pf[:, 0:32], in1=ellT[:, :],
                                    op=ALU.subtract)
                    v.tensor_copy(out=edec[:, :], in_=pf[:, 32:64])
                    v.tensor_copy(out=gtile[:, :], in_=pf[:, 128:256])
            with block() as b:
                @b.scalar
                def _(s):
                    s.activation(out=ftmp2[:, :], in_=ftmp[:, :], func=AF.Exp)
            with block() as b:
                @b.vector
                def _(v):
                    v.tensor_tensor(out=ffac[:, :], in0=ftmp2[:, :], in1=dtT[:, :],
                                    op=ALU.mult)
            with block() as b:
                @b.tensor
                def _(tt, c0=c0):
                    for kc in range(16):
                        tt.transpose(pgb[:, kc * 128:(kc + 1) * 128],
                                     xbcraw[:, kc * CR + 3 + c0:kc * CR + 3 + c0 + 128],
                                     idb[:, :])
            with block() as b:
                @b.vector
                def _(v):
                    for kc in range(8):
                        v.tensor_copy(out=xtok[:, kc * 128:(kc + 1) * 128],
                                      in_=pgb[:, kc * 128:(kc + 1) * 128])
                @b.scalar
                def _(s):
                    for kc in range(8, 16):
                        s.copy(out=xtok[:, kc * 128:(kc + 1) * 128],
                               in_=pgb[:, kc * 128:(kc + 1) * 128])
            with block() as b:
                @b.tensor
                def _(tt, c0=c0):
                    for kc in range(16):
                        tt.transpose(pgb[:, kc * 128:(kc + 1) * 128],
                                     siluz[:, kc * TILE + c0:kc * TILE + c0 + 128],
                                     idb[:, :])
            with block() as b:
                @b.vector
                def _(v):
                    for kc in range(8):
                        v.tensor_copy(out=szt[:, kc * 128:(kc + 1) * 128],
                                      in_=pgb[:, kc * 128:(kc + 1) * 128])
                @b.scalar
                def _(s):
                    for kc in range(8, 16):
                        s.copy(out=szt[:, kc * 128:(kc + 1) * 128],
                               in_=pgb[:, kc * 128:(kc + 1) * 128])
            with block() as b:
                @b.tensor
                def _(tt, c0=c0):
                    tt.transpose(pgb[:, 0:128],
                                 xbcraw[:, 16 * CR + 3 + c0:16 * CR + 3 + c0 + 128],
                                 idb[:, :])
            with block() as b:
                @b.vector
                def _(v):
                    v.tensor_copy(out=btok[:, :], in_=pgb[:, 0:128])
            with block() as b:
                @b.tensor
                def _(tt):
                    for h in range(16):
                        tt.matmul(pf[:, h * 128:(h + 1) * 128],
                                  lstk[:, h * 128:(h + 1) * 128],
                                  rstk[:, h * 128:(h + 1) * 128], start=True, stop=True)
            with block() as b:
                @b.vector
                def _(v):
                    v.tensor_tensor(
                        out=difsb[:, :].rearrange("p (h q) -> p h q", h=16),
                        in0=pf[:, 0:2048].rearrange("p (h q) -> p h q", h=16),
                        in1=masks[:, :].rearrange("p q -> p () q").broadcast_to((128, 16, 128)),
                        op=ALU.add)
            with block() as b:
                @b.scalar
                def _(s):
                    s.activation(out=expsb[:, :], in_=difsb[:, :], func=AF.Exp)
                @b.tensor
                def _(tt):
                    for h in range(16, 32):
                        tt.matmul(pf[:, (h - 16) * 128:(h - 15) * 128],
                                  lstk[:, h * 128:(h + 1) * 128],
                                  rstk[:, h * 128:(h + 1) * 128], start=True, stop=True)
            with block() as b:
                @b.vector
                def _(v):
                    v.tensor_tensor(
                        out=lhsT_all[:, 0:2048].rearrange("p (h q) -> p h q", h=16),
                        in0=expsb[:, :].rearrange("p (h q) -> p h q", h=16),
                        in1=gtile[:, :].rearrange("p q -> p () q").broadcast_to((128, 16, 128)),
                        op=ALU.mult)
                    v.tensor_tensor(
                        out=difsb[:, :].rearrange("p (h q) -> p h q", h=16),
                        in0=pf[:, 0:2048].rearrange("p (h q) -> p h q", h=16),
                        in1=masks[:, :].rearrange("p q -> p () q").broadcast_to((128, 16, 128)),
                        op=ALU.add)
            with block() as b:
                @b.scalar
                def _(s):
                    s.activation(out=expsb[:, :], in_=difsb[:, :], func=AF.Exp)
            with block() as b:
                @b.vector
                def _(v):
                    v.tensor_tensor(
                        out=lhsT_all[:, 2048:4096].rearrange("p (h q) -> p h q", h=16),
                        in0=expsb[:, :].rearrange("p (h q) -> p h q", h=16),
                        in1=gtile[:, :].rearrange("p q -> p () q").broadcast_to((128, 16, 128)),
                        op=ALU.mult)
            for half in range(2):
                h0 = half * 16
                with block() as b:
                    @b.tensor
                    def _(tt, h0=h0, c0=c0):
                        for i in range(16):
                            h = h0 + i
                            tt.matmul(pf[:, i * 64:(i + 1) * 64],
                                      lhsT_all[:, h * 128:(h + 1) * 128],
                                      xtok[:, h * 64:(h + 1) * 64], start=True, stop=True)
                        for i in range(16):
                            h = h0 + i
                            tt.matmul(pf[:, 1024 + i * 64:1024 + (i + 1) * 64],
                                      xbcraw[:, 17 * CR + 3 + c0:17 * CR + 3 + c0 + 128],
                                      hstb[:, h * 64:(h + 1) * 64], start=True, stop=True)
                with block() as b:
                    @b.vector
                    def _(v, h0=h0, half=half):
                        v.tensor_tensor(
                            out=ytmp2[:, 0:1024].rearrange("p (h q) -> p h q", h=16),
                            in0=pf[:, 1024:2048].rearrange("p (h q) -> p h q", h=16),
                            in1=ecolT[:, h0:h0 + 16].rearrange("p h -> p h ()")
                                .broadcast_to((128, 16, 64)),
                            op=ALU.mult)
                        v.tensor_tensor(out=ygat[:, half * 1024:(half + 1) * 1024],
                                        in0=ytmp2[:, 0:1024], in1=pf[:, 0:1024],
                                        op=ALU.add)
            with block() as b:
                @b.vector
                def _(v):
                    v.tensor_tensor(
                        out=ytmp2[:, :].rearrange("p (h q) -> p h q", h=32),
                        in0=xtok[:, :].rearrange("p (h q) -> p h q", h=32),
                        in1=dstage[:, :].rearrange("p h -> p h ()").broadcast_to((128, 32, 64)),
                        op=ALU.mult)
            with block() as b:
                @b.vector
                def _(v):
                    v.tensor_tensor(out=ynrm[:, :], in0=ygat[:, :], in1=ytmp2[:, :],
                                    op=ALU.add)
            with block() as b:
                @b.vector
                def _(v):
                    v.tensor_tensor(out=ygat[:, :], in0=ynrm[:, :], in1=szt[:, :],
                                    op=ALU.mult)
            if "dbg_y" in dbg and cg == 0:
                with block() as b:
                    @b.gpsimd
                    def _(g):
                        gq.go(g, dbg["dbg_y"][:, :], ygat[:, :])
                        gq.wait(g)
            with block() as b:
                @b.scalar
                def _(s):
                    s.activation(out=ytmp2[:, :], in_=ygat[:, :], func=AF.Square,
                                 accum_out=rmscol[:, :])
            with block() as b:
                @b.scalar
                def _(s):
                    s.activation(out=rstdc[:, :], in_=rmscol[:, :], func=AF.Ln,
                                 bias=epsc[:, 0:1], scale=1.0 / 2048)
            with block() as b:
                @b.scalar
                def _(s):
                    s.activation(out=rstd2c[:, :], in_=rstdc[:, :], func=AF.Exp, scale=-0.5)
            with block() as b:
                @b.vector
                def _(v):
                    v.tensor_scalar(out=ynrm[:, :], in0=ygat[:, :], scalar1=rstd2c[:, 0:1],
                                    scalar2=None, op0=ALU.mult)
            with block() as b:
                @b.tensor
                def _(tt):
                    for j in range(4):
                        tt.matmul(pf[:, j * 512:(j + 1) * 512], jrevs[:, :],
                                  ynrm[:, j * 512:(j + 1) * 512], start=True, stop=True)
            with block() as b:
                @b.vector
                def _(v):
                    for j in range(2):
                        v.tensor_copy(out=yrev[:, j * 512:(j + 1) * 512],
                                      in_=pf[:, j * 512:(j + 1) * 512])
                @b.scalar
                def _(s):
                    for j in range(2, 4):
                        s.copy(out=yrev[:, j * 512:(j + 1) * 512],
                               in_=pf[:, j * 512:(j + 1) * 512])

            def op_pass(src, dst_dram, slot):
                with block() as b:
                    @b.tensor
                    def _(tt, src=src):
                        for kc in range(16):
                            tt.transpose(pgb[:, kc * 128:(kc + 1) * 128],
                                         src[:, kc * 128:(kc + 1) * 128], idb[:, :])
                    @b.gpsimd
                    def _(g):
                        store_flush(g)
                with block() as b:
                    @b.vector
                    def _(v):
                        for kc in range(8):
                            v.tensor_copy(out=ytfm[:, kc * 128:(kc + 1) * 128],
                                          in_=pgb[:, kc * 128:(kc + 1) * 128])
                    @b.scalar
                    def _(s):
                        for kc in range(8, 16):
                            s.copy(out=ytfm[:, kc * 128:(kc + 1) * 128],
                                   in_=pgb[:, kc * 128:(kc + 1) * 128])
                with block() as b:
                    @b.tensor
                    def _(tt):
                        for m in range(8):
                            for k in range(16):
                                tt.matmul(pf[:, m * 128:(m + 1) * 128],
                                          woutb[:, k * D_MODEL + m * 128:
                                                k * D_MODEL + (m + 1) * 128],
                                          ytfm[:, k * 128:(k + 1) * 128],
                                          start=(k == 0), stop=(k == 15))
                    @b.gpsimd
                    def _(g):
                        cast_feed(g)
                with block() as b:
                    @b.vector
                    def _(v):
                        for m in range(4):
                            v.tensor_copy(out=mstage[:, m * 128:(m + 1) * 128],
                                          in_=pf[:, m * 128:(m + 1) * 128])
                    @b.scalar
                    def _(s):
                        for m in range(4, 8):
                            s.copy(out=mstage[:, m * 128:(m + 1) * 128],
                                   in_=pf[:, m * 128:(m + 1) * 128])
                pending_stores.append(
                    (dst_dram.ap().rearrange("p (k q) -> p k q", k=8)
                     [:, :, slot * 128:(slot + 1) * 128],
                     mstage[:, :].rearrange("p (k q) -> p k q", k=8)))

            op_pass(yrev, arin, 31 - cg)
            if cg < 16:
                op_pass(ynrm, mnat, cg)

            with block() as b:
                @b.vector
                def _(v):
                    v.tensor_tensor(
                        out=xts[:, :].rearrange("p (h q) -> p h q", h=32),
                        in0=xtok[:, :].rearrange("p (h q) -> p h q", h=32),
                        in1=ffac[:, :].rearrange("p h -> p h ()").broadcast_to((128, 32, 64)),
                        op=ALU.mult)
            with block() as b:
                @b.tensor
                def _(tt):
                    for h in range(32):
                        tt.matmul(pf[:, h * 64:(h + 1) * 64], btok[:, :],
                                  xts[:, h * 64:(h + 1) * 64], start=True, stop=True)
                @b.gpsimd
                def _(g):
                    store_flush(g)
                    cast_feed(g)
            with block() as b:
                @b.vector
                def _(v):
                    v.tensor_tensor(
                        out=ytok[:, :].rearrange("p (h q) -> p h q", h=32),
                        in0=hstf[:, :].rearrange("p (h q) -> p h q", h=32),
                        in1=edec[:, :].rearrange("p h -> p h ()").broadcast_to((128, 32, 64)),
                        op=ALU.mult)
            with block() as b:
                @b.vector
                def _(v):
                    v.tensor_tensor(out=hstf[:, :], in0=ytok[:, :], in1=pf[:, 0:2048],
                                    op=ALU.add)
            with block() as b:
                @b.scalar
                def _(s):
                    s.copy(out=hstb[:, :], in_=hstf[:, :])

    with block() as b:
        @b.gpsimd
        def _(g):
            g.collective_compute(
                "AllReduce", ALU.add,
                replica_groups=[[0, 1], [2, 3], [4, 5], [6, 7]],
                ins=[arin.ap().opt()], outs=[arout.ap().opt()],
            ).then_inc(ccs)
            cc_n[0] += 1
            g.wait_ge(ccs, cc_n[0])

    if "dbg_mrev" in dbg:
        with block() as b:
            @b.gpsimd
            def _(g):
                gq.go(g, dbg["dbg_mrev"][:, :], arin[:, :])
                gq.wait(g)
    if "dbg_mnat" in dbg:
        with block() as b:
            @b.gpsimd
            def _(g):
                gq.go(g, dbg["dbg_mnat"][:, :], mnat[:, :])
                gq.wait(g)

    if cast_chunks or pending_stores:
        with block() as b:
            @b.gpsimd
            def _(g):
                store_flush(g)
                cast_feed(g, len(cast_chunks))

    rwf = gtile
    with block() as b:
        @b.gpsimd
        def _(g):
            gq.go(g, rwf[:, :].rearrange("p (k e) -> p k e", k=8),
                  d["rwT"].ap().rearrange("(k p) e -> p k e", p=128))
            gq.go(g, dstage[:, 0:16], d["ebias"][:, 0:16])
            gq.go(g, ffac[:, 0:16], d["tiew"][:, 0:16])
            gq.go(g, ffac[:, 16:20], d["tiew4"][:, 0:4])
            g.memset(rmscol[:, :], 0.0)
            gq.wait(g)
    ebias_t = ellT
    tiew_t = dtT
    with block() as b:
        @b.gpsimd
        def _(g):
            gq.go(g, ebias_t[:, 0:16], d["ebias"][:, 0:16])
            gq.go(g, tiew_t[:, 0:16], d["tiew"][:, 0:16])
            gq.go(g, tiew_t[:, 16:20], d["tiew4"][:, 0:4])
            gq.wait(g)

    rwh = mstage[:, 0:128]
    rwl = mstage[:, 128:256]
    with block() as b:
        @b.vector
        def _(v):
            v.tensor_copy(out=rwh, in_=rwf[:, :])
    with block() as b:
        @b.vector
        def _(v):
            v.tensor_tensor(out=rwl, in0=rwf[:, :], in1=rwh, op=ALU.subtract)

    NTB = 8
    TB = 256
    gww = siluz
    uww = xbcraw
    DWRING = [lhsT_all[:, 0:4096], wslice[:, 0:4096], wslice[:, 4096:8192]]

    def moe_w_load(g, e, half):
        moe_w_load_gu(g, e, half)
        moe_w_load_d(g, e)

    def moe_w_load_gu(g, e, half):
        gq.go(g, gww[:, half * 4096:(half + 1) * 4096].rearrange("p (k i) -> p k i", k=8),
              gwb.ap().rearrange("e (k p) i -> p (e k) i", p=128)[:, e * 8:(e + 1) * 8, :])
        gq.go(g, uww[:, half * 4096:(half + 1) * 4096].rearrange("p (k i) -> p k i", k=8),
              uwb.ap().rearrange("e (k p) i -> p (e k) i", p=128)[:, e * 8:(e + 1) * 8, :])

    def moe_w_load_d(g, e):
        gq.go(g, DWRING[e % 3].rearrange("p (k f) -> p k f", k=4),
              dwb.ap().rearrange("e (k p) f -> p (e k) f", p=128)[:, e * 4:(e + 1) * 4, :])

    moe_tgt = {}

    for ti in range(NTB):
        tok0 = ti * TB
        with block() as b:
            @b.gpsimd
            def _(g, tok0=tok0, ti=ti):
                for k in range(8):
                    gq.go(g, xpart(k)[:, 0:TB],
                          d["x"].ap().rearrange("(k p) q -> p k q", p=128)
                          [:, k, tok0:tok0 + TB])
                gq.go(g, xtok[:, :].rearrange("p (k q) -> p k q", k=8),
                      arout.ap().rearrange("p (k q) -> p k q", k=8)[:, :, tok0:tok0 + TB])
                gq.go(g, szt[:, :].rearrange("p (k q) -> p k q", k=8),
                      arin.ap().rearrange("p (k q) -> p k q", k=8)[:, :, tok0:tok0 + TB])
                gq.go(g, ynrm[:, :].rearrange("p (k q) -> p k q", k=8),
                      mnat.ap().rearrange("p (k q) -> p k q", k=8)[:, :, tok0:tok0 + TB])
                moe_w_load(g, 0, 0)
                moe_tgt[0] = gq.n
                gq.wait(g)
        if "dbg_y" in dbg and ti == 0:
            with block() as b:
                @b.gpsimd
                def _(g):
                    for k in range(8):
                        gq.go(g, dbg["dbg_y"][:, k * TB:(k + 1) * TB], xpart(k)[:, 0:TB])
                    gq.wait(g)
        with block() as b:
            @b.vector
            def _(v):
                v.tensor_tensor(out=yrev[:, :], in0=xtok[:, :], in1=szt[:, :],
                                op=ALU.subtract)
        with block() as b:
            @b.vector
            def _(v):
                v.tensor_tensor(out=yrev[:, :], in0=yrev[:, :], in1=ynrm[:, :], op=ALU.add)
        if "dbg_s" in dbg:
            with block() as b:
                @b.gpsimd
                def _(g, tok0=tok0):
                    gq.go(g, dbg["dbg_s"][:, :].rearrange("p (k q) -> p k q", k=8)
                          [:, :, tok0:tok0 + TB],
                          yrev[:, :].rearrange("p (k q) -> p k q", k=8))
                    gq.wait(g)
        with block() as b:
            @b.vector
            def _(v):
                for k in range(8):
                    v.tensor_copy(out=expsb[:, k * TB:(k + 1) * TB], in_=xpart(k)[:, 0:TB])
            @b.scalar
            def _(s):
                for k in range(8):
                    s.activation(out=hstb[:, k * TB:(k + 1) * TB], in_=xpart(k)[:, 0:TB],
                                 func=AF.Square)
        with block() as b:
            @b.tensor
            def _(tt):
                for k in range(8):
                    tt.matmul(pg[0:1, 0:TB], onesb[:, 0:1], expsb[:, k * TB:(k + 1) * TB],
                              start=(k == 0), stop=(k == 7))
                for k in range(8):
                    tt.matmul(pg[0:1, 512:512 + TB], onesb[:, 0:1],
                              hstb[:, k * TB:(k + 1) * TB],
                              start=(k == 0), stop=(k == 7))
        with block() as b:
            @b.vector
            def _(v):
                v.tensor_scalar(out=mrow[0:1, 0:TB], in0=pg[0:1, 0:TB], scalar1=1.0 / 1024,
                                scalar2=None, op0=ALU.mult)
                v.tensor_scalar(out=e2row[0:1, 0:TB], in0=pg[0:1, 512:512 + TB],
                                scalar1=1.0 / 1024, scalar2=None, op0=ALU.mult)
        with block() as b:
            @b.vector
            def _(v):
                v.tensor_tensor(out=vrow[0:1, 0:TB], in0=mrow[0:1, 0:TB],
                                in1=mrow[0:1, 0:TB], op=ALU.mult)
        with block() as b:
            @b.vector
            def _(v):
                v.tensor_tensor(out=vrow2[0:1, 0:TB], in0=e2row[0:1, 0:TB],
                                in1=vrow[0:1, 0:TB], op=ALU.subtract)
        with block() as b:
            @b.scalar
            def _(s):
                s.activation(out=vrow[0:1, 0:TB], in_=vrow2[0:1, 0:TB], func=AF.Ln,
                             bias=epsc[0:1, 0:1])
        with block() as b:
            @b.scalar
            def _(s):
                s.activation(out=rstdr[0:1, 0:TB], in_=vrow[0:1, 0:TB], func=AF.Exp,
                             scale=-0.5)
        with block() as b:
            @b.tensor
            def _(tt):
                tt.matmul(pq[:, 0:TB], onesf[0:1, :], mrow[0:1, 0:TB], start=True, stop=True)
                tt.matmul(pq[:, 512:512 + TB], onesf[0:1, :], rstdr[0:1, 0:TB],
                          start=True, stop=True)
        with block() as b:
            @b.vector
            def _(v):
                for k in range(8):
                    v.tensor_tensor(out=difsb[:, k * TB:(k + 1) * TB],
                                    in0=xpart(k)[:, 0:TB], in1=pq[:, 0:TB], op=ALU.subtract)
        with block() as b:
            @b.vector
            def _(v):
                for k in range(8):
                    v.tensor_tensor(out=difsb[:, k * TB:(k + 1) * TB],
                                    in0=difsb[:, k * TB:(k + 1) * TB],
                                    in1=pq[:, 512:512 + TB], op=ALU.mult)
        with block() as b:
            @b.vector
            def _(v):
                v.tensor_copy(out=ytfm[:, :], in_=difsb[:, 0:2048])
        with block() as b:
            @b.vector
            def _(v):
                v.tensor_tensor(out=expsb[:, :], in0=difsb[:, 0:2048], in1=ytfm[:, :],
                                op=ALU.subtract)
        if "dbg_ffn" in dbg and ti == 0:
            with block() as b:
                @b.gpsimd
                def _(g):
                    gq.go(g, dbg["dbg_ffn"][:, 0:2048], ytfm[:, :])
                    for k in range(8):
                        gq.go(g, dbg["dbg_ffn"][:, 2048 + k * TB:2048 + (k + 1) * TB],
                              xpart(k)[:, 0:TB])
                    gq.wait(g)
        if "dbg_upr" in dbg and ti == 0:
            with block() as b:
                @b.gpsimd
                def _(g):
                    gq.go(g, dbg["dbg_upr"][:, 0:2048], ytfm[:, :])
                    gq.go(g, dbg["dbg_upr"][:, 2048:4096], expsb[:, :])
                    gq.go(g, dbg["dbg_upr"][:, 4096:4224], rwh)
                    gq.go(g, dbg["dbg_upr"][:, 4224:4352], rwl)
                    gq.wait(g)
        with block() as b:
            @b.tensor
            def _(tt):
                for k in range(8):
                    for lhs, rhs in ((rwh, ytfm), (rwh, expsb), (rwl, ytfm)):
                        tt.matmul(pg[0:16, 0:TB],
                                  lhs[:, k * 16:(k + 1) * 16],
                                  rhs[:, k * TB:(k + 1) * TB],
                                  start=(k == 0 and lhs is rwh and rhs is ytfm),
                                  stop=(k == 7 and lhs is rwl))
        with block() as b:
            @b.vector
            def _(v):
                v.tensor_copy(out=dtraw[0:16, 0:TB], in_=pg[0:16, 0:TB])
        with block() as b:
            @b.tensor
            def _(tt):
                for j in range(2):
                    tt.transpose(pq[0:128, j * 512:j * 512 + 16],
                                 dtraw[0:16, j * 128:(j + 1) * 128], idf[0:16, 0:16])
        with block() as b:
            @b.vector
            def _(v):
                v.tensor_copy(out=ecolT[:, 0:16], in_=pq[0:128, 0:16])
                v.tensor_copy(out=ecolT[:, 16:32], in_=pq[0:128, 512:528])
        with block() as b:
            @b.scalar
            def _(s):
                s.activation(out=edec[:, 0:32], in_=ecolT[:, 0:32], func=AF.Sigmoid)
        with block() as b:
            @b.vector
            def _(v):
                v.tensor_tensor(
                    out=ftmp2[:, 0:32].rearrange("p (s e) -> p s e", s=2),
                    in0=edec[:, 0:32].rearrange("p (s e) -> p s e", s=2),
                    in1=ebias_t[:, 0:16].rearrange("p e -> p () e").broadcast_to((128, 2, 16)),
                    op=ALU.add)
        gsv = difsb[:, 0:8]
        with block() as b:
            @b.vector
            def _(v):
                scg = ftmp2[:, 0:32].rearrange("p (sg j) -> p sg j", j=4)
                for i, (a, bb) in enumerate([(0, 1), (0, 2), (0, 3), (1, 2), (1, 3), (2, 3)]):
                    v.tensor_tensor(
                        out=difsb[:, 1024 + i * 8:1032 + i * 8].rearrange("p (sg) -> p sg ()"),
                        in0=scg[:, :, a:a + 1], in1=scg[:, :, bb:bb + 1], op=ALU.add)
        with block() as b:
            @b.vector
            def _(v):
                v.tensor_tensor(out=difsb[:, 1072:1080], in0=difsb[:, 1024:1032],
                                in1=difsb[:, 1032:1040], op=ALU.max)
                v.tensor_tensor(out=difsb[:, 1080:1088], in0=difsb[:, 1040:1048],
                                in1=difsb[:, 1048:1056], op=ALU.max)
                v.tensor_tensor(out=difsb[:, 1088:1096], in0=difsb[:, 1056:1064],
                                in1=difsb[:, 1064:1072], op=ALU.max)
        with block() as b:
            @b.vector
            def _(v):
                v.tensor_tensor(out=difsb[:, 1096:1104], in0=difsb[:, 1072:1080],
                                in1=difsb[:, 1080:1088], op=ALU.max)
        with block() as b:
            @b.vector
            def _(v):
                v.tensor_tensor(out=gsv, in0=difsb[:, 1096:1104],
                                in1=difsb[:, 1088:1096], op=ALU.max)
        if "dbg_upr" in dbg and ti == 0:
            with block() as b:
                @b.gpsimd
                def _(g):
                    gq.go(g, dbg["dbg_upr"][:, 4352:4360], gsv)
                    gq.wait(g)
        gmask = difsb[:, 16:24]
        with block() as b:
            @b.vector
            def _(v):
                v.tensor_scalar(out=gmask, in0=gsv, scalar1=0.0, scalar2=None, op0=ALU.mult)
        for it in range(2):
            with block() as b:
                @b.vector
                def _(v):
                    for sj in range(2):
                        gss = difsb[:, sj * 4:(sj + 1) * 4]
                        v.tensor_reduce(out=difsb[:, 24 + sj:25 + sj], in_=gss, axis=AX,
                                        op=ALU.max)
            with block() as b:
                @b.vector
                def _(v):
                    for sj in range(2):
                        gss = difsb[:, sj * 4:(sj + 1) * 4]
                        v.tensor_scalar(out=difsb[:, 32 + sj * 4:32 + (sj + 1) * 4],
                                        in0=gss, scalar1=difsb[:, 24 + sj:25 + sj],
                                        scalar2=None, op0=ALU.is_equal)
            with block() as b:
                @b.vector
                def _(v):
                    v.tensor_tensor(out=difsb[:, 32:40].rearrange("p (s j) -> p s j", s=2),
                                    in0=difsb[:, 32:40].rearrange("p (s j) -> p s j", s=2),
                                    in1=tiew_t[:, 16:20].rearrange("p j -> p () j")
                                    .broadcast_to((128, 2, 4)),
                                    op=ALU.mult)
            with block() as b:
                @b.vector
                def _(v):
                    for sj in range(2):
                        v.tensor_reduce(out=difsb[:, 24 + sj:25 + sj],
                                        in_=difsb[:, 32 + sj * 4:32 + (sj + 1) * 4],
                                        axis=AX, op=ALU.max)
            with block() as b:
                @b.vector
                def _(v):
                    for sj in range(2):
                        v.tensor_scalar(out=difsb[:, 40 + sj * 4:40 + (sj + 1) * 4],
                                        in0=difsb[:, 32 + sj * 4:32 + (sj + 1) * 4],
                                        scalar1=difsb[:, 24 + sj:25 + sj], scalar2=None,
                                        op0=ALU.is_equal)
            with block() as b:
                @b.vector
                def _(v):
                    v.tensor_tensor(out=gmask, in0=gmask, in1=difsb[:, 40:48], op=ALU.add)
                    v.scalar_tensor_tensor(out=difsb[:, 0:8], in0=difsb[:, 40:48],
                                           scalar=-1e30, in1=difsb[:, 0:8],
                                           op0=ALU.mult, op1=ALU.add)
        with block() as b:
            @b.vector
            def _(v):
                v.tensor_copy(
                    out=difsb[:, 64:96].rearrange("p (sg j) -> p sg j", j=4),
                    in_=gmask.rearrange("p sg -> p sg ()").broadcast_to((128, 8, 4)))
        with block() as b:
            @b.vector
            def _(v):
                v.tensor_tensor(out=difsb[:, 96:128], in0=ftmp2[:, 0:32],
                                in1=difsb[:, 64:96], op=ALU.mult)
        sel = difsb[:, 128:160]
        with block() as b:
            @b.vector
            def _(v):
                v.tensor_scalar(out=sel, in0=difsb[:, 96:128], scalar1=0.0,
                                scalar2=None, op0=ALU.mult)
        for it in range(4):
            with block() as b:
                @b.vector
                def _(v):
                    for sj in range(2):
                        v.tensor_reduce(out=difsb[:, 24 + sj:25 + sj],
                                        in_=difsb[:, 96 + sj * 16:96 + (sj + 1) * 16],
                                        axis=AX, op=ALU.max)
            with block() as b:
                @b.vector
                def _(v):
                    for sj in range(2):
                        v.tensor_scalar(out=difsb[:, 160 + sj * 16:160 + (sj + 1) * 16],
                                        in0=difsb[:, 96 + sj * 16:96 + (sj + 1) * 16],
                                        scalar1=difsb[:, 24 + sj:25 + sj], scalar2=None,
                                        op0=ALU.is_equal)
            with block() as b:
                @b.vector
                def _(v):
                    v.tensor_tensor(out=difsb[:, 160:192].rearrange("p (s e) -> p s e", s=2),
                                    in0=difsb[:, 160:192].rearrange("p (s e) -> p s e", s=2),
                                    in1=tiew_t[:, 0:16].rearrange("p e -> p () e")
                                    .broadcast_to((128, 2, 16)),
                                    op=ALU.mult)
            with block() as b:
                @b.vector
                def _(v):
                    for sj in range(2):
                        v.tensor_reduce(out=difsb[:, 24 + sj:25 + sj],
                                        in_=difsb[:, 160 + sj * 16:160 + (sj + 1) * 16],
                                        axis=AX, op=ALU.max)
            with block() as b:
                @b.vector
                def _(v):
                    for sj in range(2):
                        v.tensor_scalar(out=difsb[:, 192 + sj * 16:192 + (sj + 1) * 16],
                                        in0=difsb[:, 160 + sj * 16:160 + (sj + 1) * 16],
                                        scalar1=difsb[:, 24 + sj:25 + sj], scalar2=None,
                                        op0=ALU.is_equal)
            with block() as b:
                @b.vector
                def _(v):
                    v.tensor_tensor(out=sel, in0=sel, in1=difsb[:, 192:224], op=ALU.add)
                    v.scalar_tensor_tensor(out=difsb[:, 96:128], in0=difsb[:, 192:224],
                                           scalar=-1e30, in1=difsb[:, 96:128],
                                           op0=ALU.mult, op1=ALU.add)
        if "dbg_upr" in dbg and ti == 0:
            with block() as b:
                @b.gpsimd
                def _(g):
                    gq.go(g, dbg["dbg_upr"][:, 4360:4368], gmask)
                    gq.go(g, dbg["dbg_upr"][:, 4368:4400], sel)
                    gq.go(g, dbg["dbg_upr"][:, 4400:4432], difsb[:, 64:96])
                    gq.wait(g)
        with block() as b:
            @b.vector
            def _(v):
                v.tensor_tensor(out=difsb[:, 224:256], in0=edec[:, 0:32], in1=sel,
                                op=ALU.mult)
        with block() as b:
            @b.vector
            def _(v):
                for sj in range(2):
                    v.tensor_reduce(out=difsb[:, 24 + sj:25 + sj],
                                    in_=difsb[:, 224 + sj * 16:224 + (sj + 1) * 16],
                                    axis=AX, op=ALU.add)
        with block() as b:
            @b.vector
            def _(v):
                v.tensor_scalar(out=difsb[:, 24:26], in0=difsb[:, 24:26], scalar1=1e-20,
                                scalar2=None, op0=ALU.add)
        with block() as b:
            @b.vector
            def _(v):
                v.reciprocal(out=difsb[:, 26:28], in_=difsb[:, 24:26])
        with block() as b:
            @b.vector
            def _(v):
                for sj in range(2):
                    v.tensor_scalar(out=difsb[:, 224 + sj * 16:224 + (sj + 1) * 16],
                                    in0=difsb[:, 224 + sj * 16:224 + (sj + 1) * 16],
                                    scalar1=difsb[:, 26 + sj:27 + sj], scalar2=2.5,
                                    op0=ALU.mult, op1=ALU.mult)
        if "dbg_scores" in dbg and ti == 0:
            with block() as b:
                @b.gpsimd
                def _(g):
                    gq.go(g, dbg["dbg_scores"][:, 0:32], ecolT[:, 0:32])
                    gq.go(g, dbg["dbg_scores"][0:16, 32:64].rearrange(
                        "p q -> p () q") if False else dbg["dbg_scores"][0:16, 32:64],
                        dtraw[0:16, 0:32])
                    gq.wait(g)
        if "dbg_comb" in dbg and ti == 0:
            with block() as b:
                @b.vector
                def _(v):
                    v.tensor_copy(out=ftmp[:, 0:32], in_=difsb[:, 224:256])
            with block() as b:
                @b.gpsimd
                def _(g):
                    gq.go(g, dbg["dbg_comb"][:, 0:32], ftmp[:, 0:32])
                    gq.go(g, dbg["dbg_comb"][:, 32:64], edec[:, 0:32])
                    gq.wait(g)
        with block() as b:
            @b.tensor
            def _(tt):
                for j in range(2):
                    tt.transpose(pq[0:16, j * 512:j * 512 + 128],
                                 difsb[:, 224 + j * 16:224 + (j + 1) * 16],
                                 idf[:, :])
        with block() as b:
            @b.vector
            def _(v):
                for j in range(2):
                    v.tensor_copy(out=dta[0:16, j * 128:(j + 1) * 128],
                                  in_=pq[0:16, j * 512:j * 512 + 128])
        if "dbg_scores" in dbg and ti == 0:
            with block() as b:
                @b.gpsimd
                def _(g):
                    gq.go(g, dbg["dbg_scores"][0:16, 0:32], dta[0:16, 0:32])
                    gq.wait(g)
        with block() as b:
            @b.gpsimd
            def _(g):
                gq.go(g, combuf.ap()[0:32, :].rearrange("(e a) q -> e (a q)", a=2),
                      dta[0:16, 0:TB])
                gq.wait(g)
                gq.go(g, lstk[0:1, :],
                      combuf.ap().rearrange("p q -> () (p q)"))
                gq.wait(g)
        if "dbg_scores" in dbg and ti == 0:
            with block() as b:
                @b.gpsimd
                def _(g):
                    gq.go(g, dbg["dbg_scores"][0:1, 32:64], lstk[0:1, 0:32])
                    gq.wait(g)
        def moe_down(tt, e):
            dw = DWRING[e % 3]
            for m in range(8):
                for ic in range(4):
                    tt.matmul(pf[:, m * TB:(m + 1) * TB],
                              dw[:, ic * 1024 + m * 128:ic * 1024 + (m + 1) * 128],
                              ygat[:, 1024 + ic * TB:1024 + (ic + 1) * TB],
                              start=(e == 0 and ic == 0 and m % 2 == 0),
                              stop=False)

        for e in range(16):
            half = e % 2
            goff = half * 4096
            with block() as b:
                @b.tensor
                def _(tt, goff=goff, e=e):
                    if e > 0:
                        moe_down(tt, e - 1)
                    tt.wait_ge(gq.h, moe_tgt[e])
                    for ic in range(4):
                        for k in range(8):
                            tt.matmul(pq[:, ic * TB:(ic + 1) * TB],
                                      gww[:, goff + k * 512 + ic * 128:
                                          goff + k * 512 + (ic + 1) * 128],
                                      ytfm[:, k * TB:(k + 1) * TB],
                                      start=(k == 0), stop=(k == 7))
                    for ic in range(4):
                        for k in range(8):
                            tt.matmul(pg[:, ic * TB:(ic + 1) * TB],
                                      uww[:, goff + k * 512 + ic * 128:
                                          goff + k * 512 + (ic + 1) * 128],
                                      ytfm[:, k * TB:(k + 1) * TB],
                                      start=(k == 0), stop=(k == 7))
                @b.gpsimd
                def _(g, e=e):
                    if e + 1 < 16:
                        moe_w_load(g, e + 1, (e + 1) % 2)
                        moe_tgt[e + 1] = gq.n
            with block() as b:
                @b.scalar
                def _(s):
                    s.activation(out=xts[:, 0:1024], in_=pq[:, 0:1024], func=AF.Silu)
                @b.vector
                def _(v):
                    v.tensor_copy(out=xts[:, 1024:2048], in_=pg[:, 0:1024])

            with block() as b:
                @b.tensor
                def _(tt, e=e):
                    tt.matmul(pg[:, 0:TB], onesrb[0:1, :],
                              lstk[0:1, e * TB:(e + 1) * TB], start=True, stop=True)
            with block() as b:
                @b.vector
                def _(v):
                    v.tensor_copy(out=difsb[:, 256:256 + TB], in_=pg[:, 0:TB])
                    v.tensor_tensor(
                        out=ygat[:, 0:1024].rearrange("p (i q) -> p i q", i=4),
                        in0=xts[:, 1024:2048].rearrange("p (i q) -> p i q", i=4),
                        in1=difsb[:, 256:256 + TB].rearrange("p q -> p () q")
                        .broadcast_to((128, 4, TB)),
                        op=ALU.mult)
                    v.tensor_tensor(out=ygat[:, 1024:2048], in0=xts[:, 0:1024],
                                    in1=ygat[:, 0:1024], op=ALU.mult)
        shv = shb.ap().rearrange("e (k p) f -> p (e k) f", p=128)
        with block() as b:
            @b.tensor
            def _(tt):
                moe_down(tt, 15)
            @b.gpsimd
            def _(g):
                gq.go(g, gww[:, :].rearrange("p (k f) -> p k f", k=8), shv[:, 0:8, :])
                gq.go(g, uww[:, 0:8192].rearrange("p (k f) -> p k f", k=8), shv[:, 8:16, :])
        sh_gu_tgt = gq.n
        with block() as b:
            @b.tensor
            def _(tt):
                tt.wait_ge(gq.h, sh_gu_tgt)
                for m in range(8):
                    dst = pq[:, (m % 4) * TB:(m % 4 + 1) * TB] if m < 4                         else pg[:, (m % 4) * TB:(m % 4 + 1) * TB]
                    for k in range(8):
                        tt.matmul(dst, gww[:, k * 1024 + m * 128:k * 1024 + (m + 1) * 128],
                                  ytfm[:, k * TB:(k + 1) * TB],
                                  start=(k == 0), stop=(k == 7))
            @b.gpsimd
            def _(g):
                gq.go(g, DWRING[1].rearrange("p (k f) -> p k f", k=4), shv[:, 16:20, :])
                gq.go(g, DWRING[2].rearrange("p (k f) -> p k f", k=4), shv[:, 20:24, :])
        sh_d_tgt = gq.n
        with block() as b:
            @b.scalar
            def _(s):
                s.activation(out=xts[:, 0:1024], in_=pq[:, 0:1024], func=AF.Silu)
                s.activation(out=xts[:, 1024:2048], in_=pg[:, 0:1024], func=AF.Silu)
        with block() as b:
            @b.tensor
            def _(tt):
                for m in range(8):
                    dst = pq[:, (m % 4) * TB:(m % 4 + 1) * TB] if m < 4                         else pg[:, (m % 4) * TB:(m % 4 + 1) * TB]
                    for k in range(8):
                        tt.matmul(dst, uww[:, k * 1024 + m * 128:k * 1024 + (m + 1) * 128],
                                  ytfm[:, k * TB:(k + 1) * TB],
                                  start=(k == 0), stop=(k == 7))
        with block() as b:
            @b.vector
            def _(v):
                v.tensor_tensor(out=hstb[:, 0:1024], in0=xts[:, 0:1024], in1=pq[:, 0:1024],
                                op=ALU.mult)
                v.tensor_tensor(out=hstb[:, 1024:2048], in0=xts[:, 1024:2048],
                                in1=pg[:, 0:1024], op=ALU.mult)
        with block() as b:
            @b.tensor
            def _(tt):
                tt.wait_ge(gq.h, sh_d_tgt)
                for m in range(8):
                    for k in range(8):
                        dw = DWRING[1] if k < 4 else DWRING[2]
                        tt.matmul(pf[:, m * TB:(m + 1) * TB],
                                  dw[:, (k % 4) * 1024 + m * 128:(k % 4) * 1024 + (m + 1) * 128],
                                  hstb[:, k * TB:(k + 1) * TB],
                                  start=False, stop=(k == 7))
        with block() as b:
            @b.scalar
            def _(s):
                s.activation(out=hstb[:, :], in_=pf[:, 0:2048], func=AF.Silu)
        if "dbg_g" in dbg:
            with block() as b:
                @b.gpsimd
                def _(g, tok0=tok0):
                    gq.go(g, dbg["dbg_g"][:, :].rearrange("p (k q) -> p k q", k=8)
                          [:, :, tok0:tok0 + TB],
                          hstb[:, :].rearrange("p (k q) -> p k q", k=8))
                    gq.wait(g)
        with block() as b:
            @b.vector
            def _(v):
                v.tensor_tensor(out=ytmp2[:, :], in0=yrev[:, :], in1=hstb[:, :], op=ALU.mult)
        with block() as b:
            @b.vector
            def _(v):
                for k in range(8):
                    v.tensor_tensor(out=xpart(k)[:, 0:TB], in0=xpart(k)[:, 0:TB],
                                    in1=ytmp2[:, k * TB:(k + 1) * TB], op=ALU.add)
        with block() as b:
            @b.gpsimd
            def _(g, tok0=tok0):
                for k in range(8):
                    gq.go(g, x1d.ap().rearrange("p (k q) -> p k q", k=8)
                          [:, k, tok0:tok0 + TB], xpart(k)[:, 0:TB])
                gq.wait(g)
        if "dbg_x1" in dbg:
            with block() as b:
                @b.gpsimd
                def _(g, tok0=tok0):
                    for k in range(8):
                        gq.go(g, dbg["dbg_x1"][:, :].rearrange("p (k q) -> p k q", k=8)
                              [:, k, tok0:tok0 + TB], xpart(k)[:, 0:TB])
                    gq.wait(g)
        with block() as b:
            @b.vector
            def _(v):
                for k in range(8):
                    v.tensor_copy(out=expsb[:, k * TB:(k + 1) * TB], in_=xpart(k)[:, 0:TB])
            @b.scalar
            def _(s):
                for k in range(8):
                    s.activation(out=hstb[:, k * TB:(k + 1) * TB], in_=xpart(k)[:, 0:TB],
                                 func=AF.Square)
        with block() as b:
            @b.tensor
            def _(tt):
                for k in range(8):
                    tt.matmul(pg[0:1, 0:TB], onesb[:, 0:1], expsb[:, k * TB:(k + 1) * TB],
                              start=(k == 0), stop=(k == 7))
                for k in range(8):
                    tt.matmul(pg[0:1, 512:512 + TB], onesb[:, 0:1],
                              hstb[:, k * TB:(k + 1) * TB],
                              start=(k == 0), stop=(k == 7))
        with block() as b:
            @b.vector
            def _(v):
                v.tensor_scalar(out=mrow[0:1, 0:TB], in0=pg[0:1, 0:TB], scalar1=1.0 / 1024,
                                scalar2=None, op0=ALU.mult)
                v.tensor_scalar(out=e2row[0:1, 0:TB], in0=pg[0:1, 512:512 + TB],
                                scalar1=1.0 / 1024, scalar2=None, op0=ALU.mult)
        with block() as b:
            @b.vector
            def _(v):
                v.tensor_tensor(out=vrow[0:1, 0:TB], in0=mrow[0:1, 0:TB],
                                in1=mrow[0:1, 0:TB], op=ALU.mult)
        with block() as b:
            @b.vector
            def _(v):
                v.tensor_tensor(out=vrow2[0:1, 0:TB], in0=e2row[0:1, 0:TB],
                                in1=vrow[0:1, 0:TB], op=ALU.subtract)
        with block() as b:
            @b.scalar
            def _(s):
                s.activation(out=vrow[0:1, 0:TB], in_=vrow2[0:1, 0:TB], func=AF.Ln,
                             bias=epsc[0:1, 0:1])
        with block() as b:
            @b.scalar
            def _(s):
                s.activation(out=rstdr[0:1, 0:TB], in_=vrow[0:1, 0:TB], func=AF.Exp,
                             scale=-0.5)
        with block() as b:
            @b.tensor
            def _(tt):
                tt.matmul(pq[:, 0:TB], onesf[0:1, :], mrow[0:1, 0:TB], start=True, stop=True)
                tt.matmul(pq[:, 512:512 + TB], onesf[0:1, :], rstdr[0:1, 0:TB],
                          start=True, stop=True)
        with block() as b:
            @b.vector
            def _(v):
                for k in range(8):
                    v.tensor_tensor(out=difsb[:, k * TB:(k + 1) * TB],
                                    in0=xpart(k)[:, 0:TB], in1=pq[:, 0:TB], op=ALU.subtract)
        with block() as b:
            @b.vector
            def _(v):
                for k in range(8):
                    v.tensor_tensor(out=ytfm[:, k * TB:(k + 1) * TB],
                                    in0=difsb[:, k * TB:(k + 1) * TB],
                                    in1=pq[:, 512:512 + TB], op=ALU.mult)
        with block() as b:
            @b.gpsimd
            def _(g, tok0=tok0):
                gq.go(g, xn2d.ap().rearrange("p (k q) -> p k q", k=8)
                      [:, :, tok0:tok0 + TB],
                      ytfm[:, :].rearrange("p (k q) -> p k q", k=8))
                gq.wait(g)
        if "dbg_xn2" in dbg:
            with block() as b:
                @b.gpsimd
                def _(g, tok0=tok0):
                    gq.go(g, dbg["dbg_xn2"][:, :].rearrange("p (k q) -> p k q", k=8)
                          [:, :, tok0:tok0 + TB],
                          ytfm[:, :].rearrange("p (k q) -> p k q", k=8))
                    gq.wait(g)
        with block() as b:
            @b.vector
            def _(v):
                v.tensor_reduce(out=rstdc[:, :], in_=ytfm[:, :], axis=AX, op=ALU.max,
                                apply_absolute_value=True)
        with block() as b:
            @b.vector
            def _(v):
                v.tensor_tensor(out=rmscol[:, :], in0=rmscol[:, :], in1=rstdc[:, :],
                                op=ALU.max)

    itile = ytmp[:, :].bitcast(I32)
    with block() as b:
        @b.gpsimd
        def _(g):
            gq.go(g, gww[:, :].rearrange("p (k f) -> p k f", k=8),
                  wvb.ap().rearrange("(k p) f -> p k f", p=128))
            g.memset(edec[:, 0:8], 0.0)
            g.memset(lendr[:, :], 0.0)
            gq.wait(g)
    for ti in range(8):
        tok0 = ti * 256
        with block() as b:
            @b.gpsimd
            def _(g, tok0=tok0):
                gq.go(g, ytfm[:, :].rearrange("p (k q) -> p k q", k=8),
                      xn2d.ap().rearrange("p (k q) -> p k q", k=8)[:, :, tok0:tok0 + 256])
                gq.wait(g)
        with block() as b:
            @b.tensor
            def _(tt):
                for m in range(8):
                    dst = pq[:, (m % 4) * 256:(m % 4 + 1) * 256] if m < 4 \
                        else pg[:, (m % 4) * 256:(m % 4 + 1) * 256]
                    for k in range(8):
                        tt.matmul(dst, gww[:, k * 1024 + m * 128:k * 1024 + (m + 1) * 128],
                                  ytfm[:, k * 256:(k + 1) * 256],
                                  start=(k == 0), stop=(k == 7))
        with block() as b:
            @b.vector
            def _(v):
                for m in range(8):
                    src = pq[:, (m % 4) * 256:(m % 4 + 1) * 256] if m < 4 \
                        else pg[:, (m % 4) * 256:(m % 4 + 1) * 256]
                    v.tensor_reduce(out=difsb[:, 24 + m:25 + m], in_=src, axis=AX,
                                    op=ALU.add)
        with block() as b:
            @b.vector
            def _(v):
                for m in range(8):
                    v.tensor_tensor(out=edec[:, m:m + 1], in0=edec[:, m:m + 1],
                                    in1=difsb[:, 24 + m:25 + m], op=ALU.add)
    with block() as b:
        @b.gpsimd
        def _(g):
            gq.go(g, vsin[:, :], edec[:, 0:8])
            gq.wait(g)
            g.collective_compute(
                "AllReduce", ALU.add,
                replica_groups=[[0, 1], [2, 3], [4, 5], [6, 7]],
                ins=[vsin.ap().opt()], outs=[vsout.ap().opt()],
            ).then_inc(ccs)
            cc_n[0] += 1
            g.wait_ge(ccs, cc_n[0])
            gq.go(g, edec[:, 8:16], vsout[:, :])
            gq.wait(g)
    with block() as b:
        @b.vector
        def _(v):
            v.tensor_copy(out=ffac[:, 0:8], in_=edec[:, 8:16])
    with block() as b:
        @b.gpsimd
        def _(g):
            gq.go(g, gww[:, :].rearrange("p (k f) -> p k f", k=8),
                  wob.ap().rearrange("(k p) f -> p k f", p=128))
            gq.wait(g)
    with block() as b:
        @b.tensor
        def _(tt):
            for m in range(8):
                for k in range(8):
                    tt.matmul(pq[:, m:m + 1],
                              gww[:, k * 1024 + m * 128:k * 1024 + (m + 1) * 128],
                              ffac[:, k:k + 1], start=(k == 0), stop=(k == 7))
    with block() as b:
        @b.vector
        def _(v):
            v.tensor_copy(out=ecolT[:, 0:8], in_=pq[:, 0:8])
    if "dbg_s1" in dbg:
        with block() as b:
            @b.gpsimd
            def _(g):
                gq.go(g, dbg["dbg_s1"][:, :], ecolT[:, 0:8])
                gq.wait(g)
    with block() as b:
        @b.tensor
        def _(tt):
            tt.transpose(pg[0:1, 0:128], rmscol[:, :], idf[:, :])
    with block() as b:
        @b.vector
        def _(v):
            v.tensor_copy(out=lendr[0:1, 0:16].rearrange("a (b q) -> a b q", b=1)
                          if False else lendE[0:1, 0:32], in_=pg[0:1, 0:32])
    with block() as b:
        @b.vector
        def _(v):
            v.tensor_copy(out=dtraw[0:1, 0:128], in_=pg[0:1, 0:128])
    with block() as b:
        @b.vector
        def _(v):
            v.tensor_reduce(out=lendr[0:1, 0:1], in_=dtraw[0:1, 0:128], axis=AX, op=ALU.max)
    with block() as b:
        @b.gpsimd
        def _(g):
            g.memset(dtraw[0:1, 128:144], 0.0)
    with block() as b:
        @b.vector
        def _(v):
            v.tensor_copy(out=dtraw[0:1, 128:129], in_=lendr[0:1, 0:1])
    with block() as b:
        @b.gpsimd
        def _(g):
            gq.go(g, armx[0:1, 0:16], dtraw[0:1, 128:144])
            gq.wait(g)
            g.collective_compute(
                "AllReduce", ALU.max, replica_groups=[list(range(8))],
                ins=[armx.ap().opt()], outs=[armxo.ap().opt()],
            ).then_inc(ccs)
            cc_n[0] += 1
            g.wait_ge(ccs, cc_n[0])
            gq.go(g, lendr[0:1, 0:16], armxo[0:1, 0:16])
            gq.wait(g)
    def make_scale(src_ap, dst_off):
        with block() as b:
            @b.vector
            def _(v):
                v.tensor_scalar(out=lendE[0:1, dst_off:dst_off + 1], in0=src_ap,
                                scalar1=1.0 / 32767.0, scalar2=1e-12,
                                op0=ALU.mult, op1=ALU.max)
        with block() as b:
            @b.vector
            def _(v):
                v.reciprocal(out=lendE[0:1, dst_off + 1:dst_off + 2],
                             in_=lendE[0:1, dst_off:dst_off + 1])
    make_scale(lendr[0:1, 0:1], 0)
    with block() as b:
        @b.tensor
        def _(tt):
            tt.matmul(pq[:, 0:2], onesf[0:1, :], lendE[0:1, 0:2], start=True, stop=True)
    with block() as b:
        @b.vector
        def _(v):
            v.tensor_copy(out=ftmp[:, 0:2], in_=pq[:, 0:2])

    def dequant(src_rows, n_iter, dstT, dst_row_of, dst_col_of):
        wring = [difsb[:, 0:1024], difsb[:, 1024:2048], ytok[:, 0:1024],
                 ytok[:, 1024:2048], hstf[:, 0:1024]]
        mxring = [gtile[:, 0:32], gtile[:, 32:64], gtile[:, 64:96]]
        scring = [ytmp[:, 1120:1152], ytmp[:, 1152:1184], ytmp[:, 1184:1216]]
        ivring = [ytmp[:, 1024:1056], ytmp[:, 1056:1088], ytmp[:, 1088:1120]]
        wqring = [ytfm[:, 0:1024], ytfm[:, 1024:2048]]
        evring = [ygat[:, 0:1024], ygat[:, 1024:2048]]
        iscr = ytmp[:, :].bitcast(I32)[:, 0:1024]
        for j in range(n_iter + 8):
            with block() as b:
                @b.gpsimd
                def _(g, j=j):
                    if j < n_iter:
                        gq.go(g, wring[j % 5], src_rows(j))
                    i = j - 7
                    if 0 <= i < n_iter:
                        r0 = dst_row_of(i, 0)
                        gq.go(g, dstT[r0:r0 + 1024,
                                      dst_col_of(i):dst_col_of(i) + 128]
                              .rearrange("(kc p) c -> p kc c", p=128),
                              evring[i % 2][:, :].rearrange("p (kc q) -> p kc q", kc=8))
                    gq.wait(g)
                @b.vector
                def _(v, j=j):
                    i = j - 1
                    if 0 <= i < n_iter:
                        v.tensor_reduce(
                            out=mxring[i % 3].rearrange("p (a q) -> p a q", q=1),
                            in_=wring[i % 5].rearrange("p (a q) -> p a q", q=32),
                            axis=AX, op=ALU.max, apply_absolute_value=True)
                    i = j - 2
                    if 0 <= i < n_iter:
                        v.tensor_scalar(out=scring[i % 3], in0=mxring[i % 3],
                                        scalar1=1.0 / 7.0, scalar2=1e-12,
                                        op0=ALU.mult, op1=ALU.max)
                    i = j - 3
                    if 0 <= i < n_iter:
                        v.reciprocal(out=ivring[i % 3], in_=scring[i % 3])
                    i = j - 4
                    if 0 <= i < n_iter:
                        w = wring[i % 5]
                        v.tensor_tensor(
                            out=w.rearrange("p (a q) -> p a q", q=32),
                            in0=w.rearrange("p (a q) -> p a q", q=32),
                            in1=ivring[i % 3].rearrange("p a -> p a ()")
                            .broadcast_to((128, 32, 32)), op=ALU.mult)
                        v.tensor_copy(out=iscr, in_=w)
                        v.tensor_copy(out=w, in_=iscr)
                        v.tensor_tensor(
                            out=wqring[i % 2].rearrange("p (a q) -> p a q", q=32),
                            in0=w.rearrange("p (a q) -> p a q", q=32),
                            in1=scring[i % 3].rearrange("p a -> p a ()")
                            .broadcast_to((128, 32, 32)), op=ALU.mult)
                    i = j - 6
                    if 0 <= i < n_iter:
                        v.tensor_copy(out=evring[i % 2],
                                      in_=pgb[:, (i % 2) * 1024:(i % 2) * 1024 + 1024])
                @b.tensor
                def _(tt, j=j):
                    i = j - 5
                    if 0 <= i < n_iter:
                        for kc in range(8):
                            tt.transpose(pgb[:, (i % 2) * 1024 + kc * 128:
                                             (i % 2) * 1024 + (kc + 1) * 128],
                                         wqring[i % 2][:, kc * 128:(kc + 1) * 128],
                                         idb[:, :])

    dequant(lambda r: d["wup"].ap()[r * 128:(r + 1) * 128, :], 24, wuqT,
            lambda r, kc: kc * 128, lambda r: r * 128)
    dequant(lambda r: d["wgate"].ap()[r * 128:(r + 1) * 128, :], 24, wgqT,
            lambda r, kc: kc * 128, lambda r: r * 128)
    dequant(lambda r: d["wdown"].ap()[(r // 3) * 128:(r // 3 + 1) * 128,
                                      (r % 3) * 1024:(r % 3 + 1) * 1024], 24, wdqT,
            lambda r, kc: (r % 3) * 1024 + kc * 128, lambda r: (r // 3) * 128)

    with block() as b:
        @b.gpsimd
        def _(g):
            g.memset(ftmp2[:, 0:4], 0.0)
    for ti in range(8):
        tok0 = ti * 256
        with block() as b:
            @b.gpsimd
            def _(g, tok0=tok0):
                gq.go(g, ytfm[:, :].rearrange("p (k q) -> p k q", k=8),
                      xn2d.ap().rearrange("p (k q) -> p k q", k=8)[:, :, tok0:tok0 + 256])
                gq.wait(g)
        with block() as b:
            @b.vector
            def _(v):
                v.tensor_scalar(out=difsb[:, 0:2048], in0=ytfm[:, :],
                                scalar1=ftmp[:, 1:2], scalar2=None, op0=ALU.mult)
                v.tensor_copy(out=itile[:, 0:2048], in_=difsb[:, 0:2048])
                v.tensor_copy(out=difsb[:, 0:2048], in_=itile[:, 0:2048])
                v.tensor_scalar(out=ytfm[:, :], in0=difsb[:, 0:2048],
                                scalar1=ftmp[:, 0:1], scalar2=None, op0=ALU.mult)
        steps = [(srcT, dst_t, mg)
                 for srcT, dst_t in [(wuqT, xbcraw), (wgqT, siluz)] for mg in range(6)]
        wring2 = [wslice[:, 0:4096], wslice[:, 4096:8192]]
        for j in range(len(steps) + 2):
            with block() as b:
                @b.gpsimd
                def _(g, j=j):
                    if j < len(steps):
                        srcT, _, mg = steps[j]
                        gq.go(g, wring2[j % 2].rearrange("p (k c) -> p k c", k=8),
                              srcT.ap().rearrange("(k p) c -> p k c", p=128)
                              [:, :, mg * 512:(mg + 1) * 512])
                    gq.wait(g)
                @b.tensor
                def _(tt, j=j):
                    i = j - 1
                    if 0 <= i < len(steps):
                        ps_ = pq if i % 2 == 0 else pg
                        wr = wring2[i % 2]
                        for mi in range(4):
                            for k in range(8):
                                tt.matmul(ps_[:, mi * 256:(mi + 1) * 256],
                                          wr[:, k * 512 + mi * 128:k * 512 + (mi + 1) * 128],
                                          ytfm[:, k * 256:(k + 1) * 256],
                                          start=(k == 0), stop=(k == 7))
                @b.vector
                def _(v, j=j):
                    i = j - 2
                    if 0 <= i < len(steps):
                        _, dst_t, mg = steps[i]
                        ps_ = pq if i % 2 == 0 else pg
                        v.tensor_copy(out=dst_t[:, mg * 1024:(mg + 1) * 1024],
                                      in_=ps_[:, 0:1024])
        if "dbg_upr" in dbg and ti == 0:
            with block() as b:
                @b.gpsimd
                def _(g):
                    gq.go(g, dbg["dbg_upr"][:, 0:6144], xbcraw[:, 0:6144])
                    gq.wait(g)
        with block() as b:
            @b.vector
            def _(v):
                v.tensor_reduce(out=difsb[:, 24:25], in_=xbcraw[:, 0:6144], axis=AX,
                                op=ALU.max, apply_absolute_value=True)
                v.tensor_reduce(out=difsb[:, 25:26], in_=siluz[:, 0:6144], axis=AX,
                                op=ALU.max, apply_absolute_value=True)
        with block() as b:
            @b.vector
            def _(v):
                v.tensor_tensor(out=ftmp2[:, 0:2], in0=ftmp2[:, 0:2],
                                in1=difsb[:, 24:26], op=ALU.max)
        with block() as b:
            @b.gpsimd
            def _(g, tok0=tok0):
                gq.go(g, upraw.ap().rearrange("p (m q) -> p m q", m=24)
                      [:, :, tok0:tok0 + 256],
                      xbcraw[:, 0:6144].rearrange("p (m q) -> p m q", m=24))
                gq.go(g, gateraw.ap().rearrange("p (m q) -> p m q", m=24)
                      [:, :, tok0:tok0 + 256],
                      siluz[:, 0:6144].rearrange("p (m q) -> p m q", m=24))
                gq.wait(g)
    with block() as b:
        @b.tensor
        def _(tt):
            tt.transpose(pg[0:2, 0:128], ftmp2[:, 0:2], idf[:, :])
    with block() as b:
        @b.vector
        def _(v):
            v.tensor_copy(out=dtraw[0:2, 0:128], in_=pg[0:2, 0:128])
    with block() as b:
        @b.vector
        def _(v):
            v.tensor_reduce(out=dtraw[0:2, 200:201], in_=dtraw[0:2, 0:128], axis=AX,
                            op=ALU.max)
    with block() as b:
        @b.gpsimd
        def _(g):
            g.memset(dtraw[0:1, 128:144], 0.0)
    with block() as b:
        @b.gpsimd
        def _(g):
            gq.go(g, armx[0:1, 0:16], dtraw[0:1, 128:144])
            gq.wait(g)
            gq.go(g, armx[0:1, 0:1], dtraw[0:1, 200:201])
            gq.go(g, armx[0:1, 1:2], dtraw[1:2, 200:201])
            gq.wait(g)
            g.collective_compute(
                "AllReduce", ALU.max, replica_groups=[list(range(8))],
                ins=[armx.ap().opt()], outs=[armxo.ap().opt()],
            ).then_inc(ccs)
            cc_n[0] += 1
            g.wait_ge(ccs, cc_n[0])
            gq.go(g, lendr[0:1, 8:10], armxo[0:1, 0:2])
            gq.wait(g)
    make_scale(lendr[0:1, 8:9], 4)
    make_scale(lendr[0:1, 9:10], 6)
    with block() as b:
        @b.tensor
        def _(tt):
            tt.matmul(pq[:, 4:8], onesf[0:1, :], lendE[0:1, 4:8], start=True, stop=True)
    with block() as b:
        @b.vector
        def _(v):
            v.tensor_copy(out=ftmp[:, 4:8], in_=pq[:, 4:8])
    with block() as b:
        @b.gpsimd
        def _(g):
            g.memset(ftmp2[:, 2:4], 0.0)
    for ti in range(8):
        tok0 = ti * 256
        with block() as b:
            @b.gpsimd
            def _(g, tok0=tok0):
                gq.go(g, siluz[:, 0:6144].rearrange("p (m q) -> p m q", m=24),
                      gateraw.ap().rearrange("p (m q) -> p m q", m=24)
                      [:, :, tok0:tok0 + 256])
                gq.wait(g)
        for ch in range(3):
            co = ch * 2048
            with block() as b:
                @b.vector
                def _(v, co=co):
                    v.tensor_scalar(out=difsb[:, 0:2048], in0=siluz[:, co:co + 2048],
                                    scalar1=ftmp[:, 7:8], scalar2=None, op0=ALU.mult)
                    v.tensor_copy(out=itile[:, 0:2048], in_=difsb[:, 0:2048])
                    v.tensor_copy(out=difsb[:, 0:2048], in_=itile[:, 0:2048])
                    v.tensor_scalar(out=siluz[:, co:co + 2048], in0=difsb[:, 0:2048],
                                    scalar1=ftmp[:, 6:7], scalar2=None, op0=ALU.mult)
            with block() as b:
                @b.scalar
                def _(sx, co=co):
                    sx.activation(out=difsb[:, 0:2048], in_=siluz[:, co:co + 2048],
                                  func=AF.Sigmoid)
            with block() as b:
                @b.vector
                def _(v, co=co):
                    v.tensor_scalar(out=ytok[:, 0:2048], in0=difsb[:, 0:2048],
                                    scalar1=65536.0, scalar2=None, op0=ALU.mult)
                    v.tensor_copy(out=itile[:, 0:2048], in_=ytok[:, 0:2048])
                    v.tensor_copy(out=ytok[:, 0:2048], in_=itile[:, 0:2048])
                    v.tensor_scalar(out=ytok[:, 0:2048], in0=ytok[:, 0:2048],
                                    scalar1=65535.0, scalar2=1.0 / 65536.0,
                                    op0=ALU.min, op1=ALU.mult)
                    v.tensor_tensor(out=siluz[:, co:co + 2048], in0=siluz[:, co:co + 2048],
                                    in1=ytok[:, 0:2048], op=ALU.mult)
        with block() as b:
            @b.vector
            def _(v):
                v.tensor_reduce(out=difsb[:, 24:25], in_=siluz[:, 0:6144], axis=AX,
                                op=ALU.max, apply_absolute_value=True)
        with block() as b:
            @b.vector
            def _(v):
                v.tensor_tensor(out=ftmp2[:, 2:3], in0=ftmp2[:, 2:3],
                                in1=difsb[:, 24:25], op=ALU.max)
        with block() as b:
            @b.gpsimd
            def _(g, tok0=tok0):
                gq.go(g, gate2d.ap().rearrange("p (m q) -> p m q", m=24)
                      [:, :, tok0:tok0 + 256],
                      siluz[:, 0:6144].rearrange("p (m q) -> p m q", m=24))
                gq.wait(g)
    with block() as b:
        @b.tensor
        def _(tt):
            tt.transpose(pg[0:1, 0:128], ftmp2[:, 2:3], idf[:, :])
    with block() as b:
        @b.vector
        def _(v):
            v.tensor_copy(out=dtraw[0:1, 0:128], in_=pg[0:1, 0:128])
    with block() as b:
        @b.vector
        def _(v):
            v.tensor_reduce(out=lendr[0:1, 4:5], in_=dtraw[0:1, 0:128], axis=AX, op=ALU.max)
    with block() as b:
        @b.gpsimd
        def _(g):
            g.memset(dtraw[0:1, 128:144], 0.0)
    with block() as b:
        @b.vector
        def _(v):
            v.tensor_copy(out=dtraw[0:1, 128:129], in_=lendr[0:1, 4:5])
    with block() as b:
        @b.gpsimd
        def _(g):
            gq.go(g, armx[0:1, 0:16], dtraw[0:1, 128:144])
            gq.wait(g)
            g.collective_compute(
                "AllReduce", ALU.max, replica_groups=[list(range(8))],
                ins=[armx.ap().opt()], outs=[armxo.ap().opt()],
            ).then_inc(ccs)
            cc_n[0] += 1
            g.wait_ge(ccs, cc_n[0])
            gq.go(g, lendr[0:1, 10:11], armxo[0:1, 0:1])
            gq.wait(g)
    make_scale(lendr[0:1, 10:11], 8)
    with block() as b:
        @b.tensor
        def _(tt):
            tt.matmul(pq[:, 8:10], onesf[0:1, :], lendE[0:1, 8:10], start=True, stop=True)
    with block() as b:
        @b.vector
        def _(v):
            v.tensor_copy(out=ftmp[:, 8:10], in_=pq[:, 8:10])
    with block() as b:
        @b.gpsimd
        def _(g):
            g.memset(ftmp2[:, 3:4], 0.0)
    for ti in range(8):
        tok0 = ti * 256
        with block() as b:
            @b.gpsimd
            def _(g, tok0=tok0):
                gq.go(g, siluz[:, 0:6144].rearrange("p (m q) -> p m q", m=24),
                      gate2d.ap().rearrange("p (m q) -> p m q", m=24)
                      [:, :, tok0:tok0 + 256])
                gq.go(g, xbcraw[:, 0:6144].rearrange("p (m q) -> p m q", m=24),
                      upraw.ap().rearrange("p (m q) -> p m q", m=24)
                      [:, :, tok0:tok0 + 256])
                gq.wait(g)
        for ch in range(3):
            co = ch * 2048
            with block() as b:
                @b.vector
                def _(v, co=co):
                    v.tensor_scalar(out=difsb[:, 0:2048], in0=siluz[:, co:co + 2048],
                                    scalar1=ftmp[:, 9:10], scalar2=None, op0=ALU.mult)
                    v.tensor_copy(out=itile[:, 0:2048], in_=difsb[:, 0:2048])
                    v.tensor_copy(out=difsb[:, 0:2048], in_=itile[:, 0:2048])
                    v.tensor_scalar(out=siluz[:, co:co + 2048], in0=difsb[:, 0:2048],
                                    scalar1=ftmp[:, 8:9], scalar2=None, op0=ALU.mult)
                    v.tensor_scalar(out=difsb[:, 0:2048], in0=xbcraw[:, co:co + 2048],
                                    scalar1=ftmp[:, 5:6], scalar2=None, op0=ALU.mult)
                    v.tensor_copy(out=itile[:, 0:2048], in_=difsb[:, 0:2048])
                    v.tensor_copy(out=difsb[:, 0:2048], in_=itile[:, 0:2048])
                    v.tensor_scalar(out=xbcraw[:, co:co + 2048], in0=difsb[:, 0:2048],
                                    scalar1=ftmp[:, 4:5], scalar2=None, op0=ALU.mult)
                    v.tensor_tensor(out=siluz[:, co:co + 2048], in0=siluz[:, co:co + 2048],
                                    in1=xbcraw[:, co:co + 2048], op=ALU.mult)
        with block() as b:
            @b.vector
            def _(v):
                v.tensor_reduce(out=difsb[:, 24:25], in_=siluz[:, 0:6144], axis=AX,
                                op=ALU.max, apply_absolute_value=True)
        with block() as b:
            @b.vector
            def _(v):
                v.tensor_tensor(out=ftmp2[:, 3:4], in0=ftmp2[:, 3:4],
                                in1=difsb[:, 24:25], op=ALU.max)
        with block() as b:
            @b.gpsimd
            def _(g, tok0=tok0):
                gq.go(g, od.ap().rearrange("p (m q) -> p m q", m=24)
                      [:, :, tok0:tok0 + 256],
                      siluz[:, 0:6144].rearrange("p (m q) -> p m q", m=24))
                gq.wait(g)
    with block() as b:
        @b.tensor
        def _(tt):
            tt.transpose(pg[0:1, 0:128], ftmp2[:, 3:4], idf[:, :])
    with block() as b:
        @b.vector
        def _(v):
            v.tensor_copy(out=dtraw[0:1, 0:128], in_=pg[0:1, 0:128])
    with block() as b:
        @b.vector
        def _(v):
            v.tensor_reduce(out=lendr[0:1, 5:6], in_=dtraw[0:1, 0:128], axis=AX, op=ALU.max)
    with block() as b:
        @b.gpsimd
        def _(g):
            g.memset(dtraw[0:1, 128:144], 0.0)
    with block() as b:
        @b.vector
        def _(v):
            v.tensor_copy(out=dtraw[0:1, 128:129], in_=lendr[0:1, 5:6])
    with block() as b:
        @b.gpsimd
        def _(g):
            gq.go(g, armx[0:1, 0:16], dtraw[0:1, 128:144])
            gq.wait(g)
            g.collective_compute(
                "AllReduce", ALU.max, replica_groups=[list(range(8))],
                ins=[armx.ap().opt()], outs=[armxo.ap().opt()],
            ).then_inc(ccs)
            cc_n[0] += 1
            g.wait_ge(ccs, cc_n[0])
            gq.go(g, lendr[0:1, 11:12], armxo[0:1, 0:1])
            gq.wait(g)
    make_scale(lendr[0:1, 11:12], 10)
    with block() as b:
        @b.tensor
        def _(tt):
            tt.matmul(pq[:, 10:12], onesf[0:1, :], lendE[0:1, 10:12], start=True, stop=True)
    with block() as b:
        @b.vector
        def _(v):
            v.tensor_copy(out=ftmp[:, 10:12], in_=pq[:, 10:12])
    if False:
        pass
    for ti in range(8):
        tok0 = ti * 256
        with block() as b:
            @b.gpsimd
            def _(g, tok0=tok0):
                gq.go(g, siluz[:, 0:6144].rearrange("p (m q) -> p m q", m=24),
                      od.ap().rearrange("p (m q) -> p m q", m=24)[:, :, tok0:tok0 + 256])
                gq.wait(g)
        if "dbg_xbc" in dbg and ti == 0:
            with block() as b:
                @b.gpsimd
                def _(g):
                    gq.go(g, dbg["dbg_xbc"][:, 0:6144], siluz[:, 0:6144])
                    gq.wait(g)
        for ch in range(3):
            co = ch * 2048
            with block() as b:
                @b.vector
                def _(v, co=co):
                    v.tensor_scalar(out=difsb[:, 0:2048], in0=siluz[:, co:co + 2048],
                                    scalar1=ftmp[:, 11:12], scalar2=None, op0=ALU.mult)
                    v.tensor_copy(out=itile[:, 0:2048], in_=difsb[:, 0:2048])
                    v.tensor_copy(out=difsb[:, 0:2048], in_=itile[:, 0:2048])
                    v.tensor_scalar(out=siluz[:, co:co + 2048], in0=difsb[:, 0:2048],
                                    scalar1=ftmp[:, 10:11], scalar2=None, op0=ALU.mult)
        for kg in range(3):
            with block() as b:
                @b.gpsimd
                def _(g, kg=kg):
                    gq.go(g, xbcraw[:, 0:8192].rearrange("p (k f) -> p k f", k=8),
                          wdqT.ap().rearrange("(k p) f -> p k f", p=128)
                          [:, kg * 8:(kg + 1) * 8, :])
                    gq.wait(g)
            with block() as b:
                @b.tensor
                def _(tt, kg=kg):
                    for m in range(8):
                        dst = pq[:, (m % 4) * 256:(m % 4 + 1) * 256] if m < 4 \
                            else pg[:, (m % 4) * 256:(m % 4 + 1) * 256]
                        for k in range(8):
                            tt.matmul(dst,
                                      xbcraw[:, k * 1024 + m * 128:k * 1024 + (m + 1) * 128],
                                      siluz[:, (kg * 8 + k) * 256:(kg * 8 + k + 1) * 256],
                                      start=(kg == 0 and k == 0 and m % 2 == 0),
                                      stop=(kg == 2 and k == 7))
        with block() as b:
            @b.scalar
            def _(sx):
                sx.activation(out=hstb[:, 0:1024], in_=pq[:, 0:1024], func=AF.Silu)
                sx.activation(out=hstb[:, 1024:2048], in_=pg[:, 0:1024], func=AF.Silu)
        if "dbg_ffn" in dbg:
            with block() as b:
                @b.vector
                def _(v):
                    v.tensor_copy(out=ytok[:, 0:1024], in_=pq[:, 0:1024])
                    v.tensor_copy(out=ytok[:, 1024:2048], in_=pg[:, 0:1024])
            with block() as b:
                @b.gpsimd
                def _(g, tok0=tok0):
                    gq.go(g, dbg["dbg_ffn"][:, :].rearrange("p (k q) -> p k q", k=8)
                          [:, :, tok0:tok0 + 256],
                          ytok[:, 0:2048].rearrange("p (k q) -> p k q", k=8))
                    gq.wait(g)
        with block() as b:
            @b.gpsimd
            def _(g, tok0=tok0):
                gq.go(g, difsb[:, 0:2048].rearrange("p (k q) -> p k q", k=8),
                      x1d.ap().rearrange("p (k q) -> p k q", k=8)[:, :, tok0:tok0 + 256])
                gq.wait(g)
        with block() as b:
            @b.vector
            def _(v):
                for k in range(8):
                    v.tensor_scalar(out=ygat[:, k * 256:(k + 1) * 256],
                                    in0=hstb[:, k * 256:(k + 1) * 256],
                                    scalar1=ecolT[:, k:k + 1], scalar2=None, op0=ALU.mult)
        with block() as b:
            @b.vector
            def _(v):
                v.tensor_tensor(out=difsb[:, 0:2048], in0=difsb[:, 0:2048],
                                in1=ygat[:, 0:2048], op=ALU.add)
        with block() as b:
            @b.gpsimd
            def _(g, tok0=tok0):
                gq.go(g, out.ap().rearrange("(k p) q -> p k q", p=128)
                      [:, :, tok0:tok0 + 256],
                      difsb[:, 0:2048].rearrange("p (k q) -> p k q", k=8))
                gq.wait(g)
    if "dbg_mrev" in dbg:
        with block() as b:
            @b.gpsimd
            def _(g):
                for rr in range(4):
                    gq.go(g, ytfm[:, 0:1024],
                          wdqT.ap()[rr * 128:(rr + 1) * 128, 0:1024])
                    gq.wait(g)
                    gq.go(g, dbg["dbg_mrev"][:, rr * 1024:(rr + 1) * 1024], ytfm[:, 0:1024])
                    gq.wait(g)
    with block() as b:
        @b.gpsimd
        def _(g):
            gq.wait(g)
        @b.sync
        def _(sx):
            sq.wait(sx)

    stack.close()
    return nc


def _stage(inputs):
    ident = np.eye(128, dtype=np.float32)
    jrev = ident[:, ::-1].copy()
    ii, jj = np.meshgrid(np.arange(128), np.arange(128), indexing="ij")
    maskneg = np.where(jj >= ii, 0.0, -1e30).astype(np.float32)
    tiew = np.tile(np.arange(16, 0, -1, dtype=np.float32), (128, 1)).copy()
    tiew4 = np.tile(np.arange(4, 0, -1, dtype=np.float32), (128, 1)).copy()

    x = inputs["L0_attn_hidden_states"]
    per_core = []
    for c in range(8):
        b, dd = c // 2, c % 2
        pre = "L0_attn_fwd_" if dd == 0 else "L0_attn_bwd_"
        xb = x[b] if dd == 0 else x[b][::-1]
        m = {
            "x": np.ascontiguousarray(xb.T),
            "win": inputs[pre + "W_in"],
            "wout": inputs[pre + "W_out"],
            "convw": np.ascontiguousarray(inputs[pre + "conv_w"].T),
            "convb": np.ascontiguousarray(inputs[pre + "conv_b"].reshape(18, 128).T),
            "dtbias": inputs[pre + "dt_bias"].reshape(32, 1),
            "alog": inputs[pre + "A_log"].reshape(32, 1),
            "dhead": np.ascontiguousarray(np.tile(inputs[pre + "D"], (128, 1))),
            "normw": np.ascontiguousarray(inputs[pre + "norm_w"].reshape(16, 128).T),
            "ident": ident,
            "maskneg": maskneg,
            "jrev": jrev,
            "rwT": np.ascontiguousarray(inputs["L0_ffn_router_weight"].T),
            "ebias": np.ascontiguousarray(np.tile(inputs["L0_ffn_e_bias"], (128, 1))),
            "tiew": tiew,
            "tiew4": tiew4,
            "gwT": np.ascontiguousarray(inputs["L0_ffn_gate_w"].transpose(0, 2, 1)),
            "uwT": np.ascontiguousarray(inputs["L0_ffn_up_w"].transpose(0, 2, 1)),
            "dwT": np.ascontiguousarray(inputs["L0_ffn_down_w"].transpose(0, 2, 1)),
            "shgT": np.ascontiguousarray(inputs["L0_ffn_sh_gate_w"].T),
            "shuT": np.ascontiguousarray(inputs["L0_ffn_sh_up_w"].T),
            "shdT": np.ascontiguousarray(inputs["L0_ffn_sh_down_w"].T),
            "wvT": np.ascontiguousarray(inputs["L1_attn_w_qkv"][2048:3072].T),
            "woT": np.ascontiguousarray(inputs["L1_attn_w_o"].T),
            "wgate": inputs["L1_ffn_w_gate"],
            "wup": inputs["L1_ffn_w_up"],
            "wdown": inputs["L1_ffn_w_down"],
        }
        per_core.append({k: np.ascontiguousarray(v, dtype=np.float32)
                         for k, v in m.items()})
    return per_core


def kernel(**inputs):
    per_core = _stage(inputs)
    nc = build()
    res = run_bass_kernel_spmd(nc, per_core, core_ids=list(range(8)))
    out = np.empty((4, L, D_MODEL), dtype=np.float32)
    for c in range(8):
        b, dd = c // 2, c % 2
        o = np.asarray(res.results[c]["out"])
        if dd == 0:
            out[b, 0:TLOC] = o.T
        else:
            out[b, TLOC:] = o.T[::-1]
    return out


# version 06
# speedup vs baseline: 1.3167x; 1.0025x over previous
import contextlib
import numpy as np

import concourse.bass as bass
import concourse.mybir as mybir
from concourse.bass_utils import run_bass_kernel_spmd

F32 = mybir.dt.float32
BF16 = mybir.dt.bfloat16
I32 = mybir.dt.int32
AF = mybir.ActivationFunctionType
ALU = mybir.AluOpType
AX = mybir.AxisListType.X

D_MODEL = 1024
D_INNER = 2048
D_IN_PROJ = 4384
NM_IN = 35
L = 4096
TLOC = 2048
TILE = 512
NT = L // TILE
EPS = 1e-5
CR = TILE + 3

DBG_NAMES = [
    ("dbg_xn", [128, 8 * TILE]), ("dbg_xbc", [128, 18 * TILE]),
    ("dbg_dt", [32, TILE]), ("dbg_y", [128, 2048]),
    ("dbg_mrev", [128, 8 * L]), ("dbg_mnat", [128, 8 * TLOC]),
    ("dbg_s", [128, 8 * TLOC]), ("dbg_x1", [128, 8 * TLOC]),
    ("dbg_xn2", [128, 8 * TLOC]), ("dbg_g", [128, 8 * TLOC]),
    ("dbg_scores", [128, 64]), ("dbg_comb", [128, 64]),
    ("dbg_s1", [128, 8]), ("dbg_ffn", [128, 8 * TLOC]),
    ("dbg_upr", [128, 24 * 256]),
]

MGROUPS = [list(range(g, min(g + 4, NM_IN))) for g in range(0, NM_IN, 4)]
WHALF = 8 * 4 * 128


def build(debug=()):
    nc = bass.Bass("TRN2", target_bir_lowering=False, debug=False)
    stack = contextlib.ExitStack()

    d = {}
    def din(name, shape):
        d[name] = nc.dram_tensor(name, list(shape), F32, kind="ExternalInput")

    din("x", [D_MODEL, L])
    din("win", [D_MODEL, D_IN_PROJ])
    din("wout", [D_INNER, D_MODEL])
    din("convw", [2304, 4])
    din("convb", [128, 18])
    din("dtbias", [32, 1])
    din("alog", [32, 1])
    din("dhead", [128, 32])
    din("normw", [128, 16])
    din("ident", [128, 128])
    din("maskneg", [128, 128])
    din("jrev", [128, 128])
    din("rwT", [D_MODEL, 16])
    din("ebias", [128, 16])
    din("tiew", [128, 16])
    din("tiew4", [128, 4])
    din("gwT", [16, D_MODEL, 512])
    din("uwT", [16, D_MODEL, 512])
    din("dwT", [16, 512, D_MODEL])
    din("shgT", [D_MODEL, D_MODEL])
    din("shuT", [D_MODEL, D_MODEL])
    din("shdT", [D_MODEL, D_MODEL])
    din("wvT", [D_MODEL, D_MODEL])
    din("woT", [D_MODEL, D_MODEL])
    din("wgate", [3072, D_MODEL])
    din("wup", [3072, D_MODEL])
    din("wdown", [D_MODEL, 3072])

    out = nc.dram_tensor("out", [D_MODEL, TLOC], F32, kind="ExternalOutput")
    dbg = {}
    for name, shape in DBG_NAMES:
        if name in debug:
            dbg[name] = nc.dram_tensor(name, shape, F32, kind="ExternalOutput")

    ellbuf = nc.dram_tensor("ellbuf", [4 * 32, 128], F32)
    arin = nc.dram_tensor("arin", [128, 8 * L], BF16)
    arout = nc.dram_tensor("arout", [128, 8 * L], BF16)
    mnat = nc.dram_tensor("mnat", [128, 8 * TLOC], BF16)
    gwb = nc.dram_tensor("gwb", [16, D_MODEL, 512], BF16)
    uwb = nc.dram_tensor("uwb", [16, D_MODEL, 512], BF16)
    dwb = nc.dram_tensor("dwb", [16, 512, D_MODEL], BF16)
    shb = nc.dram_tensor("shb", [3, D_MODEL, D_MODEL], BF16)
    wvb = nc.dram_tensor("wvb", [D_MODEL, D_MODEL], BF16)
    wob = nc.dram_tensor("wob", [D_MODEL, D_MODEL], BF16)
    combuf = nc.dram_tensor("combuf", [32, 128], F32)
    x1d = nc.dram_tensor("x1d", [128, 8 * TLOC], F32)
    xn2d = nc.dram_tensor("xn2d", [128, 8 * TLOC], BF16)
    armx = nc.dram_tensor("armx", [1, 16], F32)
    armxo = nc.dram_tensor("armxo", [1, 16], F32)
    vsin = nc.dram_tensor("vsin", [128, 8], F32)
    vsout = nc.dram_tensor("vsout", [128, 8], F32)
    wuqT = nc.dram_tensor("wuqT", [1024, 3072], BF16)
    wgqT = nc.dram_tensor("wgqT", [1024, 3072], BF16)
    wdqT = nc.dram_tensor("wdqT", [3072, 1024], BF16)
    upraw = nc.dram_tensor("upraw", [128, 24 * TLOC], BF16)
    gateraw = nc.dram_tensor("gateraw", [128, 24 * TLOC], BF16)
    gate2d = nc.dram_tensor("gate2d", [128, 24 * TLOC], BF16)
    od = nc.dram_tensor("od", [128, 24 * TLOC], BF16)

    sbuf = lambda name, shape, dt=F32: stack.enter_context(nc.sbuf_tensor(name, shape, dt))
    psum = lambda name, shape, dt=F32: stack.enter_context(nc.psum_tensor(name, shape, dt))

    wslice = sbuf("wslice", [128, 2 * WHALF], BF16)
    woutb = sbuf("woutb", [128, 16 * D_MODEL], BF16)
    xnb = sbuf("xnb", [128, 8 * TILE], BF16)
    siluz = sbuf("siluz", [128, 16 * TILE], BF16)
    xbcraw = sbuf("xbcraw", [128, 18 * CR], BF16)
    tailb = sbuf("tailb", [128, 18 * 3], BF16)
    dtraw = sbuf("dtraw", [32, TILE], F32)
    dtp = sbuf("dtp", [32, TILE], F32)
    dta = sbuf("dta", [32, TILE], F32)
    lndt = sbuf("lndt", [32, TILE], F32)
    ones32 = sbuf("ones32", [32, 128], F32)
    onec32 = sbuf("onec32", [32, 1], F32)
    ellf = sbuf("ellf", [32, 128], F32)
    lmef = sbuf("lmef", [32, 128], F32)
    hl1 = sbuf("hl1", [32, 128], BF16)
    hl2 = sbuf("hl2", [32, 128], BF16)
    ellT = sbuf("ellT", [128, 32], F32)
    dtT = sbuf("dtT", [128, 32], F32)
    ecolT = sbuf("ecolT", [128, 32], F32)
    lendr = sbuf("lendr", [1, 32], F32)
    lendE = sbuf("lendE", [1, 32], F32)
    ffac = sbuf("ffac", [128, 32], BF16)
    ftmp = sbuf("ftmp", [128, 32], F32)
    ftmp2 = sbuf("ftmp2", [128, 32], F32)
    edec = sbuf("edec", [128, 32], F32)
    lstk = sbuf("lstk", [4, 4096], BF16)
    rstk = sbuf("rstk", [4, 4096], BF16)
    gtile = sbuf("gtile", [128, 128], F32)
    masks = sbuf("masks", [128, 128], F32)
    jrevs = sbuf("jrevs", [128, 128], BF16)
    idf = sbuf("idf", [128, 128], F32)
    idb = sbuf("idb", [128, 128], BF16)
    onesb = sbuf("onesb", [128, 1], BF16)
    onesrb = sbuf("onesrb", [1, 128], BF16)
    onesf = sbuf("onesf", [1, 128], F32)
    epsc = sbuf("epsc", [128, 1], F32)
    dstage = sbuf("dstage", [128, 32], BF16)
    dtbias_c = sbuf("dtbias_c", [32, 1], F32)
    alog_c = sbuf("alog_c", [32, 1], F32)
    acol = sbuf("acol", [32, 1], F32)
    convw_c = sbuf("convw_c", [128, 18 * 4], F32)
    convb_c = sbuf("convb_c", [128, 18], F32)
    normw_c = sbuf("normw_c", [128, 16], F32)
    xtok = sbuf("xtok", [128, 2048], BF16)
    btok = sbuf("btok", [128, 128], BF16)
    difsb = sbuf("difsb", [128, 2048], F32)
    expsb = sbuf("expsb", [128, 2048], BF16)
    lhsT_all = sbuf("lhsT_all", [128, 4096], BF16)
    ytok = sbuf("ytok", [128, 2048], F32)
    ytmp = sbuf("ytmp", [128, 2048], F32)
    ytmp2 = sbuf("ytmp2", [128, 2048], BF16)
    ygat = sbuf("ygat", [128, 2048], BF16)
    szt = sbuf("szt", [128, 2048], BF16)
    ynrm = sbuf("ynrm", [128, 2048], BF16)
    yrev = sbuf("yrev", [128, 2048], BF16)
    ytfm = sbuf("ytfm", [128, 2048], BF16)
    mstage = sbuf("mstage", [128, 8 * 128], BF16)
    rmscol = sbuf("rmscol", [128, 1], F32)
    rstdc = sbuf("rstdc", [128, 1], F32)
    rstd2c = sbuf("rstd2c", [128, 1], F32)
    hstf = sbuf("hstf", [128, 2048], F32)
    hstb = sbuf("hstb", [128, 2048], BF16)
    xts = sbuf("xts", [128, 2048], BF16)
    mrow = dtraw[0:1, 0:TILE]
    e2row = dtp[0:1, 0:TILE]
    vrow = dta[0:1, 0:TILE]
    vrow2 = lndt[0:1, 0:TILE]
    rstdr = lndt[0:1, 0:TILE]

    pf = psum("pf", [128, 2048], F32)
    pq = psum("pq", [128, 1024], F32)
    pg = psum("pg", [128, 1024], F32)
    pgb = pg[:, :].bitcast(BF16)

    class DQ:
        def __init__(self, name):
            self.h = stack.enter_context(nc.semaphore(name))
            self.n = 0
        def go(self, eng, out_, in_, **kw):
            eng.dma_start(out=out_, in_=in_, **kw).then_inc(self.h, 16)
            self.n += 16
        def wait(self, eng, n=None):
            eng.wait_ge(self.h, self.n if n is None else n)

    gq = DQ("gq")
    sq = DQ("sq")
    ccs = stack.enter_context(nc.semaphore("ccs"))
    cc_n = [0]

    def block():
        return nc.Block()

    def xpart(k):
        buf = ytok if k < 4 else ytmp
        return buf[:, (k % 4) * TILE:(k % 4 + 1) * TILE]

    xbf_v = siluz[:, 0:8 * TILE]
    xsq_v = siluz[:, 8 * TILE:16 * TILE]

    def grp_cols(ml):
        return 128 * len(ml) - (96 if 34 in ml else 0)

    def win_load(eng, grp, half):
        ml = MGROUPS[grp]
        gw = grp_cols(ml)
        gq.go(eng,
              wslice[:, half * WHALF:half * WHALF + 8 * gw]
              .rearrange("p (k m) -> p k m", k=8),
              d["win"].ap().rearrange("(k p) m -> p k m", p=128)
              [:, :, ml[0] * 128:ml[0] * 128 + gw])

    with block() as b:
        @b.gpsimd
        def _(g):
            gq.go(g, masks[:, :], d["maskneg"][:, :])
            gq.go(g, jrevs[:, :], d["jrev"][:, :])
            gq.go(g, idf[:, :], d["ident"][:, :])
            gq.go(g, idb[:, :], d["ident"][:, :])
            gq.go(g, dtbias_c[:, :], d["dtbias"][:, :])
            gq.go(g, alog_c[:, :], d["alog"][:, :])
            gq.go(g, dstage[:, :], d["dhead"][:, :])
            gq.go(g, convw_c[:, :].rearrange("p (c k) -> p c k", c=18),
                  d["convw"].ap().rearrange("(c p) k -> p c k", p=128))
            gq.go(g, convb_c[:, :], d["convb"][:, :])
            gq.go(g, normw_c[:, :], d["normw"][:, :])
            g.memset(onesb[:, :], 1.0)
            g.memset(onesrb[:, :], 1.0)
            g.memset(onesf[:, :], 1.0)
            g.memset(ones32[:, :], 1.0)
            g.memset(onec32[:, :], 1.0)
            g.memset(epsc[:, :], EPS)
            g.memset(lstk[:, :], 1.0)
            g.memset(rstk[:, :], 1.0)
            g.memset(hstf[:, :], 0.0)
            g.memset(hstb[:, :], 0.0)
            g.memset(tailb[:, :], 0.0)
            gq.go(g, woutb[:, :].rearrange("p (k m) -> p k m", k=16),
                  d["wout"].ap().rearrange("(k p) m -> p k m", p=128))
            win_load(g, 0, 0)
        @b.vector
        def _(v):
            gq.wait(v)
    with block() as b:
        @b.scalar
        def _(s):
            s.activation(out=acol[:, :], in_=alog_c[:, :], func=AF.Exp)
        @b.vector
        def _(v):
            for k in range(16):
                v.tensor_scalar(out=woutb[:, k * D_MODEL:(k + 1) * D_MODEL],
                                in0=woutb[:, k * D_MODEL:(k + 1) * D_MODEL],
                                scalar1=normw_c[:, k:k + 1], scalar2=None, op0=ALU.mult)
    with block() as b:
        @b.vector
        def _(v):
            v.tensor_scalar(out=acol[:, :], in0=acol[:, :], scalar1=-1.0,
                            scalar2=None, op0=ALU.mult)

    CASTC = 256 * 1024
    cast_chunks = []

    def add_cast_job(src, dst, n):
        for i in range(0, n, CASTC):
            cast_chunks.append((dst[:, i:i + CASTC], src[:, i:i + CASTC]))

    for e in range(16):
        add_cast_job(d["gwT"].ap().rearrange("e a b -> e (a b)")[e:e + 1, :],
                     gwb.ap().rearrange("e a b -> e (a b)")[e:e + 1, :], 512 * 1024)
        add_cast_job(d["uwT"].ap().rearrange("e a b -> e (a b)")[e:e + 1, :],
                     uwb.ap().rearrange("e a b -> e (a b)")[e:e + 1, :], 512 * 1024)
        add_cast_job(d["dwT"].ap().rearrange("e a b -> e (a b)")[e:e + 1, :],
                     dwb.ap().rearrange("e a b -> e (a b)")[e:e + 1, :], 512 * 1024)
    for i, nm in enumerate(["shgT", "shuT", "shdT"]):
        add_cast_job(d[nm].ap().rearrange("a b -> () (a b)"),
                     shb.ap().rearrange("e a b -> e (a b)")[i:i + 1, :], 1024 * 1024)
    add_cast_job(d["wvT"].ap().rearrange("a b -> () (a b)"),
                 wvb.ap().rearrange("a b -> () (a b)"), 1024 * 1024)
    add_cast_job(d["woT"].ap().rearrange("a b -> () (a b)"),
                 wob.ap().rearrange("a b -> () (a b)"), 1024 * 1024)

    def cast_feed(g, k=1):
        for _ in range(k):
            if cast_chunks:
                dst, src = cast_chunks.pop(0)
                gq.go(g, dst, src)

    pending_stores = []

    def store_flush(g):
        for dst, src in pending_stores:
            gq.go(g, dst, src)
        pending_stores.clear()

    for t in range(NT):
        tok0 = t * TILE
        with block() as b:
            @b.gpsimd
            def _(g, tok0=tok0):
                for k in range(8):
                    gq.go(g, xpart(k),
                          d["x"].ap().rearrange("(k p) q -> p k q", p=128)
                          [:, k, tok0:tok0 + TILE])
                gq.wait(g)
        with block() as b:
            @b.vector
            def _(v):
                for k in range(8):
                    v.tensor_copy(out=xbf_v[:, k * TILE:(k + 1) * TILE], in_=xpart(k))
            @b.scalar
            def _(s):
                for k in range(8):
                    s.activation(out=xsq_v[:, k * TILE:(k + 1) * TILE], in_=xpart(k),
                                 func=AF.Square)
        with block() as b:
            @b.tensor
            def _(tt):
                for k in range(8):
                    tt.matmul(pg[0:1, 0:TILE], onesb[:, 0:1],
                              xbf_v[:, k * TILE:(k + 1) * TILE],
                              start=(k == 0), stop=(k == 7))
                for k in range(8):
                    tt.matmul(pg[0:1, TILE:2 * TILE], onesb[:, 0:1],
                              xsq_v[:, k * TILE:(k + 1) * TILE],
                              start=(k == 0), stop=(k == 7))
        with block() as b:
            @b.vector
            def _(v):
                v.tensor_scalar(out=mrow, in0=pg[0:1, 0:TILE], scalar1=1.0 / 1024,
                                scalar2=None, op0=ALU.mult)
                v.tensor_scalar(out=e2row, in0=pg[0:1, TILE:2 * TILE],
                                scalar1=1.0 / 1024, scalar2=None, op0=ALU.mult)
        with block() as b:
            @b.vector
            def _(v):
                v.tensor_tensor(out=vrow, in0=mrow, in1=mrow, op=ALU.mult)
        with block() as b:
            @b.vector
            def _(v):
                v.tensor_tensor(out=vrow2, in0=e2row, in1=vrow,
                                op=ALU.subtract)
        with block() as b:
            @b.scalar
            def _(s):
                s.activation(out=vrow, in_=vrow2, func=AF.Ln, bias=epsc[0:1, 0:1])
        with block() as b:
            @b.scalar
            def _(s):
                s.activation(out=rstdr, in_=vrow, func=AF.Exp, scale=-0.5)
        with block() as b:
            @b.tensor
            def _(tt):
                tt.matmul(pf[:, 0:TILE], onesf[0:1, :], mrow, start=True, stop=True)
                tt.matmul(pf[:, TILE:2 * TILE], onesf[0:1, :], rstdr,
                          start=True, stop=True)
        with block() as b:
            @b.vector
            def _(v):
                for k in range(8):
                    v.tensor_tensor(out=xpart(k), in0=xpart(k), in1=pf[:, 0:TILE],
                                    op=ALU.subtract)
                    v.tensor_tensor(out=xnb[:, k * TILE:(k + 1) * TILE],
                                    in0=xpart(k), in1=pf[:, TILE:2 * TILE], op=ALU.mult)
        if "dbg_xn" in dbg and t == 0:
            with block() as b:
                @b.vector
                def _(v):
                    for k in range(8):
                        v.tensor_copy(out=xpart(k), in_=xnb[:, k * TILE:(k + 1) * TILE])
            with block() as b:
                @b.gpsimd
                def _(g):
                    for k in range(8):
                        gq.go(g, dbg["dbg_xn"][:, k * TILE:(k + 1) * TILE], xpart(k))
                    gq.wait(g)

        with block() as b:
            @b.vector
            def _(v):
                v.tensor_copy(
                    out=xbcraw[:, :].rearrange("p (c q) -> p c q", c=18)[:, :, 0:3],
                    in_=tailb[:, :].rearrange("p (c k) -> p c k", c=18))

        def inproj_psum(grp, j, mw=128):
            buf = pf if grp % 2 == 0 else (pq if j < 2 else pg)
            jj = j if grp % 2 == 0 else j % 2
            return buf[0:mw, jj * TILE:(jj + 1) * TILE]

        def inproj_evac(eng, grp, kind):
            for j, m in enumerate(MGROUPS[grp]):
                src = inproj_psum(grp, j)
                if m < 16 and kind == 's':
                    eng.activation(out=siluz[:, m * TILE:(m + 1) * TILE],
                                   in_=src, func=AF.Silu)
                elif 16 <= m < 34 and kind == 'v':
                    c = m - 16
                    eng.tensor_copy(out=xbcraw[:, c * CR + 3:(c + 1) * CR], in_=src)
                elif m == 34 and kind == 'v':
                    eng.tensor_copy(out=dtraw[:, :], in_=src[0:32, :])

        for grp, ml in enumerate(MGROUPS):
            half = grp % 2
            gw = grp_cols(ml)
            gq_target = gq.n
            with block() as b:
                @b.gpsimd
                def _(g, grp=grp):
                    nxt = grp + 1
                    if nxt < len(MGROUPS):
                        win_load(g, nxt, nxt % 2)
                    if grp >= 5:
                        cast_feed(g)
                @b.tensor
                def _(tt, grp=grp, ml=ml, half=half, gw=gw, gq_target=gq_target):
                    gq.wait(tt, gq_target)
                    for j, m in enumerate(ml):
                        mw = 128 if m < 34 else 32
                        for k in range(8):
                            tt.matmul(inproj_psum(grp, j, mw),
                                      wslice[:, half * WHALF + k * gw + j * 128:
                                             half * WHALF + k * gw + j * 128 + mw],
                                      xnb[:, k * TILE:(k + 1) * TILE],
                                      start=(k == 0), stop=(k == 7))
                @b.scalar
                def _(s, grp=grp):
                    if grp > 0:
                        inproj_evac(s, grp - 1, 's')
                @b.vector
                def _(v, grp=grp):
                    if grp > 0:
                        inproj_evac(v, grp - 1, 'v')
        with block() as b:
            @b.vector
            def _(v):
                inproj_evac(v, len(MGROUPS) - 1, 'v')

        with block() as b:
            @b.gpsimd
            def _(g, t=t):
                if t + 1 < NT:
                    win_load(g, 0, 0)
            @b.vector
            def _(v):
                v.tensor_copy(
                    out=tailb[:, :].rearrange("p (c k) -> p c k", c=18),
                    in_=xbcraw[:, :].rearrange("p (c q) -> p c q", c=18)[:, :, TILE:TILE + 3])

        for c in range(19):
            with block() as b:
                @b.vector
                def _(v, c=c):
                    if c < 18:
                        base = c * CR
                        o1 = (c % 2) * 1024
                        v.tensor_scalar(out=ytok[:, o1:o1 + TILE],
                                        in0=xbcraw[:, base:base + TILE],
                                        scalar1=convw_c[:, c * 4:c * 4 + 1],
                                        scalar2=convb_c[:, c:c + 1],
                                        op0=ALU.mult, op1=ALU.add)
                        v.scalar_tensor_tensor(out=ytok[:, o1 + TILE:o1 + 2 * TILE],
                                               in0=xbcraw[:, base + 1:base + 1 + TILE],
                                               scalar=convw_c[:, c * 4 + 1:c * 4 + 2],
                                               in1=ytok[:, o1:o1 + TILE],
                                               op0=ALU.mult, op1=ALU.add)
                        v.scalar_tensor_tensor(out=ytok[:, o1:o1 + TILE],
                                               in0=xbcraw[:, base + 2:base + 2 + TILE],
                                               scalar=convw_c[:, c * 4 + 2:c * 4 + 3],
                                               in1=ytok[:, o1 + TILE:o1 + 2 * TILE],
                                               op0=ALU.mult, op1=ALU.add)
                        v.scalar_tensor_tensor(out=ytok[:, o1 + TILE:o1 + 2 * TILE],
                                               in0=xbcraw[:, base + 3:base + 3 + TILE],
                                               scalar=convw_c[:, c * 4 + 3:c * 4 + 4],
                                               in1=ytok[:, o1:o1 + TILE],
                                               op0=ALU.mult, op1=ALU.add)
                @b.scalar
                def _(s, c=c):
                    cp = c - 1
                    if cp >= 0:
                        base = cp * CR
                        o1 = (cp % 2) * 1024
                        s.activation(out=xbcraw[:, base + 3:base + 3 + TILE],
                                     in_=ytok[:, o1 + TILE:o1 + 2 * TILE], func=AF.Silu)
        if "dbg_xbc" in dbg and t == 0:
            with block() as b:
                @b.gpsimd
                def _(g):
                    gq.go(g, dbg["dbg_xbc"][:, :].rearrange("p (c q) -> p c q", c=18),
                          xbcraw[:, :].rearrange("p (c q) -> p c q", c=18)[:, :, 3:3 + TILE])
                    gq.wait(g)

        with block() as b:
            @b.vector
            def _(v):
                v.tensor_scalar(out=dtp[:, :], in0=dtraw[:, :],
                                scalar1=dtbias_c[:, 0:1], scalar2=None, op0=ALU.add)
        with block() as b:
            @b.scalar
            def _(s):
                s.activation(out=dta[:, :], in_=dtp[:, :], func=AF.Abs)
            @b.vector
            def _(v):
                v.tensor_scalar(out=dtraw[:, :], in0=dtp[:, :], scalar1=0.0,
                                scalar2=None, op0=ALU.max)
        with block() as b:
            @b.scalar
            def _(s):
                s.activation(out=lndt[:, :], in_=dta[:, :], func=AF.Exp, scale=-1.0)
        with block() as b:
            @b.scalar
            def _(s):
                s.activation(out=dta[:, :], in_=lndt[:, :], func=AF.Ln, bias=onec32[:, 0:1])
        with block() as b:
            @b.vector
            def _(v):
                v.tensor_tensor(out=dtp[:, :], in0=dtraw[:, :], in1=dta[:, :], op=ALU.add)
        with block() as b:
            @b.vector
            def _(v):
                v.tensor_scalar(out=dta[:, :], in0=dtp[:, :], scalar1=acol[:, 0:1],
                                scalar2=None, op0=ALU.mult)
            @b.scalar
            def _(s):
                s.activation(out=lndt[:, :], in_=dtp[:, :], func=AF.Ln)
        if "dbg_dt" in dbg and t == 0:
            with block() as b:
                @b.gpsimd
                def _(g):
                    gq.go(g, dbg["dbg_dt"][:, :], dtp[:, :])
                    gq.wait(g)

        for c in range(4):
            cg = t * 4 + c
            c0 = c * 128
            with block() as b:
                @b.vector
                def _(v, c0=c0):
                    v.tensor_tensor_scan(out=ellf[:, :], data0=ones32[:, :],
                                         data1=dta[:, c0:c0 + 128], initial=0.0,
                                         op0=ALU.mult, op1=ALU.add)
            with block() as b:
                @b.vector
                def _(v, c0=c0):
                    v.tensor_tensor(out=lmef[:, :], in0=lndt[:, c0:c0 + 128],
                                    in1=ellf[:, :], op=ALU.subtract)
            with block() as b:
                @b.vector
                def _(v):
                    v.tensor_copy(out=hl1[:, :], in_=lmef[:, :])
                    v.tensor_copy(out=hl2[:, :], in_=ellf[:, :])
            with block() as b:
                @b.vector
                def _(v):
                    v.tensor_tensor(out=lmef[:, :], in0=lmef[:, :], in1=hl1[:, :],
                                    op=ALU.subtract)
                    v.tensor_tensor(out=ellf[:, :], in0=ellf[:, :], in1=hl2[:, :],
                                    op=ALU.subtract)
            with block() as b:
                @b.gpsimd
                def _(g):
                    gq.go(g, ellbuf[0:32, :], hl1[:, :])
                    gq.go(g, ellbuf[32:64, :], lmef[:, :])
                    gq.go(g, ellbuf[64:96, :], hl2[:, :])
                    gq.go(g, ellbuf[96:128, :], ellf[:, :])
                    gq.wait(g)
                    flat = ellbuf.ap().rearrange("(a p) q -> a (p q)", a=4)
                    gq.go(g, lstk[0:1, :], flat[0:1, :])
                    gq.go(g, lstk[1:2, :], flat[1:2, :])
                    gq.go(g, rstk[2:3, :], flat[2:3, :])
                    gq.go(g, rstk[3:4, :], flat[3:4, :])
                    gq.wait(g)
            with block() as b:
                @b.vector
                def _(v):
                    v.tensor_tensor(out=ellf[:, :], in0=hl2[:, :], in1=ellf[:, :], op=ALU.add)
            with block() as b:
                @b.tensor
                def _(tt, c0=c0):
                    tt.transpose(pg[:, 0:32], ellf[:, :], idf[0:32, 0:32])
                    tt.transpose(pg[:, 32:64], dtp[:, c0:c0 + 128], idf[0:32, 0:32])
                    tt.transpose(pg[0:1, 64:96], ellf[:, 127:128], idf[0:32, 0:32])
            with block() as b:
                @b.vector
                def _(v):
                    v.tensor_copy(out=ellT[:, :], in_=pg[:, 0:32])
                    v.tensor_copy(out=dtT[:, :], in_=pg[:, 32:64])
                    v.tensor_copy(out=lendr[:, :], in_=pg[0:1, 64:96])
            with block() as b:
                @b.scalar
                def _(s):
                    s.activation(out=ecolT[:, :], in_=ellT[:, :], func=AF.Exp)
            with block() as b:
                @b.scalar
                def _(s):
                    s.activation(out=lendE[:, :], in_=lendr[:, :], func=AF.Exp)
            with block() as b:
                @b.tensor
                def _(tt, c0=c0):
                    tt.matmul(pf[:, 0:32], onesf[0:1, :], lendr[:, :], start=True, stop=True)
                    tt.matmul(pf[:, 32:64], onesf[0:1, :], lendE[:, :], start=True, stop=True)
                    tt.matmul(pf[:, 128:256],
                              xbcraw[:, 16 * CR + 3 + c0:16 * CR + 3 + c0 + 128],
                              xbcraw[:, 17 * CR + 3 + c0:17 * CR + 3 + c0 + 128],
                              start=True, stop=True)
            with block() as b:
                @b.vector
                def _(v):
                    v.tensor_tensor(out=ftmp[:, :], in0=pf[:, 0:32], in1=ellT[:, :],
                                    op=ALU.subtract)
                    v.tensor_copy(out=edec[:, :], in_=pf[:, 32:64])
                    v.tensor_copy(out=gtile[:, :], in_=pf[:, 128:256])
            with block() as b:
                @b.scalar
                def _(s):
                    s.activation(out=ftmp2[:, :], in_=ftmp[:, :], func=AF.Exp)
            with block() as b:
                @b.vector
                def _(v):
                    v.tensor_tensor(out=ffac[:, :], in0=ftmp2[:, :], in1=dtT[:, :],
                                    op=ALU.mult)
            with block() as b:
                @b.tensor
                def _(tt, c0=c0):
                    for kc in range(16):
                        tt.transpose(pgb[:, kc * 128:(kc + 1) * 128],
                                     xbcraw[:, kc * CR + 3 + c0:kc * CR + 3 + c0 + 128],
                                     idb[:, :])
            with block() as b:
                @b.vector
                def _(v):
                    for kc in range(8):
                        v.tensor_copy(out=xtok[:, kc * 128:(kc + 1) * 128],
                                      in_=pgb[:, kc * 128:(kc + 1) * 128])
                @b.scalar
                def _(s):
                    for kc in range(8, 16):
                        s.copy(out=xtok[:, kc * 128:(kc + 1) * 128],
                               in_=pgb[:, kc * 128:(kc + 1) * 128])
            with block() as b:
                @b.tensor
                def _(tt, c0=c0):
                    for kc in range(16):
                        tt.transpose(pgb[:, kc * 128:(kc + 1) * 128],
                                     siluz[:, kc * TILE + c0:kc * TILE + c0 + 128],
                                     idb[:, :])
            with block() as b:
                @b.vector
                def _(v):
                    for kc in range(8):
                        v.tensor_copy(out=szt[:, kc * 128:(kc + 1) * 128],
                                      in_=pgb[:, kc * 128:(kc + 1) * 128])
                @b.scalar
                def _(s):
                    for kc in range(8, 16):
                        s.copy(out=szt[:, kc * 128:(kc + 1) * 128],
                               in_=pgb[:, kc * 128:(kc + 1) * 128])
            with block() as b:
                @b.tensor
                def _(tt, c0=c0):
                    tt.transpose(pgb[:, 0:128],
                                 xbcraw[:, 16 * CR + 3 + c0:16 * CR + 3 + c0 + 128],
                                 idb[:, :])
            with block() as b:
                @b.vector
                def _(v):
                    v.tensor_copy(out=btok[:, :], in_=pgb[:, 0:128])
            with block() as b:
                @b.tensor
                def _(tt):
                    for h in range(16):
                        tt.matmul(pf[:, h * 128:(h + 1) * 128],
                                  lstk[:, h * 128:(h + 1) * 128],
                                  rstk[:, h * 128:(h + 1) * 128], start=True, stop=True)
            with block() as b:
                @b.vector
                def _(v):
                    v.tensor_tensor(
                        out=difsb[:, :].rearrange("p (h q) -> p h q", h=16),
                        in0=pf[:, 0:2048].rearrange("p (h q) -> p h q", h=16),
                        in1=masks[:, :].rearrange("p q -> p () q").broadcast_to((128, 16, 128)),
                        op=ALU.add)
            with block() as b:
                @b.scalar
                def _(s):
                    s.activation(out=expsb[:, :], in_=difsb[:, :], func=AF.Exp)
                @b.tensor
                def _(tt):
                    for h in range(16, 32):
                        tt.matmul(pf[:, (h - 16) * 128:(h - 15) * 128],
                                  lstk[:, h * 128:(h + 1) * 128],
                                  rstk[:, h * 128:(h + 1) * 128], start=True, stop=True)
            with block() as b:
                @b.vector
                def _(v):
                    v.tensor_tensor(
                        out=lhsT_all[:, 0:2048].rearrange("p (h q) -> p h q", h=16),
                        in0=expsb[:, :].rearrange("p (h q) -> p h q", h=16),
                        in1=gtile[:, :].rearrange("p q -> p () q").broadcast_to((128, 16, 128)),
                        op=ALU.mult)
                    v.tensor_tensor(
                        out=difsb[:, :].rearrange("p (h q) -> p h q", h=16),
                        in0=pf[:, 0:2048].rearrange("p (h q) -> p h q", h=16),
                        in1=masks[:, :].rearrange("p q -> p () q").broadcast_to((128, 16, 128)),
                        op=ALU.add)
            with block() as b:
                @b.scalar
                def _(s):
                    s.activation(out=expsb[:, :], in_=difsb[:, :], func=AF.Exp)
            with block() as b:
                @b.vector
                def _(v):
                    v.tensor_tensor(
                        out=lhsT_all[:, 2048:4096].rearrange("p (h q) -> p h q", h=16),
                        in0=expsb[:, :].rearrange("p (h q) -> p h q", h=16),
                        in1=gtile[:, :].rearrange("p q -> p () q").broadcast_to((128, 16, 128)),
                        op=ALU.mult)
            for half in range(2):
                h0 = half * 16
                with block() as b:
                    @b.tensor
                    def _(tt, h0=h0, c0=c0):
                        for i in range(16):
                            h = h0 + i
                            tt.matmul(pf[:, i * 64:(i + 1) * 64],
                                      lhsT_all[:, h * 128:(h + 1) * 128],
                                      xtok[:, h * 64:(h + 1) * 64], start=True, stop=True)
                        for i in range(16):
                            h = h0 + i
                            tt.matmul(pf[:, 1024 + i * 64:1024 + (i + 1) * 64],
                                      xbcraw[:, 17 * CR + 3 + c0:17 * CR + 3 + c0 + 128],
                                      hstb[:, h * 64:(h + 1) * 64], start=True, stop=True)
                with block() as b:
                    @b.vector
                    def _(v, h0=h0, half=half):
                        v.tensor_tensor(
                            out=ytmp2[:, 0:1024].rearrange("p (h q) -> p h q", h=16),
                            in0=pf[:, 1024:2048].rearrange("p (h q) -> p h q", h=16),
                            in1=ecolT[:, h0:h0 + 16].rearrange("p h -> p h ()")
                                .broadcast_to((128, 16, 64)),
                            op=ALU.mult)
                        v.tensor_tensor(out=ygat[:, half * 1024:(half + 1) * 1024],
                                        in0=ytmp2[:, 0:1024], in1=pf[:, 0:1024],
                                        op=ALU.add)
            with block() as b:
                @b.vector
                def _(v):
                    v.tensor_tensor(
                        out=ytmp2[:, :].rearrange("p (h q) -> p h q", h=32),
                        in0=xtok[:, :].rearrange("p (h q) -> p h q", h=32),
                        in1=dstage[:, :].rearrange("p h -> p h ()").broadcast_to((128, 32, 64)),
                        op=ALU.mult)
            with block() as b:
                @b.vector
                def _(v):
                    v.tensor_tensor(out=ynrm[:, :], in0=ygat[:, :], in1=ytmp2[:, :],
                                    op=ALU.add)
            with block() as b:
                @b.vector
                def _(v):
                    v.tensor_tensor(out=ygat[:, :], in0=ynrm[:, :], in1=szt[:, :],
                                    op=ALU.mult)
            if "dbg_y" in dbg and cg == 0:
                with block() as b:
                    @b.gpsimd
                    def _(g):
                        gq.go(g, dbg["dbg_y"][:, :], ygat[:, :])
                        gq.wait(g)
            with block() as b:
                @b.scalar
                def _(s):
                    s.activation(out=ytmp2[:, :], in_=ygat[:, :], func=AF.Square,
                                 accum_out=rmscol[:, :])
            with block() as b:
                @b.scalar
                def _(s):
                    s.activation(out=rstdc[:, :], in_=rmscol[:, :], func=AF.Ln,
                                 bias=epsc[:, 0:1], scale=1.0 / 2048)
            with block() as b:
                @b.scalar
                def _(s):
                    s.activation(out=rstd2c[:, :], in_=rstdc[:, :], func=AF.Exp, scale=-0.5)
            with block() as b:
                @b.vector
                def _(v):
                    v.tensor_scalar(out=ynrm[:, :], in0=ygat[:, :], scalar1=rstd2c[:, 0:1],
                                    scalar2=None, op0=ALU.mult)
            with block() as b:
                @b.tensor
                def _(tt):
                    for j in range(4):
                        tt.matmul(pf[:, j * 512:(j + 1) * 512], jrevs[:, :],
                                  ynrm[:, j * 512:(j + 1) * 512], start=True, stop=True)
            with block() as b:
                @b.vector
                def _(v):
                    for j in range(2):
                        v.tensor_copy(out=yrev[:, j * 512:(j + 1) * 512],
                                      in_=pf[:, j * 512:(j + 1) * 512])
                @b.scalar
                def _(s):
                    for j in range(2, 4):
                        s.copy(out=yrev[:, j * 512:(j + 1) * 512],
                               in_=pf[:, j * 512:(j + 1) * 512])

            def op_pass(src, dst_dram, slot):
                with block() as b:
                    @b.tensor
                    def _(tt, src=src):
                        for kc in range(16):
                            tt.transpose(pgb[:, kc * 128:(kc + 1) * 128],
                                         src[:, kc * 128:(kc + 1) * 128], idb[:, :])
                    @b.gpsimd
                    def _(g):
                        store_flush(g)
                with block() as b:
                    @b.vector
                    def _(v):
                        for kc in range(8):
                            v.tensor_copy(out=ytfm[:, kc * 128:(kc + 1) * 128],
                                          in_=pgb[:, kc * 128:(kc + 1) * 128])
                    @b.scalar
                    def _(s):
                        for kc in range(8, 16):
                            s.copy(out=ytfm[:, kc * 128:(kc + 1) * 128],
                                   in_=pgb[:, kc * 128:(kc + 1) * 128])
                with block() as b:
                    @b.tensor
                    def _(tt):
                        for m in range(8):
                            for k in range(16):
                                tt.matmul(pf[:, m * 128:(m + 1) * 128],
                                          woutb[:, k * D_MODEL + m * 128:
                                                k * D_MODEL + (m + 1) * 128],
                                          ytfm[:, k * 128:(k + 1) * 128],
                                          start=(k == 0), stop=(k == 15))
                    @b.gpsimd
                    def _(g):
                        cast_feed(g)
                with block() as b:
                    @b.vector
                    def _(v):
                        for m in range(4):
                            v.tensor_copy(out=mstage[:, m * 128:(m + 1) * 128],
                                          in_=pf[:, m * 128:(m + 1) * 128])
                    @b.scalar
                    def _(s):
                        for m in range(4, 8):
                            s.copy(out=mstage[:, m * 128:(m + 1) * 128],
                                   in_=pf[:, m * 128:(m + 1) * 128])
                pending_stores.append(
                    (dst_dram.ap().rearrange("p (k q) -> p k q", k=8)
                     [:, :, slot * 128:(slot + 1) * 128],
                     mstage[:, :].rearrange("p (k q) -> p k q", k=8)))

            op_pass(yrev, arin, 31 - cg)
            if cg < 16:
                op_pass(ynrm, mnat, cg)

            with block() as b:
                @b.vector
                def _(v):
                    v.tensor_tensor(
                        out=xts[:, :].rearrange("p (h q) -> p h q", h=32),
                        in0=xtok[:, :].rearrange("p (h q) -> p h q", h=32),
                        in1=ffac[:, :].rearrange("p h -> p h ()").broadcast_to((128, 32, 64)),
                        op=ALU.mult)
            with block() as b:
                @b.tensor
                def _(tt):
                    for h in range(32):
                        tt.matmul(pf[:, h * 64:(h + 1) * 64], btok[:, :],
                                  xts[:, h * 64:(h + 1) * 64], start=True, stop=True)
                @b.gpsimd
                def _(g):
                    store_flush(g)
                    cast_feed(g)
            with block() as b:
                @b.vector
                def _(v):
                    v.tensor_tensor(
                        out=ytok[:, :].rearrange("p (h q) -> p h q", h=32),
                        in0=hstf[:, :].rearrange("p (h q) -> p h q", h=32),
                        in1=edec[:, :].rearrange("p h -> p h ()").broadcast_to((128, 32, 64)),
                        op=ALU.mult)
            with block() as b:
                @b.vector
                def _(v):
                    v.tensor_tensor(out=hstf[:, :], in0=ytok[:, :], in1=pf[:, 0:2048],
                                    op=ALU.add)
            with block() as b:
                @b.scalar
                def _(s):
                    s.copy(out=hstb[:, :], in_=hstf[:, :])

    with block() as b:
        @b.gpsimd
        def _(g):
            g.collective_compute(
                "AllReduce", ALU.add,
                replica_groups=[[0, 1], [2, 3], [4, 5], [6, 7]],
                ins=[arin.ap().opt()], outs=[arout.ap().opt()],
            ).then_inc(ccs)
            cc_n[0] += 1
            g.wait_ge(ccs, cc_n[0])

    if "dbg_mrev" in dbg:
        with block() as b:
            @b.gpsimd
            def _(g):
                gq.go(g, dbg["dbg_mrev"][:, :], arin[:, :])
                gq.wait(g)
    if "dbg_mnat" in dbg:
        with block() as b:
            @b.gpsimd
            def _(g):
                gq.go(g, dbg["dbg_mnat"][:, :], mnat[:, :])
                gq.wait(g)

    if cast_chunks or pending_stores:
        with block() as b:
            @b.gpsimd
            def _(g):
                store_flush(g)
                cast_feed(g, len(cast_chunks))

    rwf = gtile
    with block() as b:
        @b.gpsimd
        def _(g):
            gq.go(g, rwf[:, :].rearrange("p (k e) -> p k e", k=8),
                  d["rwT"].ap().rearrange("(k p) e -> p k e", p=128))
            gq.go(g, dstage[:, 0:16], d["ebias"][:, 0:16])
            gq.go(g, ffac[:, 0:16], d["tiew"][:, 0:16])
            gq.go(g, ffac[:, 16:20], d["tiew4"][:, 0:4])
            g.memset(rmscol[:, :], 0.0)
            gq.wait(g)
    ebias_t = ellT
    tiew_t = dtT
    with block() as b:
        @b.gpsimd
        def _(g):
            gq.go(g, ebias_t[:, 0:16], d["ebias"][:, 0:16])
            gq.go(g, tiew_t[:, 0:16], d["tiew"][:, 0:16])
            gq.go(g, tiew_t[:, 16:20], d["tiew4"][:, 0:4])
            gq.wait(g)

    rwh = mstage[:, 0:128]
    rwl = mstage[:, 128:256]
    with block() as b:
        @b.vector
        def _(v):
            v.tensor_copy(out=rwh, in_=rwf[:, :])
    with block() as b:
        @b.vector
        def _(v):
            v.tensor_tensor(out=rwl, in0=rwf[:, :], in1=rwh, op=ALU.subtract)

    NTB = 8
    TB = 256
    gww = siluz
    uww = xbcraw
    DWRING = [lhsT_all[:, 0:4096], wslice[:, 0:4096], wslice[:, 4096:8192]]

    def moe_w_load(g, e, half):
        moe_w_load_gu(g, e, half)
        moe_w_load_d(g, e)

    def moe_w_load_gu(g, e, half):
        gq.go(g, gww[:, half * 4096:(half + 1) * 4096].rearrange("p (k i) -> p k i", k=8),
              gwb.ap().rearrange("e (k p) i -> p (e k) i", p=128)[:, e * 8:(e + 1) * 8, :])
        gq.go(g, uww[:, half * 4096:(half + 1) * 4096].rearrange("p (k i) -> p k i", k=8),
              uwb.ap().rearrange("e (k p) i -> p (e k) i", p=128)[:, e * 8:(e + 1) * 8, :])

    def moe_w_load_d(g, e):
        gq.go(g, DWRING[e % 3].rearrange("p (k f) -> p k f", k=4),
              dwb.ap().rearrange("e (k p) f -> p (e k) f", p=128)[:, e * 4:(e + 1) * 4, :])

    moe_tgt = {}

    for ti in range(NTB):
        tok0 = ti * TB
        with block() as b:
            @b.gpsimd
            def _(g, tok0=tok0, ti=ti):
                gq.go(g, xtok[:, :].rearrange("p (k q) -> p k q", k=8),
                      arout.ap().rearrange("p (k q) -> p k q", k=8)[:, :, tok0:tok0 + TB])
                gq.go(g, szt[:, :].rearrange("p (k q) -> p k q", k=8),
                      arin.ap().rearrange("p (k q) -> p k q", k=8)[:, :, tok0:tok0 + TB])
                gq.go(g, ynrm[:, :].rearrange("p (k q) -> p k q", k=8),
                      mnat.ap().rearrange("p (k q) -> p k q", k=8)[:, :, tok0:tok0 + TB])
                gq.wait(g)
        if "dbg_y" in dbg and ti == 0:
            with block() as b:
                @b.gpsimd
                def _(g):
                    for k in range(8):
                        gq.go(g, dbg["dbg_y"][:, k * TB:(k + 1) * TB], xpart(k)[:, 0:TB])
                    gq.wait(g)
        with block() as b:
            @b.vector
            def _(v):
                v.tensor_tensor(out=yrev[:, :], in0=xtok[:, :], in1=szt[:, :],
                                op=ALU.subtract)
            @b.gpsimd
            def _(g, tok0=tok0):
                xsrc = d["x"].ap().rearrange("(k p) q -> p k q", p=128)
                gq.go(g, ytok[:, 0:2048].rearrange("p (k q) -> p k q", k=4)[:, :, 0:TB],
                      xsrc[:, 0:4, tok0:tok0 + TB])
                gq.go(g, ytmp[:, 0:2048].rearrange("p (k q) -> p k q", k=4)[:, :, 0:TB],
                      xsrc[:, 4:8, tok0:tok0 + TB])
        with block() as b:
            @b.vector
            def _(v):
                v.tensor_tensor(out=yrev[:, :], in0=yrev[:, :], in1=ynrm[:, :], op=ALU.add)
        if "dbg_s" in dbg:
            with block() as b:
                @b.gpsimd
                def _(g, tok0=tok0):
                    gq.go(g, dbg["dbg_s"][:, :].rearrange("p (k q) -> p k q", k=8)
                          [:, :, tok0:tok0 + TB],
                          yrev[:, :].rearrange("p (k q) -> p k q", k=8))
                    gq.wait(g)
        with block() as b:
            @b.vector
            def _(v):
                for k in range(8):
                    v.tensor_copy(out=expsb[:, k * TB:(k + 1) * TB], in_=xpart(k)[:, 0:TB])
            @b.scalar
            def _(s):
                for k in range(8):
                    s.activation(out=hstb[:, k * TB:(k + 1) * TB], in_=xpart(k)[:, 0:TB],
                                 func=AF.Square)
        with block() as b:
            @b.tensor
            def _(tt):
                for k in range(8):
                    tt.matmul(pg[0:1, 0:TB], onesb[:, 0:1], expsb[:, k * TB:(k + 1) * TB],
                              start=(k == 0), stop=(k == 7))
                for k in range(8):
                    tt.matmul(pg[0:1, 512:512 + TB], onesb[:, 0:1],
                              hstb[:, k * TB:(k + 1) * TB],
                              start=(k == 0), stop=(k == 7))
            @b.gpsimd
            def _(g):
                moe_w_load_gu(g, 0, 0)
        with block() as b:
            @b.vector
            def _(v):
                v.tensor_scalar(out=mrow[0:1, 0:TB], in0=pg[0:1, 0:TB], scalar1=1.0 / 1024,
                                scalar2=None, op0=ALU.mult)
                v.tensor_scalar(out=e2row[0:1, 0:TB], in0=pg[0:1, 512:512 + TB],
                                scalar1=1.0 / 1024, scalar2=None, op0=ALU.mult)
        with block() as b:
            @b.vector
            def _(v):
                v.tensor_tensor(out=vrow[0:1, 0:TB], in0=mrow[0:1, 0:TB],
                                in1=mrow[0:1, 0:TB], op=ALU.mult)
        with block() as b:
            @b.vector
            def _(v):
                v.tensor_tensor(out=vrow2[0:1, 0:TB], in0=e2row[0:1, 0:TB],
                                in1=vrow[0:1, 0:TB], op=ALU.subtract)
        with block() as b:
            @b.scalar
            def _(s):
                s.activation(out=vrow[0:1, 0:TB], in_=vrow2[0:1, 0:TB], func=AF.Ln,
                             bias=epsc[0:1, 0:1])
        with block() as b:
            @b.scalar
            def _(s):
                s.activation(out=rstdr[0:1, 0:TB], in_=vrow[0:1, 0:TB], func=AF.Exp,
                             scale=-0.5)
        with block() as b:
            @b.tensor
            def _(tt):
                tt.matmul(pq[:, 0:TB], onesf[0:1, :], mrow[0:1, 0:TB], start=True, stop=True)
                tt.matmul(pq[:, 512:512 + TB], onesf[0:1, :], rstdr[0:1, 0:TB],
                          start=True, stop=True)
        with block() as b:
            @b.vector
            def _(v):
                for k in range(8):
                    v.tensor_tensor(out=difsb[:, k * TB:(k + 1) * TB],
                                    in0=xpart(k)[:, 0:TB], in1=pq[:, 0:TB], op=ALU.subtract)
        with block() as b:
            @b.vector
            def _(v):
                for k in range(8):
                    v.tensor_tensor(out=difsb[:, k * TB:(k + 1) * TB],
                                    in0=difsb[:, k * TB:(k + 1) * TB],
                                    in1=pq[:, 512:512 + TB], op=ALU.mult)
        with block() as b:
            @b.vector
            def _(v):
                v.tensor_copy(out=ytfm[:, :], in_=difsb[:, 0:2048])
        with block() as b:
            @b.vector
            def _(v):
                v.tensor_tensor(out=expsb[:, :], in0=difsb[:, 0:2048], in1=ytfm[:, :],
                                op=ALU.subtract)
        if "dbg_ffn" in dbg and ti == 0:
            with block() as b:
                @b.gpsimd
                def _(g):
                    gq.go(g, dbg["dbg_ffn"][:, 0:2048], ytfm[:, :])
                    for k in range(8):
                        gq.go(g, dbg["dbg_ffn"][:, 2048 + k * TB:2048 + (k + 1) * TB],
                              xpart(k)[:, 0:TB])
                    gq.wait(g)
        if "dbg_upr" in dbg and ti == 0:
            with block() as b:
                @b.gpsimd
                def _(g):
                    gq.go(g, dbg["dbg_upr"][:, 0:2048], ytfm[:, :])
                    gq.go(g, dbg["dbg_upr"][:, 2048:4096], expsb[:, :])
                    gq.go(g, dbg["dbg_upr"][:, 4096:4224], rwh)
                    gq.go(g, dbg["dbg_upr"][:, 4224:4352], rwl)
                    gq.wait(g)
        with block() as b:
            @b.tensor
            def _(tt):
                for k in range(8):
                    for lhs, rhs in ((rwh, ytfm), (rwh, expsb), (rwl, ytfm)):
                        tt.matmul(pg[0:16, 0:TB],
                                  lhs[:, k * 16:(k + 1) * 16],
                                  rhs[:, k * TB:(k + 1) * TB],
                                  start=(k == 0 and lhs is rwh and rhs is ytfm),
                                  stop=(k == 7 and lhs is rwl))
            @b.gpsimd
            def _(g):
                moe_w_load_d(g, 0)
                moe_tgt[0] = gq.n
        with block() as b:
            @b.vector
            def _(v):
                v.tensor_copy(out=dtraw[0:16, 0:TB], in_=pg[0:16, 0:TB])
        with block() as b:
            @b.tensor
            def _(tt):
                for j in range(2):
                    tt.transpose(pq[0:128, j * 512:j * 512 + 16],
                                 dtraw[0:16, j * 128:(j + 1) * 128], idf[0:16, 0:16])
        with block() as b:
            @b.vector
            def _(v):
                v.tensor_copy(out=ecolT[:, 0:16], in_=pq[0:128, 0:16])
                v.tensor_copy(out=ecolT[:, 16:32], in_=pq[0:128, 512:528])
        with block() as b:
            @b.scalar
            def _(s):
                s.activation(out=edec[:, 0:32], in_=ecolT[:, 0:32], func=AF.Sigmoid)
        with block() as b:
            @b.vector
            def _(v):
                v.tensor_tensor(
                    out=ftmp2[:, 0:32].rearrange("p (s e) -> p s e", s=2),
                    in0=edec[:, 0:32].rearrange("p (s e) -> p s e", s=2),
                    in1=ebias_t[:, 0:16].rearrange("p e -> p () e").broadcast_to((128, 2, 16)),
                    op=ALU.add)
        gsv = difsb[:, 0:8]
        with block() as b:
            @b.vector
            def _(v):
                scg = ftmp2[:, 0:32].rearrange("p (sg j) -> p sg j", j=4)
                for i, (a, bb) in enumerate([(0, 1), (0, 2), (0, 3), (1, 2), (1, 3), (2, 3)]):
                    v.tensor_tensor(
                        out=difsb[:, 1024 + i * 8:1032 + i * 8].rearrange("p (sg) -> p sg ()"),
                        in0=scg[:, :, a:a + 1], in1=scg[:, :, bb:bb + 1], op=ALU.add)
        with block() as b:
            @b.vector
            def _(v):
                v.tensor_tensor(out=difsb[:, 1072:1080], in0=difsb[:, 1024:1032],
                                in1=difsb[:, 1032:1040], op=ALU.max)
                v.tensor_tensor(out=difsb[:, 1080:1088], in0=difsb[:, 1040:1048],
                                in1=difsb[:, 1048:1056], op=ALU.max)
                v.tensor_tensor(out=difsb[:, 1088:1096], in0=difsb[:, 1056:1064],
                                in1=difsb[:, 1064:1072], op=ALU.max)
        with block() as b:
            @b.vector
            def _(v):
                v.tensor_tensor(out=difsb[:, 1096:1104], in0=difsb[:, 1072:1080],
                                in1=difsb[:, 1080:1088], op=ALU.max)
        with block() as b:
            @b.vector
            def _(v):
                v.tensor_tensor(out=gsv, in0=difsb[:, 1096:1104],
                                in1=difsb[:, 1088:1096], op=ALU.max)
        if "dbg_upr" in dbg and ti == 0:
            with block() as b:
                @b.gpsimd
                def _(g):
                    gq.go(g, dbg["dbg_upr"][:, 4352:4360], gsv)
                    gq.wait(g)
        gmask = difsb[:, 16:24]
        with block() as b:
            @b.vector
            def _(v):
                v.tensor_scalar(out=gmask, in0=gsv, scalar1=0.0, scalar2=None, op0=ALU.mult)
        for it in range(2):
            with block() as b:
                @b.vector
                def _(v):
                    for sj in range(2):
                        gss = difsb[:, sj * 4:(sj + 1) * 4]
                        v.tensor_reduce(out=difsb[:, 24 + sj:25 + sj], in_=gss, axis=AX,
                                        op=ALU.max)
            with block() as b:
                @b.vector
                def _(v):
                    for sj in range(2):
                        gss = difsb[:, sj * 4:(sj + 1) * 4]
                        v.tensor_scalar(out=difsb[:, 32 + sj * 4:32 + (sj + 1) * 4],
                                        in0=gss, scalar1=difsb[:, 24 + sj:25 + sj],
                                        scalar2=None, op0=ALU.is_equal)
            with block() as b:
                @b.vector
                def _(v):
                    v.tensor_tensor(out=difsb[:, 32:40].rearrange("p (s j) -> p s j", s=2),
                                    in0=difsb[:, 32:40].rearrange("p (s j) -> p s j", s=2),
                                    in1=tiew_t[:, 16:20].rearrange("p j -> p () j")
                                    .broadcast_to((128, 2, 4)),
                                    op=ALU.mult)
            with block() as b:
                @b.vector
                def _(v):
                    for sj in range(2):
                        v.tensor_reduce(out=difsb[:, 24 + sj:25 + sj],
                                        in_=difsb[:, 32 + sj * 4:32 + (sj + 1) * 4],
                                        axis=AX, op=ALU.max)
            with block() as b:
                @b.vector
                def _(v):
                    for sj in range(2):
                        v.tensor_scalar(out=difsb[:, 40 + sj * 4:40 + (sj + 1) * 4],
                                        in0=difsb[:, 32 + sj * 4:32 + (sj + 1) * 4],
                                        scalar1=difsb[:, 24 + sj:25 + sj], scalar2=None,
                                        op0=ALU.is_equal)
            with block() as b:
                @b.vector
                def _(v):
                    v.tensor_tensor(out=gmask, in0=gmask, in1=difsb[:, 40:48], op=ALU.add)
                    v.scalar_tensor_tensor(out=difsb[:, 0:8], in0=difsb[:, 40:48],
                                           scalar=-1e30, in1=difsb[:, 0:8],
                                           op0=ALU.mult, op1=ALU.add)
        with block() as b:
            @b.vector
            def _(v):
                v.tensor_copy(
                    out=difsb[:, 64:96].rearrange("p (sg j) -> p sg j", j=4),
                    in_=gmask.rearrange("p sg -> p sg ()").broadcast_to((128, 8, 4)))
        with block() as b:
            @b.vector
            def _(v):
                v.tensor_tensor(out=difsb[:, 96:128], in0=ftmp2[:, 0:32],
                                in1=difsb[:, 64:96], op=ALU.mult)
        sel = difsb[:, 128:160]
        with block() as b:
            @b.vector
            def _(v):
                v.tensor_scalar(out=sel, in0=difsb[:, 96:128], scalar1=0.0,
                                scalar2=None, op0=ALU.mult)
        for it in range(4):
            with block() as b:
                @b.vector
                def _(v):
                    for sj in range(2):
                        v.tensor_reduce(out=difsb[:, 24 + sj:25 + sj],
                                        in_=difsb[:, 96 + sj * 16:96 + (sj + 1) * 16],
                                        axis=AX, op=ALU.max)
            with block() as b:
                @b.vector
                def _(v):
                    for sj in range(2):
                        v.tensor_scalar(out=difsb[:, 160 + sj * 16:160 + (sj + 1) * 16],
                                        in0=difsb[:, 96 + sj * 16:96 + (sj + 1) * 16],
                                        scalar1=difsb[:, 24 + sj:25 + sj], scalar2=None,
                                        op0=ALU.is_equal)
            with block() as b:
                @b.vector
                def _(v):
                    v.tensor_tensor(out=difsb[:, 160:192].rearrange("p (s e) -> p s e", s=2),
                                    in0=difsb[:, 160:192].rearrange("p (s e) -> p s e", s=2),
                                    in1=tiew_t[:, 0:16].rearrange("p e -> p () e")
                                    .broadcast_to((128, 2, 16)),
                                    op=ALU.mult)
            with block() as b:
                @b.vector
                def _(v):
                    for sj in range(2):
                        v.tensor_reduce(out=difsb[:, 24 + sj:25 + sj],
                                        in_=difsb[:, 160 + sj * 16:160 + (sj + 1) * 16],
                                        axis=AX, op=ALU.max)
            with block() as b:
                @b.vector
                def _(v):
                    for sj in range(2):
                        v.tensor_scalar(out=difsb[:, 192 + sj * 16:192 + (sj + 1) * 16],
                                        in0=difsb[:, 160 + sj * 16:160 + (sj + 1) * 16],
                                        scalar1=difsb[:, 24 + sj:25 + sj], scalar2=None,
                                        op0=ALU.is_equal)
            with block() as b:
                @b.vector
                def _(v):
                    v.tensor_tensor(out=sel, in0=sel, in1=difsb[:, 192:224], op=ALU.add)
                    v.scalar_tensor_tensor(out=difsb[:, 96:128], in0=difsb[:, 192:224],
                                           scalar=-1e30, in1=difsb[:, 96:128],
                                           op0=ALU.mult, op1=ALU.add)
        if "dbg_upr" in dbg and ti == 0:
            with block() as b:
                @b.gpsimd
                def _(g):
                    gq.go(g, dbg["dbg_upr"][:, 4360:4368], gmask)
                    gq.go(g, dbg["dbg_upr"][:, 4368:4400], sel)
                    gq.go(g, dbg["dbg_upr"][:, 4400:4432], difsb[:, 64:96])
                    gq.wait(g)
        with block() as b:
            @b.vector
            def _(v):
                v.tensor_tensor(out=difsb[:, 224:256], in0=edec[:, 0:32], in1=sel,
                                op=ALU.mult)
        with block() as b:
            @b.vector
            def _(v):
                for sj in range(2):
                    v.tensor_reduce(out=difsb[:, 24 + sj:25 + sj],
                                    in_=difsb[:, 224 + sj * 16:224 + (sj + 1) * 16],
                                    axis=AX, op=ALU.add)
        with block() as b:
            @b.vector
            def _(v):
                v.tensor_scalar(out=difsb[:, 24:26], in0=difsb[:, 24:26], scalar1=1e-20,
                                scalar2=None, op0=ALU.add)
        with block() as b:
            @b.vector
            def _(v):
                v.reciprocal(out=difsb[:, 26:28], in_=difsb[:, 24:26])
        with block() as b:
            @b.vector
            def _(v):
                for sj in range(2):
                    v.tensor_scalar(out=difsb[:, 224 + sj * 16:224 + (sj + 1) * 16],
                                    in0=difsb[:, 224 + sj * 16:224 + (sj + 1) * 16],
                                    scalar1=difsb[:, 26 + sj:27 + sj], scalar2=2.5,
                                    op0=ALU.mult, op1=ALU.mult)
        if "dbg_scores" in dbg and ti == 0:
            with block() as b:
                @b.gpsimd
                def _(g):
                    gq.go(g, dbg["dbg_scores"][:, 0:32], ecolT[:, 0:32])
                    gq.go(g, dbg["dbg_scores"][0:16, 32:64].rearrange(
                        "p q -> p () q") if False else dbg["dbg_scores"][0:16, 32:64],
                        dtraw[0:16, 0:32])
                    gq.wait(g)
        if "dbg_comb" in dbg and ti == 0:
            with block() as b:
                @b.vector
                def _(v):
                    v.tensor_copy(out=ftmp[:, 0:32], in_=difsb[:, 224:256])
            with block() as b:
                @b.gpsimd
                def _(g):
                    gq.go(g, dbg["dbg_comb"][:, 0:32], ftmp[:, 0:32])
                    gq.go(g, dbg["dbg_comb"][:, 32:64], edec[:, 0:32])
                    gq.wait(g)
        with block() as b:
            @b.tensor
            def _(tt):
                for j in range(2):
                    tt.transpose(pq[0:16, j * 512:j * 512 + 128],
                                 difsb[:, 224 + j * 16:224 + (j + 1) * 16],
                                 idf[:, :])
        with block() as b:
            @b.vector
            def _(v):
                for j in range(2):
                    v.tensor_copy(out=dta[0:16, j * 128:(j + 1) * 128],
                                  in_=pq[0:16, j * 512:j * 512 + 128])
        if "dbg_scores" in dbg and ti == 0:
            with block() as b:
                @b.gpsimd
                def _(g):
                    gq.go(g, dbg["dbg_scores"][0:16, 0:32], dta[0:16, 0:32])
                    gq.wait(g)
        with block() as b:
            @b.gpsimd
            def _(g):
                gq.go(g, combuf.ap()[0:32, :].rearrange("(e a) q -> e (a q)", a=2),
                      dta[0:16, 0:TB])
                gq.wait(g)
                gq.go(g, lstk[0:1, :],
                      combuf.ap().rearrange("p q -> () (p q)"))
                gq.wait(g)
        if "dbg_scores" in dbg and ti == 0:
            with block() as b:
                @b.gpsimd
                def _(g):
                    gq.go(g, dbg["dbg_scores"][0:1, 32:64], lstk[0:1, 0:32])
                    gq.wait(g)
        def moe_down(tt, e):
            dw = DWRING[e % 3]
            for m in range(8):
                for ic in range(4):
                    tt.matmul(pf[:, m * TB:(m + 1) * TB],
                              dw[:, ic * 1024 + m * 128:ic * 1024 + (m + 1) * 128],
                              ygat[:, 1024 + ic * TB:1024 + (ic + 1) * TB],
                              start=(e == 0 and ic == 0 and m % 2 == 0),
                              stop=False)

        for e in range(16):
            half = e % 2
            goff = half * 4096
            with block() as b:
                @b.tensor
                def _(tt, goff=goff, e=e):
                    if e > 0:
                        moe_down(tt, e - 1)
                    tt.wait_ge(gq.h, moe_tgt[e])
                    for ic in range(4):
                        for k in range(8):
                            tt.matmul(pq[:, ic * TB:(ic + 1) * TB],
                                      gww[:, goff + k * 512 + ic * 128:
                                          goff + k * 512 + (ic + 1) * 128],
                                      ytfm[:, k * TB:(k + 1) * TB],
                                      start=(k == 0), stop=(k == 7))
                    for ic in range(4):
                        for k in range(8):
                            tt.matmul(pg[:, ic * TB:(ic + 1) * TB],
                                      uww[:, goff + k * 512 + ic * 128:
                                          goff + k * 512 + (ic + 1) * 128],
                                      ytfm[:, k * TB:(k + 1) * TB],
                                      start=(k == 0), stop=(k == 7))
                @b.gpsimd
                def _(g, e=e):
                    if e + 1 < 16:
                        moe_w_load(g, e + 1, (e + 1) % 2)
                        moe_tgt[e + 1] = gq.n
            with block() as b:
                @b.scalar
                def _(s):
                    s.activation(out=xts[:, 0:1024], in_=pq[:, 0:1024], func=AF.Silu)
                @b.vector
                def _(v):
                    v.tensor_copy(out=xts[:, 1024:2048], in_=pg[:, 0:1024])

            with block() as b:
                @b.tensor
                def _(tt, e=e):
                    tt.matmul(pg[:, 0:TB], onesrb[0:1, :],
                              lstk[0:1, e * TB:(e + 1) * TB], start=True, stop=True)
            with block() as b:
                @b.vector
                def _(v):
                    v.tensor_copy(out=difsb[:, 256:256 + TB], in_=pg[:, 0:TB])
                    v.tensor_tensor(
                        out=ygat[:, 0:1024].rearrange("p (i q) -> p i q", i=4),
                        in0=xts[:, 1024:2048].rearrange("p (i q) -> p i q", i=4),
                        in1=difsb[:, 256:256 + TB].rearrange("p q -> p () q")
                        .broadcast_to((128, 4, TB)),
                        op=ALU.mult)
                    v.tensor_tensor(out=ygat[:, 1024:2048], in0=xts[:, 0:1024],
                                    in1=ygat[:, 0:1024], op=ALU.mult)
        shv = shb.ap().rearrange("e (k p) f -> p (e k) f", p=128)
        with block() as b:
            @b.tensor
            def _(tt):
                moe_down(tt, 15)
            @b.gpsimd
            def _(g):
                gq.go(g, gww[:, :].rearrange("p (k f) -> p k f", k=8), shv[:, 0:8, :])
                gq.go(g, uww[:, 0:8192].rearrange("p (k f) -> p k f", k=8), shv[:, 8:16, :])
        sh_gu_tgt = gq.n
        with block() as b:
            @b.tensor
            def _(tt):
                tt.wait_ge(gq.h, sh_gu_tgt)
                for m in range(8):
                    dst = pq[:, (m % 4) * TB:(m % 4 + 1) * TB] if m < 4                         else pg[:, (m % 4) * TB:(m % 4 + 1) * TB]
                    for k in range(8):
                        tt.matmul(dst, gww[:, k * 1024 + m * 128:k * 1024 + (m + 1) * 128],
                                  ytfm[:, k * TB:(k + 1) * TB],
                                  start=(k == 0), stop=(k == 7))
            @b.gpsimd
            def _(g):
                gq.go(g, DWRING[1].rearrange("p (k f) -> p k f", k=4), shv[:, 16:20, :])
                gq.go(g, DWRING[2].rearrange("p (k f) -> p k f", k=4), shv[:, 20:24, :])
        sh_d_tgt = gq.n
        with block() as b:
            @b.scalar
            def _(s):
                s.activation(out=xts[:, 0:1024], in_=pq[:, 0:1024], func=AF.Silu)
                s.activation(out=xts[:, 1024:2048], in_=pg[:, 0:1024], func=AF.Silu)
        with block() as b:
            @b.tensor
            def _(tt):
                for m in range(8):
                    dst = pq[:, (m % 4) * TB:(m % 4 + 1) * TB] if m < 4                         else pg[:, (m % 4) * TB:(m % 4 + 1) * TB]
                    for k in range(8):
                        tt.matmul(dst, uww[:, k * 1024 + m * 128:k * 1024 + (m + 1) * 128],
                                  ytfm[:, k * TB:(k + 1) * TB],
                                  start=(k == 0), stop=(k == 7))
        with block() as b:
            @b.vector
            def _(v):
                v.tensor_tensor(out=hstb[:, 0:1024], in0=xts[:, 0:1024], in1=pq[:, 0:1024],
                                op=ALU.mult)
                v.tensor_tensor(out=hstb[:, 1024:2048], in0=xts[:, 1024:2048],
                                in1=pg[:, 0:1024], op=ALU.mult)
        with block() as b:
            @b.tensor
            def _(tt):
                tt.wait_ge(gq.h, sh_d_tgt)
                for m in range(8):
                    for k in range(8):
                        dw = DWRING[1] if k < 4 else DWRING[2]
                        tt.matmul(pf[:, m * TB:(m + 1) * TB],
                                  dw[:, (k % 4) * 1024 + m * 128:(k % 4) * 1024 + (m + 1) * 128],
                                  hstb[:, k * TB:(k + 1) * TB],
                                  start=False, stop=(k == 7))
        with block() as b:
            @b.scalar
            def _(s):
                s.activation(out=hstb[:, :], in_=pf[:, 0:2048], func=AF.Silu)
        if "dbg_g" in dbg:
            with block() as b:
                @b.gpsimd
                def _(g, tok0=tok0):
                    gq.go(g, dbg["dbg_g"][:, :].rearrange("p (k q) -> p k q", k=8)
                          [:, :, tok0:tok0 + TB],
                          hstb[:, :].rearrange("p (k q) -> p k q", k=8))
                    gq.wait(g)
        with block() as b:
            @b.vector
            def _(v):
                v.tensor_tensor(out=ytmp2[:, :], in0=yrev[:, :], in1=hstb[:, :], op=ALU.mult)
        with block() as b:
            @b.vector
            def _(v):
                for k in range(8):
                    v.tensor_tensor(out=xpart(k)[:, 0:TB], in0=xpart(k)[:, 0:TB],
                                    in1=ytmp2[:, k * TB:(k + 1) * TB], op=ALU.add)
        with block() as b:
            @b.gpsimd
            def _(g, tok0=tok0):
                for k in range(8):
                    gq.go(g, x1d.ap().rearrange("p (k q) -> p k q", k=8)
                          [:, k, tok0:tok0 + TB], xpart(k)[:, 0:TB])
                gq.wait(g)
        if "dbg_x1" in dbg:
            with block() as b:
                @b.gpsimd
                def _(g, tok0=tok0):
                    for k in range(8):
                        gq.go(g, dbg["dbg_x1"][:, :].rearrange("p (k q) -> p k q", k=8)
                              [:, k, tok0:tok0 + TB], xpart(k)[:, 0:TB])
                    gq.wait(g)
        with block() as b:
            @b.vector
            def _(v):
                for k in range(8):
                    v.tensor_copy(out=expsb[:, k * TB:(k + 1) * TB], in_=xpart(k)[:, 0:TB])
            @b.scalar
            def _(s):
                for k in range(8):
                    s.activation(out=hstb[:, k * TB:(k + 1) * TB], in_=xpart(k)[:, 0:TB],
                                 func=AF.Square)
        with block() as b:
            @b.tensor
            def _(tt):
                for k in range(8):
                    tt.matmul(pg[0:1, 0:TB], onesb[:, 0:1], expsb[:, k * TB:(k + 1) * TB],
                              start=(k == 0), stop=(k == 7))
                for k in range(8):
                    tt.matmul(pg[0:1, 512:512 + TB], onesb[:, 0:1],
                              hstb[:, k * TB:(k + 1) * TB],
                              start=(k == 0), stop=(k == 7))
            @b.gpsimd
            def _(g):
                moe_w_load_gu(g, 0, 0)
        with block() as b:
            @b.vector
            def _(v):
                v.tensor_scalar(out=mrow[0:1, 0:TB], in0=pg[0:1, 0:TB], scalar1=1.0 / 1024,
                                scalar2=None, op0=ALU.mult)
                v.tensor_scalar(out=e2row[0:1, 0:TB], in0=pg[0:1, 512:512 + TB],
                                scalar1=1.0 / 1024, scalar2=None, op0=ALU.mult)
        with block() as b:
            @b.vector
            def _(v):
                v.tensor_tensor(out=vrow[0:1, 0:TB], in0=mrow[0:1, 0:TB],
                                in1=mrow[0:1, 0:TB], op=ALU.mult)
        with block() as b:
            @b.vector
            def _(v):
                v.tensor_tensor(out=vrow2[0:1, 0:TB], in0=e2row[0:1, 0:TB],
                                in1=vrow[0:1, 0:TB], op=ALU.subtract)
        with block() as b:
            @b.scalar
            def _(s):
                s.activation(out=vrow[0:1, 0:TB], in_=vrow2[0:1, 0:TB], func=AF.Ln,
                             bias=epsc[0:1, 0:1])
        with block() as b:
            @b.scalar
            def _(s):
                s.activation(out=rstdr[0:1, 0:TB], in_=vrow[0:1, 0:TB], func=AF.Exp,
                             scale=-0.5)
        with block() as b:
            @b.tensor
            def _(tt):
                tt.matmul(pq[:, 0:TB], onesf[0:1, :], mrow[0:1, 0:TB], start=True, stop=True)
                tt.matmul(pq[:, 512:512 + TB], onesf[0:1, :], rstdr[0:1, 0:TB],
                          start=True, stop=True)
        with block() as b:
            @b.vector
            def _(v):
                for k in range(8):
                    v.tensor_tensor(out=difsb[:, k * TB:(k + 1) * TB],
                                    in0=xpart(k)[:, 0:TB], in1=pq[:, 0:TB], op=ALU.subtract)
        with block() as b:
            @b.vector
            def _(v):
                for k in range(8):
                    v.tensor_tensor(out=ytfm[:, k * TB:(k + 1) * TB],
                                    in0=difsb[:, k * TB:(k + 1) * TB],
                                    in1=pq[:, 512:512 + TB], op=ALU.mult)
        with block() as b:
            @b.gpsimd
            def _(g, tok0=tok0):
                gq.go(g, xn2d.ap().rearrange("p (k q) -> p k q", k=8)
                      [:, :, tok0:tok0 + TB],
                      ytfm[:, :].rearrange("p (k q) -> p k q", k=8))
                gq.wait(g)
        if "dbg_xn2" in dbg:
            with block() as b:
                @b.gpsimd
                def _(g, tok0=tok0):
                    gq.go(g, dbg["dbg_xn2"][:, :].rearrange("p (k q) -> p k q", k=8)
                          [:, :, tok0:tok0 + TB],
                          ytfm[:, :].rearrange("p (k q) -> p k q", k=8))
                    gq.wait(g)
        with block() as b:
            @b.vector
            def _(v):
                v.tensor_reduce(out=rstdc[:, :], in_=ytfm[:, :], axis=AX, op=ALU.max,
                                apply_absolute_value=True)
        with block() as b:
            @b.vector
            def _(v):
                v.tensor_tensor(out=rmscol[:, :], in0=rmscol[:, :], in1=rstdc[:, :],
                                op=ALU.max)

    itile = ytmp[:, :].bitcast(I32)
    with block() as b:
        @b.gpsimd
        def _(g):
            gq.go(g, gww[:, :].rearrange("p (k f) -> p k f", k=8),
                  wvb.ap().rearrange("(k p) f -> p k f", p=128))
            g.memset(edec[:, 0:8], 0.0)
            g.memset(lendr[:, :], 0.0)
            gq.wait(g)
    for ti in range(8):
        tok0 = ti * 256
        with block() as b:
            @b.gpsimd
            def _(g, tok0=tok0):
                gq.go(g, ytfm[:, :].rearrange("p (k q) -> p k q", k=8),
                      xn2d.ap().rearrange("p (k q) -> p k q", k=8)[:, :, tok0:tok0 + 256])
                gq.wait(g)
        with block() as b:
            @b.tensor
            def _(tt):
                for m in range(8):
                    dst = pq[:, (m % 4) * 256:(m % 4 + 1) * 256] if m < 4 \
                        else pg[:, (m % 4) * 256:(m % 4 + 1) * 256]
                    for k in range(8):
                        tt.matmul(dst, gww[:, k * 1024 + m * 128:k * 1024 + (m + 1) * 128],
                                  ytfm[:, k * 256:(k + 1) * 256],
                                  start=(k == 0), stop=(k == 7))
        with block() as b:
            @b.vector
            def _(v):
                for m in range(8):
                    src = pq[:, (m % 4) * 256:(m % 4 + 1) * 256] if m < 4 \
                        else pg[:, (m % 4) * 256:(m % 4 + 1) * 256]
                    v.tensor_reduce(out=difsb[:, 24 + m:25 + m], in_=src, axis=AX,
                                    op=ALU.add)
        with block() as b:
            @b.vector
            def _(v):
                for m in range(8):
                    v.tensor_tensor(out=edec[:, m:m + 1], in0=edec[:, m:m + 1],
                                    in1=difsb[:, 24 + m:25 + m], op=ALU.add)
    with block() as b:
        @b.gpsimd
        def _(g):
            gq.go(g, vsin[:, :], edec[:, 0:8])
            gq.wait(g)
            g.collective_compute(
                "AllReduce", ALU.add,
                replica_groups=[[0, 1], [2, 3], [4, 5], [6, 7]],
                ins=[vsin.ap().opt()], outs=[vsout.ap().opt()],
            ).then_inc(ccs)
            cc_n[0] += 1
            g.wait_ge(ccs, cc_n[0])
            gq.go(g, edec[:, 8:16], vsout[:, :])
            gq.wait(g)
    with block() as b:
        @b.vector
        def _(v):
            v.tensor_copy(out=ffac[:, 0:8], in_=edec[:, 8:16])
    with block() as b:
        @b.gpsimd
        def _(g):
            gq.go(g, gww[:, :].rearrange("p (k f) -> p k f", k=8),
                  wob.ap().rearrange("(k p) f -> p k f", p=128))
            gq.wait(g)
    with block() as b:
        @b.tensor
        def _(tt):
            for m in range(8):
                for k in range(8):
                    tt.matmul(pq[:, m:m + 1],
                              gww[:, k * 1024 + m * 128:k * 1024 + (m + 1) * 128],
                              ffac[:, k:k + 1], start=(k == 0), stop=(k == 7))
    with block() as b:
        @b.vector
        def _(v):
            v.tensor_copy(out=ecolT[:, 0:8], in_=pq[:, 0:8])
    if "dbg_s1" in dbg:
        with block() as b:
            @b.gpsimd
            def _(g):
                gq.go(g, dbg["dbg_s1"][:, :], ecolT[:, 0:8])
                gq.wait(g)
    with block() as b:
        @b.tensor
        def _(tt):
            tt.transpose(pg[0:1, 0:128], rmscol[:, :], idf[:, :])
    with block() as b:
        @b.vector
        def _(v):
            v.tensor_copy(out=lendr[0:1, 0:16].rearrange("a (b q) -> a b q", b=1)
                          if False else lendE[0:1, 0:32], in_=pg[0:1, 0:32])
    with block() as b:
        @b.vector
        def _(v):
            v.tensor_copy(out=dtraw[0:1, 0:128], in_=pg[0:1, 0:128])
    with block() as b:
        @b.vector
        def _(v):
            v.tensor_reduce(out=lendr[0:1, 0:1], in_=dtraw[0:1, 0:128], axis=AX, op=ALU.max)
    with block() as b:
        @b.gpsimd
        def _(g):
            g.memset(dtraw[0:1, 128:144], 0.0)
    with block() as b:
        @b.vector
        def _(v):
            v.tensor_copy(out=dtraw[0:1, 128:129], in_=lendr[0:1, 0:1])
    with block() as b:
        @b.gpsimd
        def _(g):
            gq.go(g, armx[0:1, 0:16], dtraw[0:1, 128:144])
            gq.wait(g)
            g.collective_compute(
                "AllReduce", ALU.max, replica_groups=[list(range(8))],
                ins=[armx.ap().opt()], outs=[armxo.ap().opt()],
            ).then_inc(ccs)
            cc_n[0] += 1
            g.wait_ge(ccs, cc_n[0])
            gq.go(g, lendr[0:1, 0:16], armxo[0:1, 0:16])
            gq.wait(g)
    def make_scale(src_ap, dst_off):
        with block() as b:
            @b.vector
            def _(v):
                v.tensor_scalar(out=lendE[0:1, dst_off:dst_off + 1], in0=src_ap,
                                scalar1=1.0 / 32767.0, scalar2=1e-12,
                                op0=ALU.mult, op1=ALU.max)
        with block() as b:
            @b.vector
            def _(v):
                v.reciprocal(out=lendE[0:1, dst_off + 1:dst_off + 2],
                             in_=lendE[0:1, dst_off:dst_off + 1])
    make_scale(lendr[0:1, 0:1], 0)
    with block() as b:
        @b.tensor
        def _(tt):
            tt.matmul(pq[:, 0:2], onesf[0:1, :], lendE[0:1, 0:2], start=True, stop=True)
    with block() as b:
        @b.vector
        def _(v):
            v.tensor_copy(out=ftmp[:, 0:2], in_=pq[:, 0:2])

    def dequant(src_rows, n_iter, dstT, dst_row_of, dst_col_of):
        wring = [difsb[:, 0:1024], difsb[:, 1024:2048], ytok[:, 0:1024],
                 ytok[:, 1024:2048], hstf[:, 0:1024]]
        mxring = [gtile[:, 0:32], gtile[:, 32:64], gtile[:, 64:96]]
        scring = [ytmp[:, 1120:1152], ytmp[:, 1152:1184], ytmp[:, 1184:1216]]
        ivring = [ytmp[:, 1024:1056], ytmp[:, 1056:1088], ytmp[:, 1088:1120]]
        wqring = [ytfm[:, 0:1024], ytfm[:, 1024:2048]]
        evring = [ygat[:, 0:1024], ygat[:, 1024:2048]]
        iscr = ytmp[:, :].bitcast(I32)[:, 0:1024]
        for j in range(n_iter + 8):
            with block() as b:
                @b.gpsimd
                def _(g, j=j):
                    if j < n_iter:
                        gq.go(g, wring[j % 5], src_rows(j))
                    i = j - 7
                    if 0 <= i < n_iter:
                        r0 = dst_row_of(i, 0)
                        gq.go(g, dstT[r0:r0 + 1024,
                                      dst_col_of(i):dst_col_of(i) + 128]
                              .rearrange("(kc p) c -> p kc c", p=128),
                              evring[i % 2][:, :].rearrange("p (kc q) -> p kc q", kc=8))
                    gq.wait(g)
                @b.vector
                def _(v, j=j):
                    i = j - 1
                    if 0 <= i < n_iter:
                        v.tensor_reduce(
                            out=mxring[i % 3].rearrange("p (a q) -> p a q", q=1),
                            in_=wring[i % 5].rearrange("p (a q) -> p a q", q=32),
                            axis=AX, op=ALU.max, apply_absolute_value=True)
                    i = j - 2
                    if 0 <= i < n_iter:
                        v.tensor_scalar(out=scring[i % 3], in0=mxring[i % 3],
                                        scalar1=1.0 / 7.0, scalar2=1e-12,
                                        op0=ALU.mult, op1=ALU.max)
                    i = j - 3
                    if 0 <= i < n_iter:
                        v.reciprocal(out=ivring[i % 3], in_=scring[i % 3])
                    i = j - 4
                    if 0 <= i < n_iter:
                        w = wring[i % 5]
                        v.tensor_tensor(
                            out=w.rearrange("p (a q) -> p a q", q=32),
                            in0=w.rearrange("p (a q) -> p a q", q=32),
                            in1=ivring[i % 3].rearrange("p a -> p a ()")
                            .broadcast_to((128, 32, 32)), op=ALU.mult)
                        v.tensor_copy(out=iscr, in_=w)
                        v.tensor_copy(out=w, in_=iscr)
                        v.tensor_tensor(
                            out=wqring[i % 2].rearrange("p (a q) -> p a q", q=32),
                            in0=w.rearrange("p (a q) -> p a q", q=32),
                            in1=scring[i % 3].rearrange("p a -> p a ()")
                            .broadcast_to((128, 32, 32)), op=ALU.mult)
                    i = j - 6
                    if 0 <= i < n_iter:
                        v.tensor_copy(out=evring[i % 2],
                                      in_=pgb[:, (i % 2) * 1024:(i % 2) * 1024 + 1024])
                @b.tensor
                def _(tt, j=j):
                    i = j - 5
                    if 0 <= i < n_iter:
                        for kc in range(8):
                            tt.transpose(pgb[:, (i % 2) * 1024 + kc * 128:
                                             (i % 2) * 1024 + (kc + 1) * 128],
                                         wqring[i % 2][:, kc * 128:(kc + 1) * 128],
                                         idb[:, :])

    dequant(lambda r: d["wup"].ap()[r * 128:(r + 1) * 128, :], 24, wuqT,
            lambda r, kc: kc * 128, lambda r: r * 128)
    dequant(lambda r: d["wgate"].ap()[r * 128:(r + 1) * 128, :], 24, wgqT,
            lambda r, kc: kc * 128, lambda r: r * 128)
    dequant(lambda r: d["wdown"].ap()[(r // 3) * 128:(r // 3 + 1) * 128,
                                      (r % 3) * 1024:(r % 3 + 1) * 1024], 24, wdqT,
            lambda r, kc: (r % 3) * 1024 + kc * 128, lambda r: (r // 3) * 128)

    with block() as b:
        @b.gpsimd
        def _(g):
            g.memset(ftmp2[:, 0:4], 0.0)
    for ti in range(8):
        tok0 = ti * 256
        with block() as b:
            @b.gpsimd
            def _(g, tok0=tok0):
                gq.go(g, ytfm[:, :].rearrange("p (k q) -> p k q", k=8),
                      xn2d.ap().rearrange("p (k q) -> p k q", k=8)[:, :, tok0:tok0 + 256])
                gq.wait(g)
        with block() as b:
            @b.vector
            def _(v):
                v.tensor_scalar(out=difsb[:, 0:2048], in0=ytfm[:, :],
                                scalar1=ftmp[:, 1:2], scalar2=None, op0=ALU.mult)
                v.tensor_copy(out=itile[:, 0:2048], in_=difsb[:, 0:2048])
                v.tensor_copy(out=difsb[:, 0:2048], in_=itile[:, 0:2048])
                v.tensor_scalar(out=ytfm[:, :], in0=difsb[:, 0:2048],
                                scalar1=ftmp[:, 0:1], scalar2=None, op0=ALU.mult)
        steps = [(srcT, dst_t, mg)
                 for srcT, dst_t in [(wuqT, xbcraw), (wgqT, siluz)] for mg in range(6)]
        wring2 = [wslice[:, 0:4096], wslice[:, 4096:8192]]
        for j in range(len(steps) + 2):
            with block() as b:
                @b.gpsimd
                def _(g, j=j):
                    if j < len(steps):
                        srcT, _, mg = steps[j]
                        gq.go(g, wring2[j % 2].rearrange("p (k c) -> p k c", k=8),
                              srcT.ap().rearrange("(k p) c -> p k c", p=128)
                              [:, :, mg * 512:(mg + 1) * 512])
                    gq.wait(g)
                @b.tensor
                def _(tt, j=j):
                    i = j - 1
                    if 0 <= i < len(steps):
                        ps_ = pq if i % 2 == 0 else pg
                        wr = wring2[i % 2]
                        for mi in range(4):
                            for k in range(8):
                                tt.matmul(ps_[:, mi * 256:(mi + 1) * 256],
                                          wr[:, k * 512 + mi * 128:k * 512 + (mi + 1) * 128],
                                          ytfm[:, k * 256:(k + 1) * 256],
                                          start=(k == 0), stop=(k == 7))
                @b.vector
                def _(v, j=j):
                    i = j - 2
                    if 0 <= i < len(steps):
                        _, dst_t, mg = steps[i]
                        ps_ = pq if i % 2 == 0 else pg
                        v.tensor_copy(out=dst_t[:, mg * 1024:(mg + 1) * 1024],
                                      in_=ps_[:, 0:1024])
        if "dbg_upr" in dbg and ti == 0:
            with block() as b:
                @b.gpsimd
                def _(g):
                    gq.go(g, dbg["dbg_upr"][:, 0:6144], xbcraw[:, 0:6144])
                    gq.wait(g)
        with block() as b:
            @b.vector
            def _(v):
                v.tensor_reduce(out=difsb[:, 24:25], in_=xbcraw[:, 0:6144], axis=AX,
                                op=ALU.max, apply_absolute_value=True)
                v.tensor_reduce(out=difsb[:, 25:26], in_=siluz[:, 0:6144], axis=AX,
                                op=ALU.max, apply_absolute_value=True)
        with block() as b:
            @b.vector
            def _(v):
                v.tensor_tensor(out=ftmp2[:, 0:2], in0=ftmp2[:, 0:2],
                                in1=difsb[:, 24:26], op=ALU.max)
        with block() as b:
            @b.gpsimd
            def _(g, tok0=tok0):
                gq.go(g, upraw.ap().rearrange("p (m q) -> p m q", m=24)
                      [:, :, tok0:tok0 + 256],
                      xbcraw[:, 0:6144].rearrange("p (m q) -> p m q", m=24))
                gq.go(g, gateraw.ap().rearrange("p (m q) -> p m q", m=24)
                      [:, :, tok0:tok0 + 256],
                      siluz[:, 0:6144].rearrange("p (m q) -> p m q", m=24))
                gq.wait(g)
    with block() as b:
        @b.tensor
        def _(tt):
            tt.transpose(pg[0:2, 0:128], ftmp2[:, 0:2], idf[:, :])
    with block() as b:
        @b.vector
        def _(v):
            v.tensor_copy(out=dtraw[0:2, 0:128], in_=pg[0:2, 0:128])
    with block() as b:
        @b.vector
        def _(v):
            v.tensor_reduce(out=dtraw[0:2, 200:201], in_=dtraw[0:2, 0:128], axis=AX,
                            op=ALU.max)
    with block() as b:
        @b.gpsimd
        def _(g):
            g.memset(dtraw[0:1, 128:144], 0.0)
    with block() as b:
        @b.gpsimd
        def _(g):
            gq.go(g, armx[0:1, 0:16], dtraw[0:1, 128:144])
            gq.wait(g)
            gq.go(g, armx[0:1, 0:1], dtraw[0:1, 200:201])
            gq.go(g, armx[0:1, 1:2], dtraw[1:2, 200:201])
            gq.wait(g)
            g.collective_compute(
                "AllReduce", ALU.max, replica_groups=[list(range(8))],
                ins=[armx.ap().opt()], outs=[armxo.ap().opt()],
            ).then_inc(ccs)
            cc_n[0] += 1
            g.wait_ge(ccs, cc_n[0])
            gq.go(g, lendr[0:1, 8:10], armxo[0:1, 0:2])
            gq.wait(g)
    make_scale(lendr[0:1, 8:9], 4)
    make_scale(lendr[0:1, 9:10], 6)
    with block() as b:
        @b.tensor
        def _(tt):
            tt.matmul(pq[:, 4:8], onesf[0:1, :], lendE[0:1, 4:8], start=True, stop=True)
    with block() as b:
        @b.vector
        def _(v):
            v.tensor_copy(out=ftmp[:, 4:8], in_=pq[:, 4:8])
    with block() as b:
        @b.gpsimd
        def _(g):
            g.memset(ftmp2[:, 2:4], 0.0)
    for ti in range(8):
        tok0 = ti * 256
        with block() as b:
            @b.gpsimd
            def _(g, tok0=tok0):
                gq.go(g, siluz[:, 0:6144].rearrange("p (m q) -> p m q", m=24),
                      gateraw.ap().rearrange("p (m q) -> p m q", m=24)
                      [:, :, tok0:tok0 + 256])
                gq.wait(g)
        for ch in range(3):
            co = ch * 2048
            with block() as b:
                @b.vector
                def _(v, co=co):
                    v.tensor_scalar(out=difsb[:, 0:2048], in0=siluz[:, co:co + 2048],
                                    scalar1=ftmp[:, 7:8], scalar2=None, op0=ALU.mult)
                    v.tensor_copy(out=itile[:, 0:2048], in_=difsb[:, 0:2048])
                    v.tensor_copy(out=difsb[:, 0:2048], in_=itile[:, 0:2048])
                    v.tensor_scalar(out=siluz[:, co:co + 2048], in0=difsb[:, 0:2048],
                                    scalar1=ftmp[:, 6:7], scalar2=None, op0=ALU.mult)
            with block() as b:
                @b.scalar
                def _(sx, co=co):
                    sx.activation(out=difsb[:, 0:2048], in_=siluz[:, co:co + 2048],
                                  func=AF.Sigmoid)
            with block() as b:
                @b.vector
                def _(v, co=co):
                    v.tensor_scalar(out=ytok[:, 0:2048], in0=difsb[:, 0:2048],
                                    scalar1=65536.0, scalar2=None, op0=ALU.mult)
                    v.tensor_copy(out=itile[:, 0:2048], in_=ytok[:, 0:2048])
                    v.tensor_copy(out=ytok[:, 0:2048], in_=itile[:, 0:2048])
                    v.tensor_scalar(out=ytok[:, 0:2048], in0=ytok[:, 0:2048],
                                    scalar1=65535.0, scalar2=1.0 / 65536.0,
                                    op0=ALU.min, op1=ALU.mult)
                    v.tensor_tensor(out=siluz[:, co:co + 2048], in0=siluz[:, co:co + 2048],
                                    in1=ytok[:, 0:2048], op=ALU.mult)
        with block() as b:
            @b.vector
            def _(v):
                v.tensor_reduce(out=difsb[:, 24:25], in_=siluz[:, 0:6144], axis=AX,
                                op=ALU.max, apply_absolute_value=True)
        with block() as b:
            @b.vector
            def _(v):
                v.tensor_tensor(out=ftmp2[:, 2:3], in0=ftmp2[:, 2:3],
                                in1=difsb[:, 24:25], op=ALU.max)
        with block() as b:
            @b.gpsimd
            def _(g, tok0=tok0):
                gq.go(g, gate2d.ap().rearrange("p (m q) -> p m q", m=24)
                      [:, :, tok0:tok0 + 256],
                      siluz[:, 0:6144].rearrange("p (m q) -> p m q", m=24))
                gq.wait(g)
    with block() as b:
        @b.tensor
        def _(tt):
            tt.transpose(pg[0:1, 0:128], ftmp2[:, 2:3], idf[:, :])
    with block() as b:
        @b.vector
        def _(v):
            v.tensor_copy(out=dtraw[0:1, 0:128], in_=pg[0:1, 0:128])
    with block() as b:
        @b.vector
        def _(v):
            v.tensor_reduce(out=lendr[0:1, 4:5], in_=dtraw[0:1, 0:128], axis=AX, op=ALU.max)
    with block() as b:
        @b.gpsimd
        def _(g):
            g.memset(dtraw[0:1, 128:144], 0.0)
    with block() as b:
        @b.vector
        def _(v):
            v.tensor_copy(out=dtraw[0:1, 128:129], in_=lendr[0:1, 4:5])
    with block() as b:
        @b.gpsimd
        def _(g):
            gq.go(g, armx[0:1, 0:16], dtraw[0:1, 128:144])
            gq.wait(g)
            g.collective_compute(
                "AllReduce", ALU.max, replica_groups=[list(range(8))],
                ins=[armx.ap().opt()], outs=[armxo.ap().opt()],
            ).then_inc(ccs)
            cc_n[0] += 1
            g.wait_ge(ccs, cc_n[0])
            gq.go(g, lendr[0:1, 10:11], armxo[0:1, 0:1])
            gq.wait(g)
    make_scale(lendr[0:1, 10:11], 8)
    with block() as b:
        @b.tensor
        def _(tt):
            tt.matmul(pq[:, 8:10], onesf[0:1, :], lendE[0:1, 8:10], start=True, stop=True)
    with block() as b:
        @b.vector
        def _(v):
            v.tensor_copy(out=ftmp[:, 8:10], in_=pq[:, 8:10])
    with block() as b:
        @b.gpsimd
        def _(g):
            g.memset(ftmp2[:, 3:4], 0.0)
    for ti in range(8):
        tok0 = ti * 256
        with block() as b:
            @b.gpsimd
            def _(g, tok0=tok0):
                gq.go(g, siluz[:, 0:6144].rearrange("p (m q) -> p m q", m=24),
                      gate2d.ap().rearrange("p (m q) -> p m q", m=24)
                      [:, :, tok0:tok0 + 256])
                gq.go(g, xbcraw[:, 0:6144].rearrange("p (m q) -> p m q", m=24),
                      upraw.ap().rearrange("p (m q) -> p m q", m=24)
                      [:, :, tok0:tok0 + 256])
                gq.wait(g)
        for ch in range(3):
            co = ch * 2048
            with block() as b:
                @b.vector
                def _(v, co=co):
                    v.tensor_scalar(out=difsb[:, 0:2048], in0=siluz[:, co:co + 2048],
                                    scalar1=ftmp[:, 9:10], scalar2=None, op0=ALU.mult)
                    v.tensor_copy(out=itile[:, 0:2048], in_=difsb[:, 0:2048])
                    v.tensor_copy(out=difsb[:, 0:2048], in_=itile[:, 0:2048])
                    v.tensor_scalar(out=siluz[:, co:co + 2048], in0=difsb[:, 0:2048],
                                    scalar1=ftmp[:, 8:9], scalar2=None, op0=ALU.mult)
                    v.tensor_scalar(out=difsb[:, 0:2048], in0=xbcraw[:, co:co + 2048],
                                    scalar1=ftmp[:, 5:6], scalar2=None, op0=ALU.mult)
                    v.tensor_copy(out=itile[:, 0:2048], in_=difsb[:, 0:2048])
                    v.tensor_copy(out=difsb[:, 0:2048], in_=itile[:, 0:2048])
                    v.tensor_scalar(out=xbcraw[:, co:co + 2048], in0=difsb[:, 0:2048],
                                    scalar1=ftmp[:, 4:5], scalar2=None, op0=ALU.mult)
                    v.tensor_tensor(out=siluz[:, co:co + 2048], in0=siluz[:, co:co + 2048],
                                    in1=xbcraw[:, co:co + 2048], op=ALU.mult)
        with block() as b:
            @b.vector
            def _(v):
                v.tensor_reduce(out=difsb[:, 24:25], in_=siluz[:, 0:6144], axis=AX,
                                op=ALU.max, apply_absolute_value=True)
        with block() as b:
            @b.vector
            def _(v):
                v.tensor_tensor(out=ftmp2[:, 3:4], in0=ftmp2[:, 3:4],
                                in1=difsb[:, 24:25], op=ALU.max)
        with block() as b:
            @b.gpsimd
            def _(g, tok0=tok0):
                gq.go(g, od.ap().rearrange("p (m q) -> p m q", m=24)
                      [:, :, tok0:tok0 + 256],
                      siluz[:, 0:6144].rearrange("p (m q) -> p m q", m=24))
                gq.wait(g)
    with block() as b:
        @b.tensor
        def _(tt):
            tt.transpose(pg[0:1, 0:128], ftmp2[:, 3:4], idf[:, :])
    with block() as b:
        @b.vector
        def _(v):
            v.tensor_copy(out=dtraw[0:1, 0:128], in_=pg[0:1, 0:128])
    with block() as b:
        @b.vector
        def _(v):
            v.tensor_reduce(out=lendr[0:1, 5:6], in_=dtraw[0:1, 0:128], axis=AX, op=ALU.max)
    with block() as b:
        @b.gpsimd
        def _(g):
            g.memset(dtraw[0:1, 128:144], 0.0)
    with block() as b:
        @b.vector
        def _(v):
            v.tensor_copy(out=dtraw[0:1, 128:129], in_=lendr[0:1, 5:6])
    with block() as b:
        @b.gpsimd
        def _(g):
            gq.go(g, armx[0:1, 0:16], dtraw[0:1, 128:144])
            gq.wait(g)
            g.collective_compute(
                "AllReduce", ALU.max, replica_groups=[list(range(8))],
                ins=[armx.ap().opt()], outs=[armxo.ap().opt()],
            ).then_inc(ccs)
            cc_n[0] += 1
            g.wait_ge(ccs, cc_n[0])
            gq.go(g, lendr[0:1, 11:12], armxo[0:1, 0:1])
            gq.wait(g)
    make_scale(lendr[0:1, 11:12], 10)
    with block() as b:
        @b.tensor
        def _(tt):
            tt.matmul(pq[:, 10:12], onesf[0:1, :], lendE[0:1, 10:12], start=True, stop=True)
    with block() as b:
        @b.vector
        def _(v):
            v.tensor_copy(out=ftmp[:, 10:12], in_=pq[:, 10:12])
    if False:
        pass
    for ti in range(8):
        tok0 = ti * 256
        with block() as b:
            @b.gpsimd
            def _(g, tok0=tok0):
                gq.go(g, siluz[:, 0:6144].rearrange("p (m q) -> p m q", m=24),
                      od.ap().rearrange("p (m q) -> p m q", m=24)[:, :, tok0:tok0 + 256])
                gq.wait(g)
        if "dbg_xbc" in dbg and ti == 0:
            with block() as b:
                @b.gpsimd
                def _(g):
                    gq.go(g, dbg["dbg_xbc"][:, 0:6144], siluz[:, 0:6144])
                    gq.wait(g)
        for ch in range(3):
            co = ch * 2048
            with block() as b:
                @b.vector
                def _(v, co=co):
                    v.tensor_scalar(out=difsb[:, 0:2048], in0=siluz[:, co:co + 2048],
                                    scalar1=ftmp[:, 11:12], scalar2=None, op0=ALU.mult)
                    v.tensor_copy(out=itile[:, 0:2048], in_=difsb[:, 0:2048])
                    v.tensor_copy(out=difsb[:, 0:2048], in_=itile[:, 0:2048])
                    v.tensor_scalar(out=siluz[:, co:co + 2048], in0=difsb[:, 0:2048],
                                    scalar1=ftmp[:, 10:11], scalar2=None, op0=ALU.mult)
        for kg in range(3):
            with block() as b:
                @b.gpsimd
                def _(g, kg=kg):
                    gq.go(g, xbcraw[:, 0:8192].rearrange("p (k f) -> p k f", k=8),
                          wdqT.ap().rearrange("(k p) f -> p k f", p=128)
                          [:, kg * 8:(kg + 1) * 8, :])
                    gq.wait(g)
            with block() as b:
                @b.tensor
                def _(tt, kg=kg):
                    for m in range(8):
                        dst = pq[:, (m % 4) * 256:(m % 4 + 1) * 256] if m < 4 \
                            else pg[:, (m % 4) * 256:(m % 4 + 1) * 256]
                        for k in range(8):
                            tt.matmul(dst,
                                      xbcraw[:, k * 1024 + m * 128:k * 1024 + (m + 1) * 128],
                                      siluz[:, (kg * 8 + k) * 256:(kg * 8 + k + 1) * 256],
                                      start=(kg == 0 and k == 0 and m % 2 == 0),
                                      stop=(kg == 2 and k == 7))
        with block() as b:
            @b.scalar
            def _(sx):
                sx.activation(out=hstb[:, 0:1024], in_=pq[:, 0:1024], func=AF.Silu)
                sx.activation(out=hstb[:, 1024:2048], in_=pg[:, 0:1024], func=AF.Silu)
        if "dbg_ffn" in dbg:
            with block() as b:
                @b.vector
                def _(v):
                    v.tensor_copy(out=ytok[:, 0:1024], in_=pq[:, 0:1024])
                    v.tensor_copy(out=ytok[:, 1024:2048], in_=pg[:, 0:1024])
            with block() as b:
                @b.gpsimd
                def _(g, tok0=tok0):
                    gq.go(g, dbg["dbg_ffn"][:, :].rearrange("p (k q) -> p k q", k=8)
                          [:, :, tok0:tok0 + 256],
                          ytok[:, 0:2048].rearrange("p (k q) -> p k q", k=8))
                    gq.wait(g)
        with block() as b:
            @b.gpsimd
            def _(g, tok0=tok0):
                gq.go(g, difsb[:, 0:2048].rearrange("p (k q) -> p k q", k=8),
                      x1d.ap().rearrange("p (k q) -> p k q", k=8)[:, :, tok0:tok0 + 256])
                gq.wait(g)
        with block() as b:
            @b.vector
            def _(v):
                for k in range(8):
                    v.tensor_scalar(out=ygat[:, k * 256:(k + 1) * 256],
                                    in0=hstb[:, k * 256:(k + 1) * 256],
                                    scalar1=ecolT[:, k:k + 1], scalar2=None, op0=ALU.mult)
        with block() as b:
            @b.vector
            def _(v):
                v.tensor_tensor(out=difsb[:, 0:2048], in0=difsb[:, 0:2048],
                                in1=ygat[:, 0:2048], op=ALU.add)
        with block() as b:
            @b.gpsimd
            def _(g, tok0=tok0):
                gq.go(g, out.ap().rearrange("(k p) q -> p k q", p=128)
                      [:, :, tok0:tok0 + 256],
                      difsb[:, 0:2048].rearrange("p (k q) -> p k q", k=8))
                gq.wait(g)
    if "dbg_mrev" in dbg:
        with block() as b:
            @b.gpsimd
            def _(g):
                for rr in range(4):
                    gq.go(g, ytfm[:, 0:1024],
                          wdqT.ap()[rr * 128:(rr + 1) * 128, 0:1024])
                    gq.wait(g)
                    gq.go(g, dbg["dbg_mrev"][:, rr * 1024:(rr + 1) * 1024], ytfm[:, 0:1024])
                    gq.wait(g)
    with block() as b:
        @b.gpsimd
        def _(g):
            gq.wait(g)
        @b.sync
        def _(sx):
            sq.wait(sx)

    stack.close()
    return nc


def _stage(inputs):
    ident = np.eye(128, dtype=np.float32)
    jrev = ident[:, ::-1].copy()
    ii, jj = np.meshgrid(np.arange(128), np.arange(128), indexing="ij")
    maskneg = np.where(jj >= ii, 0.0, -1e30).astype(np.float32)
    tiew = np.tile(np.arange(16, 0, -1, dtype=np.float32), (128, 1)).copy()
    tiew4 = np.tile(np.arange(4, 0, -1, dtype=np.float32), (128, 1)).copy()

    x = inputs["L0_attn_hidden_states"]
    per_core = []
    for c in range(8):
        b, dd = c // 2, c % 2
        pre = "L0_attn_fwd_" if dd == 0 else "L0_attn_bwd_"
        xb = x[b] if dd == 0 else x[b][::-1]
        m = {
            "x": np.ascontiguousarray(xb.T),
            "win": inputs[pre + "W_in"],
            "wout": inputs[pre + "W_out"],
            "convw": np.ascontiguousarray(inputs[pre + "conv_w"].T),
            "convb": np.ascontiguousarray(inputs[pre + "conv_b"].reshape(18, 128).T),
            "dtbias": inputs[pre + "dt_bias"].reshape(32, 1),
            "alog": inputs[pre + "A_log"].reshape(32, 1),
            "dhead": np.ascontiguousarray(np.tile(inputs[pre + "D"], (128, 1))),
            "normw": np.ascontiguousarray(inputs[pre + "norm_w"].reshape(16, 128).T),
            "ident": ident,
            "maskneg": maskneg,
            "jrev": jrev,
            "rwT": np.ascontiguousarray(inputs["L0_ffn_router_weight"].T),
            "ebias": np.ascontiguousarray(np.tile(inputs["L0_ffn_e_bias"], (128, 1))),
            "tiew": tiew,
            "tiew4": tiew4,
            "gwT": np.ascontiguousarray(inputs["L0_ffn_gate_w"].transpose(0, 2, 1)),
            "uwT": np.ascontiguousarray(inputs["L0_ffn_up_w"].transpose(0, 2, 1)),
            "dwT": np.ascontiguousarray(inputs["L0_ffn_down_w"].transpose(0, 2, 1)),
            "shgT": np.ascontiguousarray(inputs["L0_ffn_sh_gate_w"].T),
            "shuT": np.ascontiguousarray(inputs["L0_ffn_sh_up_w"].T),
            "shdT": np.ascontiguousarray(inputs["L0_ffn_sh_down_w"].T),
            "wvT": np.ascontiguousarray(inputs["L1_attn_w_qkv"][2048:3072].T),
            "woT": np.ascontiguousarray(inputs["L1_attn_w_o"].T),
            "wgate": inputs["L1_ffn_w_gate"],
            "wup": inputs["L1_ffn_w_up"],
            "wdown": inputs["L1_ffn_w_down"],
        }
        per_core.append({k: np.ascontiguousarray(v, dtype=np.float32)
                         for k, v in m.items()})
    return per_core


def kernel(**inputs):
    per_core = _stage(inputs)
    nc = build()
    res = run_bass_kernel_spmd(nc, per_core, core_ids=list(range(8)))
    out = np.empty((4, L, D_MODEL), dtype=np.float32)
    for c in range(8):
        b, dd = c // 2, c % 2
        o = np.asarray(res.results[c]["out"])
        if dd == 0:
            out[b, 0:TLOC] = o.T
        else:
            out[b, TLOC:] = o.T[::-1]
    return out
